# version 01
# baseline (speedup 1.0000x reference)
import numpy as np
import ml_dtypes
import concourse.bass as bass
import concourse.mybir as mybir
from concourse.bass_utils import run_bass_kernel_spmd

dt = mybir.dt
AF = mybir.ActivationFunctionType
ALU = mybir.AluOpType
AXX = mybir.AxisListType.X
ML_BF16 = ml_dtypes.bfloat16

DIM = 2048
DI = 4096
NH = 64
HD = 64
HID = 8192
NPROJ = 8480
T = 256
NT = 2048 // T
Q = 128
NCH = T // Q
BIGNEG = -1.0e30
RMS_EPS = 1.1920929e-07
LN_EPS = 1e-5
INV2PI = float(1.0 / (2.0 * np.pi))
TWOPI = float(2.0 * np.pi)
GCONST = float(np.sqrt(2.0 / np.pi))
GC2 = 0.044715

F32, F32R, BF, FP8, I32 = dt.float32, dt.float32r, dt.bfloat16, dt.float8e4, dt.int32


def build_program(debug=False):
    nc = bass.Bass("TRN2", target_bir_lowering=False)

    def din(name, shape, dtype=F32):
        return nc.dram_tensor(name, shape, dtype, kind="ExternalInput").ap()

    x_d = din("x", [2048, DIM])
    w_in_d = din("w_in", [8384, DIM])
    w_lam_d = din("w_lambda", [64, DIM])
    w_th_d = din("w_theta", [32, DIM])
    w_out_d = din("w_out", [DIM, DI])
    wg0_d = din("wg0", [DIM, HID])
    wg1_d = din("wg1", [DIM, HID])
    wl_d = din("wl", [HID, DIM])
    bbias_d = din("bbias_c", [64, 1])
    cbias_d = din("cbias_c", [64, 1])
    alog_d = din("alog_c", [64, 1])
    dpar_d = din("d_c", [64, 1])
    dtb_d = din("dtb_c", [64, 1])
    i64_d = din("i64", [64, 64])
    i128_d = din("i128", [128, 128])
    trineg_d = din("trineg", [128, 128])

    y_d = nc.dram_tensor("y", [2048, DIM], F32, kind="ExternalOutput").ap()

    scr_wall = nc.dram_tensor("scr_wall", [NPROJ, DIM], BF, kind="Internal").ap()
    scr_wout = nc.dram_tensor("scr_wout", [DIM, DI], BF, kind="Internal").ap()
    scr_wg0 = nc.dram_tensor("scr_wg0", [DIM, HID], FP8, kind="Internal").ap()
    scr_wg1 = nc.dram_tensor("scr_wg1", [DIM, HID], FP8, kind="Internal").ap()
    scr_wl = nc.dram_tensor("scr_wl", [HID, DIM], FP8, kind="Internal").ap()
    scr_line = nc.dram_tensor("scr_line", [1, NH * Q], BF, kind="Internal").ap()
    scr_eline = nc.dram_tensor("scr_eline", [1, DI], BF, kind="Internal").ap()
    scr_xlast = nc.dram_tensor("scr_xlast", [1, DI], BF, kind="Internal").ap()
    scr_row = nc.dram_tensor("scr_row", [1, 256], F32, kind="Internal").ap()

    dbg = {}
    def dout(name, shape, dtype):
        dbg[name] = nc.dram_tensor(name, shape, dtype, kind="ExternalOutput").ap()
        return dbg[name]
    if debug:
        dout("dbg_ln1fm", [128, 16 * T], BF)
        dout("dbg_zfm", [128, 32 * T], BF)
        dout("dbg_bc", [128, T], F32)
        dout("dbg_dt", [64, T], F32)
        dout("dbg_ang", [32, T], F32)
        dout("dbg_bro", [64, T], F32)
        dout("dbg_mask1", [128, 1024], BF)
        dout("dbg_X", [128, 4096], BF)
        dout("dbg_gt", [128, 128], BF)
        dout("dbg_cro", [64, T], F32)
        dout("dbg_gb", [64, 2 * T], F32)
        dout("dbg_ge", [128, 4096], BF)
        dout("dbg_ytok", [128, NCH * DI], BF)
        dout("dbg_x1", [128, 2 * DIM], F32)
        dout("dbg_a8", [128, 8 * T], FP8)

    from contextlib import ExitStack
    es = ExitStack()
    sbt = lambda name, shape, dtype: es.enter_context(nc.sbuf_tensor(name, shape, dtype))
    pst = lambda name, shape: es.enter_context(nc.psum_tensor(name, shape, F32))

    x_tok = sbt("x_tok", [128, 2 * DIM], F32)
    ln1_fm = sbt("ln1_fm", [128, 16 * T], BF)
    z_fm = sbt("z_fm", [128, 32 * T], BF)
    xh_tm = sbt("xh_tm", [128, NCH * DI], BF)
    y_tok = sbt("y_tok", [128, NCH * DI], BF)
    carry = sbt("carry", [64, NH * HD], BF)
    arena_bf = sbt("arena_bf", [128, 8192], BF)
    arena2_bf = sbt("arena2_bf", [128, 8192], BF)
    arena3_bf = sbt("arena3_bf", [128, 8192], BF)
    arena_f32 = sbt("arena_f32", [128, 2048], F32)
    arena_i32 = sbt("arena_i32", [128, 2048], I32)
    wbufA = sbt("wbufA", [128, 1024], BF)
    wbufB = sbt("wbufB", [128, 1024], BF)
    w8A = sbt("w8A", [128, 2048], FP8)
    w8B = sbt("w8B", [128, 2048], FP8)
    a8 = sbt("a8", [128, 64 * T], FP8)
    ln2_f8 = sbt("ln2_f8", [128, 16 * T], FP8)
    trig = sbt("trig", [64, 2 * T], BF)
    zeros_t = sbt("zeros_t", [64, 256], F32)
    bc_t = sbt("bc_t", [128, T], F32)
    dtlam = sbt("dtlam", [128, T], F32)
    s128a = sbt("s128a", [128, T], F32)
    dt_t = sbt("dt_t", [64, T], F32)
    dt_r = sbt("dt_r", [64, T], F32R)
    lam_t = sbt("lam_t", [64, T], F32)
    a_fm_t = sbt("a_fm_t", [64, T], F32)
    gam_r = sbt("gam_r", [64, T], F32R)
    bet_r = sbt("bet_r", [64, T], F32R)
    s64a = sbt("s64a", [64, T], F32)
    s64b = sbt("s64b", [64, T], F32)
    s64c = sbt("s64c", [64, T], F32)
    craw = s64c[0:64, :]
    w1 = s64a[0:32, :]
    w2 = s64b[0:32, :]
    bmn = sbt("bmn", [64, T], F32)
    cmn = sbt("cmn", [64, T], F32)
    bm_bf = sbt("bm_bf", [64, T], BF)
    cm_bf = sbt("cm_bf", [64, T], BF)
    b_tm = sbt("b_tm", [128, NCH * 64], BF)
    theta_t = sbt("theta_t", [32, T], F32)
    ang_t = sbt("ang_t", [32, T], F32)
    ang_carry = sbt("ang_carry", [32, 1], F32)
    ang_i = arena_i32[0:32, 0:T]
    acs_fm = sbt("acs_fm", [64, Q], F32)
    acs_r = sbt("acs_r", [64, Q], F32R)
    nacs_r = sbt("nacs_r", [64, Q], F32R)
    acsline = sbt("acsline", [1, 1024], F32R)
    eacs_bf = sbt("eacs_bf", [64, Q], BF)
    d2e_f = sbt("d2e_f", [64, Q], F32)
    d2e_r = sbt("d2e_r", [64, Q], F32R)
    gt_bf = sbt("gt_bf", [128, 128], BF)
    e_sb = sbt("e_sb", [64, 2048], BF)
    row_f = sbt("row_f", [1, T], F32)
    row_r = sbt("row_r", [1, T], F32R)
    dtavg_row = row_f
    dtavg_r = row_r
    srow2 = row_f
    srow2_r = row_r
    rstd_c = sbt("rstd_c", [128, 2], F32)
    mcol = sbt("mcol", [128, 4], F32)
    i64c = sbt("i64c_s", [64, 64], F32)
    i64r = sbt("i64r", [64, 64], F32R)
    i128r = sbt("i128r", [128, 128], F32R)
    trinr = sbt("trinr", [128, 128], F32R)
    onesf = sbt("onesf_s", [128, 1], F32)
    ones64 = sbt("ones64", [64, 64], F32)
    exp_t = sbt("exp_t", [64, 64], BF)
    e_col = sbt("e_col", [64, 1], F32)
    onesr = sbt("onesr", [128, 128], F32R)
    onesb = sbt("onesb", [128, 128], BF)
    smalls_dc = sbt("smalls_dc", [64, 1], F32)
    bbias = sbt("bbias", [64, 1], F32)
    cbias = sbt("cbias", [64, 1], F32)
    dtb = sbt("dtb", [64, 1], F32)
    alog_s = sbt("alog_s", [64, 1], F32)
    mexpA = sbt("mexpA", [64, 1], F32)
    epscol = sbt("epscol", [128, 1], F32)
    rmseps = sbt("rmseps", [128, 1], F32)
    halfpi = sbt("halfpi", [128, 1], F32)

    pmA = pst("pmA", [128, 2048])
    pmB = pst("pmB", [128, 2048])
    PM = [pmA, pmB]

    thunks = {e: [] for e in ("sync", "gpsimd", "vector", "scalar", "tensor")}
    sems = {}
    DMA_SEMS = ("sInit", "sDl", "sDrT", "sPa", "sPb", "sPstA", "sPstB", "sG8a", "sG8b",
                "sP8a", "sP8b", "sX", "sTr1", "sTrXh", "sTrB", "sTrY", "sTrA",
                "sWa", "sWb", "sLam", "sCm", "sTrg", "sRp1", "sRp2", "sSh", "sEw",
                "sEv", "sE2d", "sRep", "sAcl", "sXl", "sRs", "sOut", "sDbg")
    for s in ("sT", "sV", "sS") + DMA_SEMS:
        sems[s] = [es.enter_context(nc.semaphore(s)), 0]

    def on(eng, fn):
        thunks[eng].append(fn)

    def wait(eng, sname, value):
        if value <= 0:
            return
        h = sems[sname][0]
        on(eng, lambda e, h=h, v=value: e.wait_ge(h, v))

    state = {"last_T": None, "last_SS": None}

    def _is_sbuf(ap):
        try:
            return str(ap.space) in ("MemorySpace.SBUF", "SBUF") or "SB" in str(ap.space)
        except Exception:
            return False

    def dma(eng, sname, out, in_, transpose=False, slow=False):
        h = sems[sname][0]
        if eng == "sync":
            ss = (not transpose) and _is_sbuf(out) and _is_sbuf(in_)
            if transpose and state["last_SS"]:
                sn = state["last_SS"][0]
                wait(eng, sn, sems[sn][1])
            if ss and state["last_T"]:
                sn = state["last_T"][0]
                wait(eng, sn, sems[sn][1])
        sems[sname][1] += 16
        v = sems[sname][1]
        on(eng, lambda e, o=out, i=in_, t=transpose, h=h, sl=slow: e.dma_start(
            o, i, transpose=t, allow_slow_non_contiguous=sl).then_inc(h, 16))
        tok = (sname, v)
        if eng == "sync":
            if transpose:
                state["last_T"] = tok
            elif _is_sbuf(out) and _is_sbuf(in_):
                state["last_SS"] = tok
        return tok

    def wtok(eng, tok):
        if tok is None or tok == 0:
            return
        sname, v = tok
        wait(eng, sname, v)

    ENG_SEM = {"tensor": "sT", "vector": "sV", "scalar": "sS", "gpsimd": "sG_nop"}

    DATAPATH = {"tensor_copy", "tensor_tensor", "tensor_add", "tensor_sub", "tensor_mul",
                "tensor_max", "tensor_scalar", "tensor_single_scalar", "tensor_scalar_add",
                "tensor_scalar_sub", "tensor_scalar_mul", "tensor_scalar_max",
                "tensor_scalar_min", "scalar_tensor_tensor", "tensor_reduce", "reduce_max",
                "reduce_sum", "reciprocal", "memset", "memzero", "tensor_tensor_scan",
                "select", "activation", "copy", "mul", "add", "square", "sqrt", "sign",
                "tensor_relu", "transpose"}

    class DrainAfter:
        def __init__(self, e):
            self._e = e

        def __getattr__(self, n):
            a = getattr(self._e, n)
            if n in DATAPATH:
                def w(*args, **kw):
                    r = a(*args, **kw)
                    self._e.drain()
                    return r
                return w
            return a

    def mark(eng):
        sname = {"tensor": "sT", "vector": "sV", "scalar": "sS"}[eng]
        h = sems[sname][0]
        sems[sname][1] += 1
        v = sems[sname][1]
        on(eng, lambda e, h=h: e.drain().then_inc(h, 1))
        return v

    def attach_inc(eng, buildfn):
        sname = {"tensor": "sT", "vector": "sV", "scalar": "sS"}[eng]
        h = sems[sname][0]
        sems[sname][1] += 1
        v = sems[sname][1]
        on(eng, lambda e, h=h, f=buildfn: f(e).then_inc(h, 1))
        return v

    V = lambda fn: on("vector", fn)
    S = lambda fn: on("scalar", fn)
    TT = lambda fn: on("tensor", fn)

    dma("sync", "sInit", i64c[:], i64_d)
    dma("sync", "sInit", bbias[:], bbias_d)
    dma("sync", "sInit", cbias[:], cbias_d)
    dma("sync", "sInit", dtb[:], dtb_d)
    dma("sync", "sInit", alog_s[:], alog_d)
    dma("sync", "sInit", arena_f32[:, 0:128], i128_d)
    dma("sync", "sInit", arena_f32[:, 128:256], trineg_d)
    c_loaded = dcol_v = dma("sync", "sInit", smalls_dc[:], dpar_d)

    wtok("vector", c_loaded)
    def _cinit(v):
        v.tensor_copy(i64r[:], i64c[:])
        v.tensor_copy(i128r[:], arena_f32[:, 0:128])
        v.tensor_copy(trinr[:], arena_f32[:, 128:256])
        v.memset(onesf[:], 1.0)
        v.memset(ones64[:], 1.0)
        v.memset(arena_f32[:, 256:384], 1.0)
        v.tensor_copy(onesr[:], arena_f32[:, 256:384])
        v.tensor_copy(onesb[:], arena_f32[:, 256:384])
        v.memset(epscol[:], LN_EPS)
        v.memset(rmseps[:], RMS_EPS)
        v.memset(halfpi[:], float(np.pi / 2))
        v.memset(zeros_t[:], 0.0)
        v.memset(carry[:], 0.0)
        v.memset(ang_carry[:], 0.0)
    V(_cinit)
    v_cinit = mark("vector")

    wtok("scalar", c_loaded)
    S(lambda s: s.activation(mexpA[:], alog_s[:], AF.Exp))
    S(lambda s: s.mul(mexpA[:], mexpA[:], -1.0))
    s_mexpA = mark("scalar")

    scr_dline = nc.dram_tensor("scr_dline", [1, DI], BF, kind="Internal").ap()
    V(lambda v: v.tensor_scalar(exp_t[:], ones64[:], smalls_dc[:, 0:1], None, ALU.mult))
    v_dline = mark("vector")
    wait("sync", "sV", v_dline)
    drep_v = dma("sync", "sDl", scr_dline[0:1, :].rearrange("o (h p) -> o h p", h=64), exp_t[:])

    inbufs = [arena_f32, x_tok[:, 2048:4096]]
    wq_out = arena3_bf[:, 0:4096]
    prev_q2 = [v_cinit, v_cinit]
    prev_st = [None, None]

    def fq4_chunk(idx, nrows, loads, dst):
        buf = inbufs[idx % 2]
        wq = wq_out[:, (idx % 2) * 2048:(idx % 2) * 2048 + 2048]
        wait("sync", "sV", prev_q2[idx % 2])
        psem = ("sPa", "sPb")[idx % 2]
        vload = None
        for sl, src in loads:
            vload = dma("sync", psem, sl, src)
        wtok("vector", vload)
        def _q1(v, buf=buf, nrows=nrows):
            for s_ in range(16):
                v.reduce_max(s128a[0:nrows, s_:s_ + 1],
                             buf[0:nrows, s_ * 128:(s_ + 1) * 128],
                             axis=AXX, apply_absolute_value=True)
            v.tensor_scalar_max(s128a[0:nrows, 0:16], s128a[0:nrows, 0:16], 1e-10)
            v.reciprocal(s128a[0:nrows, 16:32], s128a[0:nrows, 0:16])
            v.tensor_scalar_mul(s128a[0:nrows, 0:16], s128a[0:nrows, 0:16], float(1.0 / 7.0))
            for s_ in range(16):
                v.tensor_scalar(buf[0:nrows, s_ * 128:(s_ + 1) * 128],
                                buf[0:nrows, s_ * 128:(s_ + 1) * 128],
                                s128a[0:nrows, 16 + s_:17 + s_], None, ALU.mult)
            v.tensor_scalar(buf[0:nrows, :], buf[0:nrows, :], 7.0, 8.0, ALU.mult, ALU.add)
        V(_q1)
        q1 = mark("vector")
        wait("scalar", "sV", q1)
        def _qcast(s, buf=buf, nrows=nrows):
            s.copy(arena_i32[0:nrows, 0:1024], buf[0:nrows, 0:1024])
        S(_qcast)
        qs = mark("scalar")
        wait("vector", "sS", qs)
        def _q2b(v, buf=buf, wq=wq, nrows=nrows):
            v.tensor_copy(x_tok[0:nrows, 0:1024], arena_i32[0:nrows, 0:1024])
        V(_q2b)
        q2a = mark("vector")
        wait("scalar", "sV", q2a)
        def _qcast2(s, buf=buf, nrows=nrows):
            s.copy(arena_i32[0:nrows, 0:1024], buf[0:nrows, 1024:2048])
        S(_qcast2)
        qs2 = mark("scalar")
        wait("vector", "sS", qs2)
        wtok("vector", prev_st[idx % 2])
        def _q2c(v, buf=buf, wq=wq, nrows=nrows):
            v.tensor_copy(x_tok[0:nrows, 1024:2048], arena_i32[0:nrows, 0:1024])
            v.scalar_tensor_tensor(arena_bf[0:nrows, 0:2048], buf[0:nrows, :], 0.5,
                                   x_tok[0:nrows, 0:2048], ALU.subtract, ALU.is_ge)
            v.tensor_add(buf[0:nrows, :], x_tok[0:nrows, 0:2048], arena_bf[0:nrows, 0:2048])
            for s_ in range(16):
                v.tensor_scalar(wq[0:nrows, s_ * 128:(s_ + 1) * 128],
                                buf[0:nrows, s_ * 128:(s_ + 1) * 128],
                                -8.0, s128a[0:nrows, s_:s_ + 1], ALU.add, ALU.mult)
        V(_q2c)
        q2 = mark("vector")
        prev_q2[idx % 2] = q2
        wait("sync", "sV", q2)
        wtok("sync", prev_st[idx % 2])
        prev_st[idx % 2] = dma("sync", ("sPstA", "sPstB")[idx % 2], dst, wq[0:nrows, :])
        return q2


    idx = 0
    for ci in range(67):
        r0 = ci * 128
        nrows = min(128, NPROJ - r0)
        buf = inbufs[idx % 2]
        loads = []
        for srcap, lo, hi in ((w_in_d, 0, 8384), (w_lam_d, 8384, 8448), (w_th_d, 8448, 8480)):
            s0, s1 = max(r0, lo), min(r0 + nrows, hi)
            if s0 < s1:
                loads.append((buf[s0 - r0:s1 - r0, :], srcap[s0 - lo:s1 - lo, :]))
        fq4_chunk(idx, nrows, loads, scr_wall[r0:r0 + nrows, :])
        idx += 1
    for ci in range(32):
        r0 = (ci // 2) * 128
        c0 = (ci % 2) * 2048
        buf = inbufs[idx % 2]
        fq4_chunk(idx, 128, [(buf[:], w_out_d[r0:r0 + 128, c0:c0 + 2048])],
                  scr_wout[r0:r0 + 128, c0:c0 + 2048])
        idx += 1

    prev8 = [None, None]
    j8 = 0
    def f8hop(src_ap, dst_ap):
        nonlocal j8
        b = (w8A, w8B)[j8 % 2]
        wtok("gpsimd", prev8[j8 % 2])
        v8 = dma("gpsimd", ("sG8a", "sG8b")[j8 % 2], b[:], src_ap)
        wtok("sync", v8)
        prev8[j8 % 2] = dma("sync", ("sP8a", "sP8b")[j8 % 2], dst_ap, b[:])
        j8 += 1
    for srcd, dstd in ((wg0_d, scr_wg0), (wg1_d, scr_wg1)):
        for ci in range(16):
            r0 = ci * 128
            for half in range(4):
                c0 = half * 2048
                f8hop(srcd[r0:r0 + 128, c0:c0 + 2048], dstd[r0:r0 + 128, c0:c0 + 2048])
    for ci in range(64):
        r0 = ci * 128
        f8hop(wl_d[r0:r0 + 128, :], scr_wl[r0:r0 + 128, :])

    prep_V = sems["sV"][1]
    prep_toks = [(s, sems[s][1]) for s in ("sPstA", "sPstB", "sP8a", "sP8b", "sDl")
                 if sems[s][1] > 0]

    m_prev = {"tile_done_V": prep_V, "tile_done_S": 0, "tile_done_T": 0,
              "out_st": None, "xl": None, "drt": None}

    for t in range(NT):
        r0 = t * T
        if t == 0:
            for tk in prep_toks:
                wtok("sync", tk)
        wait("sync", "sV", m_prev["tile_done_V"])
        wait("sync", "sS", m_prev["tile_done_S"])
        wait("sync", "sT", m_prev["tile_done_T"])
        wtok("sync", m_prev.get("dbgx1"))
        x_in = None
        for c in range(2):
            x_in = dma("sync", "sX", x_tok[:, c * DIM:(c + 1) * DIM], x_d[r0 + c * 128:r0 + (c + 1) * 128, :])

        ln_tm = arena3_bf[:, 4096:8192]
        wtok("vector", x_in)
        wait("vector", "sS", s_mexpA)
        def _ln(v, src=x_tok):
            for c in range(2):
                xc = src[:, c * DIM:(c + 1) * DIM]
                v.reduce_sum(mcol[:, c:c + 1], xc, axis=AXX)
            v.tensor_scalar_mul(mcol[:, 0:2], mcol[:, 0:2], float(1.0 / DIM))
        V(_ln)
        vln = mark("vector")
        wait("scalar", "sV", vln)
        def _lns(s, src=x_tok):
            for c in range(2):
                s.activation(arena2_bf[:, c * DIM:(c + 1) * DIM], src[:, c * DIM:(c + 1) * DIM],
                             AF.Square)
        S(_lns)
        sln = mark("scalar")
        wait("vector", "sS", sln)
        def _ln2(v, src=x_tok, dstv=ln_tm):
            for c in range(2):
                v.reduce_sum(mcol[:, 2 + c:3 + c], arena2_bf[:, c * DIM:(c + 1) * DIM], axis=AXX)
            v.tensor_scalar_mul(mcol[:, 2:4], mcol[:, 2:4], float(1.0 / DIM))
            v.tensor_mul(rstd_c[:, 0:2], mcol[:, 0:2], mcol[:, 0:2])
            v.tensor_sub(mcol[:, 2:4], mcol[:, 2:4], rstd_c[:, 0:2])
        V(_ln2)
        vln2 = mark("vector")
        wait("scalar", "sV", vln2)
        def _ln3(s):
            s.activation(mcol[:, 2:4], mcol[:, 2:4], AF.Ln, bias=epscol[:, 0:1])
            s.activation(mcol[:, 2:4], mcol[:, 2:4], AF.Exp, scale=-0.5)
        S(_ln3)
        sln3 = mark("scalar")
        wait("vector", "sS", sln3)
        def _ln4(v, src=x_tok, dstv=ln_tm):
            for c in range(2):
                v.tensor_scalar(dstv[:, c * DIM:(c + 1) * DIM], src[:, c * DIM:(c + 1) * DIM],
                                mcol[:, c:c + 1], mcol[:, 2 + c:3 + c], ALU.subtract, ALU.mult)
        V(_ln4)
        vln4 = mark("vector")
        wait("sync", "sV", vln4)
        tr_v = None
        for c in range(2):
            for k in range(16):
                tr_v = dma("sync", "sTr1", ln1_fm[:, k * T + c * 128: k * T + c * 128 + 128],
                           ln_tm[:, c * DIM + k * 128: c * DIM + (k + 1) * 128], transpose=True)
        ln1fm_done = tr_v

        if debug and t == 0:
            wtok("sync", ln1fm_done)
            dma("sync", "sDbg", dbg["dbg_ln1fm"], ln1_fm[:])
            wait("sync", "sDbg", sems["sDbg"][1])

        xh_fm = arena_bf[:, 0:8192]
        wload = [None, None]
        gmark = {}
        evac_mark = {}
        step = 0
        for g in range(17):
            rg0 = g * 512
            mg = min(512, NPROJ - rg0)
            pm = PM[g % 2]
            for k in range(16):
                b = step % 2
                wb = (wbufA, wbufB)[b]
                wait("sync", "sT", gmark.get(("buf", step - 2), 0))
                wload[b] = dma("sync", ("sWa", "sWb")[b], wb[:, 0:mg],
                               scr_wall[rg0:rg0 + mg, k * 128:(k + 1) * 128], transpose=True)
                wtok("tensor", wload[b])
                if k == 0:
                    wait("tensor", "sS", evac_mark.get(g - 2, 0))
                    wtok("tensor", ln1fm_done)
                def _mm(tt, pm=pm, wb=wb, k=k, mg=mg):
                    nmc = (mg + 127) // 128
                    for mc in range(nmc):
                        M = min(128, mg - mc * 128)
                        tt.matmul(pm[0:M, mc * 512:mc * 512 + T],
                                  wb[:, mc * 128:mc * 128 + M],
                                  ln1_fm[:, k * T:(k + 1) * T],
                                  start=(k == 0), stop=(k == 15))
                TT(_mm)
                gmark[("buf", step)] = mark("tensor")
                step += 1
            gm = gmark[("buf", step - 1)]
            wait("scalar", "sT", gm)
            pm_str = pm[:, 0:2048].rearrange("p (a b) -> p a b", a=4)[:, :, 0:T]
            if g < 8:
                S(lambda s, pm_str=pm_str, g=g: s.activation(
                    z_fm[:, g * 1024:(g + 1) * 1024].rearrange("p (a b) -> p a b", a=4),
                    pm_str, AF.Silu))
            elif g < 16:
                S(lambda s, pm_str=pm_str, g=g: s.activation(
                    xh_fm[:, (g - 8) * 1024:(g - 7) * 1024].rearrange("p (a b) -> p a b", a=4),
                    pm_str, AF.Silu))
            else:
                def _e8(s, pm=pm):
                    s.copy(bc_t[:], pm[:, 0:T])
                    s.copy(dtlam[:], pm[:, 512:512 + T])
                    s.copy(theta_t[:], pm[0:32, 1024:1024 + T])
                S(_e8)
            evac_mark[g] = mark("scalar")
        evac_mark[7] = evac_mark[15]
        inproj_S = evac_mark[16]

        if debug and t == 0:
            wait("sync", "sS", evac_mark[7])
            dma("sync", "sDbg", dbg["dbg_zfm"], z_fm[:])
            wait("sync", "sDbg", sems["sDbg"][1])
            wait("sync", "sS", inproj_S)
            dma("sync", "sDbg", dbg["dbg_bc"], bc_t[:])
            wait("sync", "sDbg", sems["sDbg"][1])

        wtok("sync", m_prev["xl"])
        wait("sync", "sS", evac_mark[7])
        xh_tr = None
        for kk in range(32):
            for c in range(2):
                xh_tr = dma("sync", "sTrXh", xh_tm[:, c * DI + kk * 128: c * DI + (kk + 1) * 128],
                            xh_fm[:, kk * T + c * 128: kk * T + c * 128 + 128], transpose=True)
        xh_tm_done = xh_tr

        wait("scalar", "sS", inproj_S)
        def _sm1(s):
            s.activation(s64a[:], dtlam[0:64, :], AF.Abs, bias=dtb[:, 0:1])
            s.activation(s64a[:], s64a[:], AF.Exp, scale=-1.0)
            s.activation(s64a[:], s64a[:], AF.Ln, bias=onesf[0:64, 0:1])
            s.activation(s64b[:], dtlam[0:64, :], AF.Relu, bias=dtb[:, 0:1])
            s.activation(s128a[64:128, :], dtlam[64:128, :], AF.Exp, scale=-1.0)
        S(_sm1)
        sm1 = mark("scalar")
        wait("vector", "sS", sm1)
        def _sm2(v):
            v.tensor_add(dt_t[:], s64a[:], s64b[:])
            v.tensor_copy(dt_r[:], dt_t[:])
            v.tensor_scalar(a_fm_t[:], dt_t[:], mexpA[:, 0:1], None, ALU.mult)
            v.tensor_scalar(s128a[64:128, :], s128a[64:128, :], 1.0, None, ALU.add)
            v.reciprocal(s128a[64:128, :], s128a[64:128, :])
        V(_sm2)
        sm2 = mark("vector")
        wait("sync", "sV", sm2)
        lam_mv = dma("sync", "sLam", lam_t[:], s128a[64:128, :])
        wait("tensor", "sV", sm2)
        wait("tensor", "sS", evac_mark.get(7, 0))
        wait("tensor", "sS", inproj_S)
        TT(lambda tt: tt.matmul(pmA[0:1, 0:T], onesr[0:64, 0:1], dt_r[:], start=True, stop=True))
        t_dtavg = mark("tensor")
        wait("scalar", "sT", t_dtavg)
        S(lambda s: s.activation(dtavg_row[:], pmA[0:1, 0:T], AF.Copy, scale=float(1.0 / NH)))
        s_dtavg = mark("scalar")
        wait("vector", "sS", s_dtavg)
        V(lambda v: v.tensor_copy(dtavg_r[:], dtavg_row[:]))
        v_dtavg = mark("vector")
        wait("tensor", "sV", v_dtavg)
        TT(lambda tt: tt.matmul(pmA[0:32, 0:T], onesr[0:1, 0:32], dtavg_r[:], start=True, stop=True))
        t_dtrep = mark("tensor")
        w3 = s64c[0:32, :]
        wait("vector", "sT", t_dtrep)
        def _ang(v):
            v.tensor_mul(w1[:], theta_t[:], pmA[0:32, 0:T])
            v.tensor_tensor_scan(ang_t[:], w1[:], zeros_t[0:32, 0:T], 0.0, ALU.add, ALU.add)
            v.tensor_scalar(ang_t[:], ang_t[:], ang_carry[:, 0:1], None, ALU.add)
            v.tensor_copy(ang_carry[:], ang_t[:, T - 1:T])
            v.tensor_scalar_mul(w1[:], ang_t[:], INV2PI)
        V(_ang)
        v_ang = mark("vector")
        wait("scalar", "sV", v_ang)
        S(lambda s: s.copy(ang_i[:], w1[:]))
        s_angi = mark("scalar")
        wait("vector", "sS", s_angi)
        def _ang2(v):
            v.tensor_copy(w2[:], ang_i[:])
            v.scalar_tensor_tensor(w3[:], w1[:], 0.5, w2[:], ALU.subtract, ALU.is_ge)
            v.scalar_tensor_tensor(theta_t[:], w1[:], 0.5, w2[:], ALU.add, ALU.is_le)
            v.tensor_add(w2[:], w2[:], w3[:])
            v.tensor_sub(w2[:], w2[:], theta_t[:])
            v.scalar_tensor_tensor(w1[:], w2[:], -TWOPI, ang_t[:], ALU.mult, ALU.add)
            v.tensor_scalar(w2[:], ang_t[:], INV2PI, 0.25, ALU.mult, ALU.add)
        V(_ang2)
        v_ang2 = mark("vector")
        wait("scalar", "sV", v_ang2)
        S(lambda s: s.copy(ang_i[:], w2[:]))
        s_angi2 = mark("scalar")
        wait("vector", "sS", s_angi2)
        def _ang3(v):
            v.tensor_copy(w3[:], ang_i[:])
            v.scalar_tensor_tensor(theta_t[:], w2[:], 0.5, w3[:], ALU.subtract, ALU.is_ge)
            v.scalar_tensor_tensor(w2[:], w2[:], 0.5, w3[:], ALU.add, ALU.is_le)
            v.tensor_add(w3[:], w3[:], theta_t[:])
            v.tensor_sub(w3[:], w3[:], w2[:])
            v.scalar_tensor_tensor(w2[:], w3[:], -TWOPI, ang_t[:], ALU.mult, ALU.add)
            v.tensor_scalar(w2[:], w2[:], float(np.pi / 2), None, ALU.add)
        V(_ang3)
        v_ang3 = mark("vector")
        wait("scalar", "sV", v_ang3)
        def _trig(s):
            s.activation(trig[0:32, 0:T], w1[:], AF.Sin)
            s.activation(trig[0:32, T:2 * T], w2[:], AF.Sin)
        S(_trig)
        s_trig = mark("scalar")
        wait("sync", "sS", s_trig)
        trig_mv = dma("sync", "sTrg", trig[32:64, :], trig[0:32, :])

        if debug and t == 0:
            wait("sync", "sV", v_ang)
            dma("sync", "sDbg", dbg["dbg_ang"], ang_t[:])
            wait("sync", "sDbg", sems["sDbg"][1])
            wait("sync", "sS", inproj_S)
            dma("sync", "sDbg", dbg["dbg_dt"], dt_t[:])
            wait("sync", "sDbg", sems["sDbg"][1])

        wait("sync", "sV", v_ang3)
        cm_mv = dma("sync", "sCm", craw[:], bc_t[64:128, :])
        wait("vector", "sS", inproj_S)
        wait("vector", "sS", s_trig)
        wait("vector", "sT", t_dtrep)
        def _bc1(v):
            v.tensor_mul(s64a[:], bc_t[0:64, :], bc_t[0:64, :])
            v.tensor_copy(dt_r[:], s64a[:])
        V(_bc1)
        v_bc1 = mark("vector")
        wait("tensor", "sV", v_bc1)
        TT(lambda tt: tt.matmul(pmB[0:1, 0:T], onesr[0:64, 0:1], dt_r[:], start=True, stop=True))
        t_bssq = mark("tensor")
        wtok("vector", cm_mv)
        wait("vector", "sT", t_bssq)
        def _bc2(v):
            v.tensor_mul(s64b[:], craw[:], craw[:])
            v.tensor_copy(dt_r[:], s64b[:])
        V(_bc2)
        v_bc2 = mark("vector")
        wait("tensor", "sV", v_bc2)
        TT(lambda tt: tt.matmul(pmB[0:1, T:2 * T], onesr[0:64, 0:1], dt_r[:], start=True, stop=True))
        t_cssq = mark("tensor")
        wait("scalar", "sT", t_cssq)
        def _bc3(s):
            s.activation(srow2[:], pmB[0:1, 0:T], AF.Ln, scale=float(1.0 / 64), bias=rmseps[0:1, 0:1])
            s.activation(srow2[:], srow2[:], AF.Exp, scale=-0.5)
        S(_bc3)
        s_brstd = mark("scalar")
        wait("vector", "sS", s_brstd)
        V(lambda v: v.tensor_copy(srow2_r[:], srow2[:]))
        v_brstd = mark("vector")
        wait("tensor", "sV", v_brstd)
        TT(lambda tt: tt.matmul(pmA[0:64, 0:T], onesr[0:1, 0:64], srow2_r[:], start=True, stop=True))
        t_brep = mark("tensor")
        wait("scalar", "sT", t_brep)
        def _bc4(s):
            s.activation(srow2[:], pmB[0:1, T:2 * T], AF.Ln, scale=float(1.0 / 64), bias=rmseps[0:1, 0:1])
            s.activation(srow2[:], srow2[:], AF.Exp, scale=-0.5)
        S(_bc4)
        s_crstd = mark("scalar")
        wait("vector", "sS", s_crstd)
        wait("vector", "sT", t_brep)
        def _bc5(v):
            v.tensor_mul(bmn[:], bc_t[0:64, :], pmA[0:64, 0:T])
            v.tensor_scalar(bmn[:], bmn[:], bbias[:, 0:1], None, ALU.add)
            v.tensor_copy(srow2_r[:], srow2[:])
        V(_bc5)
        v_bc5 = mark("vector")
        wait("tensor", "sV", v_bc5)
        TT(lambda tt: tt.matmul(pmA[0:64, T:2 * T], onesr[0:1, 0:64], srow2_r[:], start=True, stop=True))
        t_crep = mark("tensor")
        wait("vector", "sT", t_crep)
        def _bc6(v):
            v.tensor_mul(cmn[:], craw[:], pmA[0:64, T:2 * T])
            v.tensor_scalar(cmn[:], cmn[:], cbias[:, 0:1], None, ALU.add)
        V(_bc6)
        v_bc6 = mark("vector")

        def rope(src, v_src_mark, prev_mv2=None):
            wait("sync", "sV", v_src_mark)
            mv1 = dma("sync", "sRp1", w1[:], src[32:64, :])
            wtok("vector", mv1)
            wtok("vector", prev_mv2)
            wtok("vector", trig_mv)
            def _r(v, src=src):
                sin0 = trig[0:32, 0:T]
                cos0 = trig[0:32, T:2 * T]
                v.tensor_mul(w2[:], w1[:], sin0)
                v.tensor_mul(theta_t[:], src[0:32, :], cos0)
                v.tensor_sub(theta_t[:], theta_t[:], w2[:])
                v.tensor_mul(w2[:], src[0:32, :], sin0)
                v.tensor_mul(w1[:], w1[:], cos0)
                v.tensor_add(w2[:], w2[:], w1[:])
                v.tensor_copy(src[0:32, :], theta_t[:])
            V(_r)
            vr = mark("vector")
            wait("sync", "sV", vr)
            mv2 = dma("sync", "sRp2", src[32:64, :], w2[:])
            return mv2
        b_roped = rope(bmn, v_bc5)
        c_roped = rope(cmn, v_bc6, prev_mv2=b_roped)
        if debug and t == 0:
            wtok("sync", c_roped)
            dma("sync", "sDbg", dbg["dbg_cro"], cmn[:])
            wait("sync", "sDbg", sems["sDbg"][1])
        wtok("vector", c_roped)
        def _bc7(v):
            v.tensor_copy(bm_bf[:], bmn[:])
            v.tensor_copy(cm_bf[:], cmn[:])
        V(_bc7)
        v_bcbf = mark("vector")
        wait("sync", "sV", v_bcbf)
        btm_v = None
        for c in range(NCH):
            btm_v = dma("sync", "sTrB", b_tm[:, c * 64:(c + 1) * 64],
                        bm_bf[0:64, c * 128:(c + 1) * 128], transpose=True)

        if debug and t == 0:
            wait("sync", "sV", v_bcbf)
            dma("sync", "sDbg", dbg["dbg_bro"], bmn[:])
            wait("sync", "sDbg", sems["sDbg"][1])

        wait("scalar", "sV", sm2)
        wait("scalar", "sV", v_bc6)
        S(lambda s: s.activation(s64c[:], a_fm_t[:], AF.Exp))
        s_expA = mark("scalar")
        wait("vector", "sS", s_expA)
        wtok("vector", lam_mv)
        wtok("vector", c_roped)
        def _gb(v):
            v.tensor_mul(s64a[:], lam_t[:], dt_t[:])
            v.tensor_copy(gam_r[:], s64a[:])
            v.tensor_scalar(s64b[:], lam_t[:], -1.0, 1.0, ALU.mult, ALU.add)
            v.tensor_mul(s64b[:], s64b[:], dt_t[:])
            v.tensor_mul(s64b[:], s64b[:], s64c[:])
            v.tensor_copy(bet_r[:], s64b[:])
        V(_gb)
        v_gb = mark("vector")
        if debug and t == 0:
            wait("sync", "sV", v_gb)
            dma("sync", "sDbg", dbg["dbg_gb"][0:64, 0:T], s64a[:])
            wait("sync", "sDbg", sems["sDbg"][1])
            dma("sync", "sDbg", dbg["dbg_gb"][0:64, T:2 * T], s64b[:])
            wait("sync", "sDbg", sems["sDbg"][1])

        prev_chunk_V = 0
        for c in range(NCH):
            ccol = c * Q
            xh_c = xh_tm[:, c * DI:(c + 1) * DI]
            xhs = arena_bf[:, 0:4096]
            Xc = arena3_bf[:, 0:4096]
            Xd = arena3_bf[:, 4096:8192]
            ge = arena2_bf[:, 0:4096]
            be = arena2_bf[:, 4096:8192]

            wtok("sync", xh_tm_done)
            wait("sync", "sS", evac_mark[7])
            wait("sync", "sV", prev_chunk_V)
            if c == 0:
                sh1 = dma("sync", "sSh", xhs[1:128, :], xh_c[0:127, :])
                if t == 0:
                    sh2 = sh1
                    wtok("vector", sh1)
                    V(lambda v: v.memset(xhs[0:1, :], 0.0))
                    mark("vector")
                else:
                    wtok("sync", m_prev["xl"])
                    sh2 = dma("sync", "sSh", xhs[0:1, :], scr_xlast[0:1, :])
            else:
                sh1 = dma("sync", "sSh", xhs[1:128, :], xh_c[0:127, :])
                sh2 = dma("sync", "sSh", xhs[0:1, :], xh_tm[127:128, (c - 1) * DI:c * DI])
            wait("tensor", "sV", v_gb)
            wait("tensor", "sV", prev_chunk_V)
            def _gbe(tt, ccol=ccol):
                for half in range(2):
                    pm = PM[half]
                    for blk in range(4):
                        h0 = half * 32 + blk * 8
                        rhs = i64r[:, h0:h0 + 8].unsqueeze(2).broadcast_to([64, 8, 64])
                        tt.matmul(pm[:, blk * 512:(blk + 1) * 512].rearrange("p (h q) -> p h q", h=8),
                                  gam_r[:, ccol:ccol + Q], rhs, start=True, stop=True)
            TT(_gbe)
            t_ge = mark("tensor")
            wait("scalar", "sT", t_ge)
            def _gee(s):
                s.copy(ge[:, 0:2048], pmA[:, 0:2048])
                s.copy(ge[:, 2048:4096], pmB[:, 0:2048])
            S(_gee)
            s_ge = mark("scalar")
            if debug and t == 0 and c == 0:
                wait("sync", "sS", s_ge)
                dma("sync", "sDbg", dbg["dbg_ge"], ge[:])
                wait("sync", "sDbg", sems["sDbg"][1])
            wait("tensor", "sS", s_ge)
            def _bbe(tt, ccol=ccol):
                for half in range(2):
                    pm = PM[half]
                    for blk in range(4):
                        h0 = half * 32 + blk * 8
                        rhs = i64r[:, h0:h0 + 8].unsqueeze(2).broadcast_to([64, 8, 64])
                        tt.matmul(pm[:, blk * 512:(blk + 1) * 512].rearrange("p (h q) -> p h q", h=8),
                                  bet_r[:, ccol:ccol + Q], rhs, start=True, stop=True)
            TT(_bbe)
            t_be = mark("tensor")
            wait("scalar", "sT", t_be)
            def _bee(s):
                s.copy(be[:, 0:2048], pmA[:, 0:2048])
                s.copy(be[:, 2048:4096], pmB[:, 0:2048])
            S(_bee)
            s_be = mark("scalar")
            wait("vector", "sS", s_be)
            if debug:
                wait("vector", "sDbg", sems["sDbg"][1])
            wtok("vector", sh2)
            def _X(v, xh_c=xh_c):
                v.tensor_mul(Xc[:], xh_c[:], ge[:])
                v.tensor_mul(Xd[:], xhs[:], be[:])
                v.tensor_add(Xc[:], Xc[:], Xd[:])
            V(_X)
            v_X = mark("vector")

            wait("vector", "sV", v_gb)
            def _acs(v, ccol=ccol):
                v.tensor_tensor_scan(acs_fm[:], a_fm_t[:, ccol:ccol + Q],
                                     zeros_t[0:64, 0:Q], 0.0, ALU.add, ALU.add)
                v.tensor_copy(acs_r[:], acs_fm[:])
                v.tensor_scalar(nacs_r[:], acs_fm[:], -1.0, None, ALU.mult)
            V(_acs)
            v_acs = mark("vector")
            wait("scalar", "sV", v_acs)
            def _acs2(s):
                s.activation(eacs_bf[:], acs_fm[:], AF.Exp)
            S(_acs2)
            s_acs2 = mark("scalar")
            wait("vector", "sS", s_acs2)
            def _vd2e(v):
                v.tensor_scalar(d2e_f[:], acs_fm[:], acs_fm[:, Q - 1:Q], -1.0,
                                ALU.subtract, ALU.mult)
                v.tensor_copy(d2e_r[:], d2e_f[:])
            V(_vd2e)
            v_d2e = mark("vector")
            wait("scalar", "sV", v_acs)
            S(lambda s: s.activation(e_col[:], acs_fm[:, Q - 1:Q], AF.Exp))
            s_erow = mark("scalar")
            wait("vector", "sS", s_erow)
            V(lambda v: v.tensor_scalar(exp_t[:], ones64[:], e_col[:, 0:1], None, ALU.mult))
            v_erow = mark("vector")
            wait("sync", "sV", v_erow)
            el_w = dma("sync", "sEw", scr_eline[0:1, :].rearrange("o (h p) -> o h p", h=64), exp_t[:])

            wait("sync", "sS", s_acs2)
            e2d_v = dma("sync", "sE2d", scr_line[:].rearrange("o (h q) -> o h q", h=64), eacs_bf[:])
            if debug and t == 0 and c == 0:
                wait("sync", "sV", v_X)
                m_prev["dbgX"] = dma("sync", "sDbg", dbg["dbg_X"], Xc[:])
                wait("sync", "sDbg", sems["sDbg"][1])
            wait("tensor", "sV", v_bcbf)
            wait("tensor", "sS", s_be)
            TT(lambda tt, ccol=ccol: tt.matmul(pmA[0:128, 0:128], bm_bf[:, ccol:ccol + Q],
                                               cm_bf[:, ccol:ccol + Q], start=True, stop=True))
            t_G = mark("tensor")
            wait("vector", "sT", t_G)
            V(lambda v: v.tensor_copy(gt_bf[:], pmA[0:128, 0:128]))
            v_G = mark("vector")

            mp = arena_bf[:, 4096:5120]
            m1a = arena_bf[:, 5120:6144]
            m1b = arena_bf[:, 6144:7168]
            grep = arena_bf[:, 0:1024]
            m2 = arena_bf[:, 1024:2048]
            cmrep = arena_bf[:, 2048:3072]
            earep = arena_bf[:, 3072:4096]
            ydone = []
            prev_evac = 0
            prev_m1 = {}
            prev_diff = {}
            prev_mk = {}
            for e8 in range(8):
                h0 = e8 * 8
                qpm = PM[e8 % 2]
                aslot = acsline[0:1, 0:1024]
                wait("sync", "sT", prev_diff.get(e8 - 1, 0))
                wait("sync", "sV", v_acs)
                acl_v = dma("sync", "sAcl", aslot.rearrange("o (h q) -> o h q", h=8),
                            acs_r[h0:h0 + 8, :])
                wtok("tensor", acl_v)
                wait("tensor", "sV", v_acs)
                wait("tensor", "sS", prev_m1.get(e8 - 2, 0))
                if e8 < 2:
                    wait("tensor", "sS", s_be)
                    wait("tensor", "sV", v_G)
                def _dmm(tt, h0=h0, qpm=qpm, aslot=aslot):
                    for half in range(2):
                        sl = qpm[:, half * 512:(half + 1) * 512]
                        h1 = h0 + half * 4
                        tt.matmul(sl, onesr[0:1, 0:128],
                                  aslot[0:1, half * 512:(half + 1) * 512],
                                  start=True, stop=False)
                        rhs = i64r[:, h1:h1 + 4].unsqueeze(2).broadcast_to([64, 4, 128])
                        tt.matmul(sl.rearrange("p (h q) -> p h q", h=4),
                                  nacs_r[:], rhs, start=False, stop=False)
                        rhs2 = i128r[:, :].unsqueeze(1).broadcast_to([128, 4, 128])
                        tt.matmul(sl.rearrange("p (h q) -> p h q", h=4),
                                  trinr[:], rhs2, start=False, stop=True)
                TT(_dmm)
                t_diff = mark("tensor")
                prev_diff[e8] = t_diff
                wait("sync", "sV", v_X if e8 == 0 else 0)
                wait("sync", "sV", prev_mk.get(e8 - 1, 0))
                wait("sync", "sV", v_G)
                g_v = dma("sync", "sRep", grep[:].rearrange("p (h q) -> p h q", h=8),
                          gt_bf[:].unsqueeze(1).broadcast_to([128, 8, 128]))
                c_v = dma("sync", "sRep", cmrep[0:64, :].rearrange("p (h q) -> p h q", h=8),
                          cm_bf[0:64, ccol:ccol + Q].unsqueeze(1).broadcast_to([64, 8, 128]))
                wtok("sync", e2d_v)
                e_v = dma("sync", "sRep", earep[0:64, :],
                          scr_line[0:1, h0 * 128:(h0 + 8) * 128].broadcast_to([64, 1024]))
                wait("scalar", "sT", t_diff)
                S(lambda s, qpm=qpm: s.activation(mp[:], qpm[:, 0:1024], AF.Exp))
                prev_m1[e8] = mark("scalar")
                m1 = (m1a, m1b)[e8 % 2]
                wait("vector", "sS", prev_m1[e8])
                if debug and e8 == 2:
                    wait("vector", "sDbg", sems["sDbg"][1])
                wtok("vector", e_v)
                def _mk(v, m1=m1):
                    v.tensor_mul(m1[:], mp[:], grep[:])
                    v.tensor_mul(m2[0:64, :], earep[0:64, :], cmrep[0:64, :])
                V(_mk)
                v_m1 = mark("vector")
                prev_mk[e8] = v_m1
                if debug and t == 0 and c == 0 and e8 == 0:
                    wait("sync", "sV", v_m1)
                    dma("sync", "sDbg", dbg["dbg_mask1"], m1[:])
                    wait("sync", "sDbg", sems["sDbg"][1])
                    wait("sync", "sV", v_G)
                    m_prev["dbgm1"] = dma("sync", "sDbg", dbg["dbg_gt"], gt_bf[:])
                    wait("sync", "sDbg", sems["sDbg"][1])
                ypm = PM[(e8 + 1) % 2]
                wait("tensor", "sV", v_m1)
                wait("tensor", "sV", v_X)
                if e8 >= 2:
                    wait("tensor", "sV", ydone[e8 - 2])
                def _ymm(tt, m1=m1, ypm=ypm, h0=h0, c=c):
                    for j in range(8):
                        h = h0 + j
                        tt.matmul(ypm[:, 1024 + j * 64: 1024 + (j + 1) * 64],
                                  m1[:, j * 128:(j + 1) * 128],
                                  Xc[:, h * 64:(h + 1) * 64], start=True, stop=False)
                        tt.matmul(ypm[:, 1024 + j * 64: 1024 + (j + 1) * 64],
                                  m2[0:64, j * 128:(j + 1) * 128],
                                  carry[0:64, h * 64:(h + 1) * 64], start=False, stop=True)
                TT(_ymm)
                t_y = mark("tensor")
                wait("vector", "sT", t_y)
                def _yev(v, ypm=ypm, h0=h0, c=c):
                    v.tensor_copy(y_tok[:, c * DI + h0 * 64: c * DI + (h0 + 8) * 64],
                                  ypm[:, 1024:1536])
                V(_yev)
                ydone.append(mark("vector"))
                prev_evac = ydone[-1]
            y_chunk_done = ydone[-1]

            d2ex = arena2_bf[:, 0:4096]
            wait("tensor", "sV", y_chunk_done)
            def _d2mm(tt):
                for half in range(2):
                    pm = PM[half]
                    for blk in range(4):
                        h0 = half * 32 + blk * 8
                        rhs = i64r[:, h0:h0 + 8].unsqueeze(2).broadcast_to([64, 8, 64])
                        tt.matmul(pm[:, blk * 512:(blk + 1) * 512].rearrange("p (h q) -> p h q", h=8),
                                  d2e_r[:], rhs, start=True, stop=True)
            TT(_d2mm)
            t_d2 = mark("tensor")
            wait("scalar", "sT", t_d2)
            def _d2ev(s):
                s.activation(d2ex[:, 0:2048], pmA[:, 0:2048], AF.Exp)
                s.activation(d2ex[:, 2048:4096], pmB[:, 0:2048], AF.Exp)
            S(_d2ev)
            s_d2 = mark("scalar")
            wait("vector", "sS", s_d2)
            V(lambda v: v.tensor_mul(Xd[:], Xc[:], d2ex[:]))
            v_Xd = mark("vector")
            wtok("tensor", btm_v)
            prev_cup = 0
            for half in range(2):
                hw0 = half * 2048
                wtok("sync", el_w)
                wait("sync", "sV", prev_cup)
                el_v = dma("sync", "sEv", e_sb[:], scr_eline[0:1, hw0:hw0 + 2048].broadcast_to([64, 2048]))
                wait("tensor", "sV", v_Xd)
                wait("tensor", "sV", prev_cup)
                def _sthalf(tt, hw0=hw0, c=c):
                    for blk in range(4):
                        tt.matmul(pmA[0:64, blk * 512:(blk + 1) * 512],
                                  b_tm[:, c * 64:(c + 1) * 64],
                                  Xd[:, hw0 + blk * 512: hw0 + (blk + 1) * 512],
                                  start=True, stop=True)
                TT(_sthalf)
                t_sth = mark("tensor")
                wait("vector", "sT", t_sth)
                wtok("vector", el_v)
                def _cup(v, hw0=hw0):
                    cu = arena3_bf[0:64, 4096 + hw0: 4096 + hw0 + 2048]
                    v.tensor_mul(cu, carry[0:64, hw0:hw0 + 2048], e_sb[:])
                    v.tensor_add(carry[0:64, hw0:hw0 + 2048], cu, pmA[0:64, 0:2048])
                V(_cup)
                prev_cup = mark("vector")
            v_carry = prev_cup
            prev_chunk_V = v_carry

        wait("sync", "sV", v_carry)
        xhl_v = dma("sync", "sXl", scr_xlast[0:1, :], xh_tm[127:128, (NCH - 1) * DI:NCH * DI])
        m_prev["xl"] = xhl_v

        dbg_ytok_tok = None
        if debug and t == 0:
            wait("sync", "sV", v_dterm) if False else None
            dbg_ytok_tok = dma("sync", "sDbg", dbg["dbg_ytok"], y_tok[:])
            wait("sync", "sDbg", sems["sDbg"][1])

        wait("sync", "sV", v_carry)
        if t == 0:
            wtok("sync", drep_v)
        wtok("sync", m_prev["drt"])
        drep_t = dma("sync", "sDrT", arena2_bf[:, 0:4096], scr_dline[0:1, :].broadcast_to([128, DI]))
        m_prev["drt"] = drep_t
        wtok("vector", drep_t)
        wtok("vector", dbg_ytok_tok)
        def _dterm(v):
            for c in range(NCH):
                v.tensor_mul(arena3_bf[:, 0:4096], xh_tm[:, c * DI:(c + 1) * DI],
                             arena2_bf[:, 0:4096])
                v.tensor_add(y_tok[:, c * DI:(c + 1) * DI], y_tok[:, c * DI:(c + 1) * DI],
                             arena3_bf[:, 0:4096])
        V(_dterm)
        v_dterm = mark("vector")

        y_fm = arena2_bf[:, 0:8192]
        wait("sync", "sV", v_dterm)
        ytr = None
        for kk in range(32):
            for c in range(NCH):
                ytr = dma("sync", "sTrY", y_fm[:, kk * T + c * 128: kk * T + c * 128 + 128],
                          y_tok[:, c * DI + kk * 128: c * DI + (kk + 1) * 128], transpose=True)
        wtok("vector", ytr)
        def _gate(v):
            v.tensor_mul(y_tok[:, 0:8192], y_fm[:], z_fm[:])
            v.tensor_mul(arena3_bf[:, 0:8192], y_tok[:, 0:8192], y_tok[:, 0:8192])
        V(_gate)
        v_gate = mark("vector")
        g_fm = y_tok[:, 0:8192]
        sq_fm = arena3_bf[:, 0:8192]
        wait("tensor", "sV", v_gate)
        def _ssq(tt):
            for kk in range(32):
                tt.matmul(pmA[0:1, 0:T], onesb[0:128, 0:1], sq_fm[:, kk * T:(kk + 1) * T],
                          start=(kk == 0), stop=(kk == 31))
        TT(_ssq)
        t_ssq = mark("tensor")
        wait("scalar", "sT", t_ssq)
        S(lambda s: s.copy(srow2[:], pmA[0:1, 0:T]))
        s_ssq = mark("scalar")
        wait("sync", "sS", s_ssq)
        r1 = dma("sync", "sRs", scr_row[0:1, 0:T], srow2[:])
        wtok("sync", r1)
        r2 = dma("sync", "sRs", rstd_c[:, 0:2], scr_row[0:1, 0:T].rearrange("o (c p) -> p (o c)", p=128), slow=True)
        wtok("scalar", r2)
        def _rstd(s):
            s.activation(rstd_c[:, 0:2], rstd_c[:, 0:2], AF.Ln, scale=float(1.0 / DI),
                         bias=rmseps[:, 0:1])
            s.activation(rstd_c[:, 0:2], rstd_c[:, 0:2], AF.Exp, scale=-0.5)
        S(_rstd)
        s_rstd = mark("scalar")

        attnout = arena3_bf[:, 0:4096]
        step2 = 0
        wo_mark = {}
        wo_evac = {}
        for g in range(4):
            pm = PM[g % 2]
            for k in range(32):
                b = step2 % 2
                wb = (wbufA, wbufB)[b]
                wait("sync", "sT", wo_mark.get(step2 - 2, 0))
                if step2 < 2:
                    wait("sync", "sT", t_ssq)
                wl_v = dma("sync", ("sWa", "sWb")[b], wb[:, 0:512],
                           scr_wout[g * 512:(g + 1) * 512, k * 128:(k + 1) * 128], transpose=True)
                wtok("tensor", wl_v)
                if k == 0:
                    wait("tensor", "sV", v_gate)
                    wait("tensor", "sS", s_ssq)
                    wait("tensor", "sS", wo_evac.get(g - 2, 0))
                def _wmm(tt, pm=pm, wb=wb, k=k):
                    for mc in range(4):
                        tt.matmul(pm[0:128, mc * 512:mc * 512 + T], wb[:, mc * 128:(mc + 1) * 128],
                                  g_fm[:, k * T:(k + 1) * T], start=(k == 0), stop=(k == 31))
                TT(_wmm)
                wo_mark[step2] = mark("tensor")
                step2 += 1
            wait("scalar", "sT", wo_mark[step2 - 1])
            S(lambda s, pm=pm, g=g: s.copy(
                attnout[:, g * 1024:(g + 1) * 1024].rearrange("p (a b) -> p a b", a=4),
                pm[:, 0:2048].rearrange("p (a b) -> p a b", a=4)[:, :, 0:T]))
            wo_evac[g] = mark("scalar")

        attn_tok = arena2_bf[:, 0:4096]
        wait("sync", "sS", wo_evac[3])
        atr = None
        for kk in range(16):
            for c in range(2):
                atr = dma("sync", "sTrA", attn_tok[:, c * 2048 + kk * 128: c * 2048 + (kk + 1) * 128],
                          attnout[:, kk * T + c * 128: kk * T + c * 128 + 128], transpose=True)
        wtok("vector", atr)
        wait("vector", "sS", s_rstd)
        def _res1(v):
            for c in range(2):
                v.scalar_tensor_tensor(x_tok[:, c * DIM:(c + 1) * DIM],
                                       attn_tok[:, c * 2048:(c + 1) * 2048],
                                       rstd_c[:, c:c + 1], x_tok[:, c * DIM:(c + 1) * DIM],
                                       ALU.mult, ALU.add)
        V(_res1)
        v_x1 = mark("vector")

        if debug and t == 0:
            wait("sync", "sV", v_x1)
            m_prev["dbgx1"] = dma("sync", "sDbg", dbg["dbg_x1"], x_tok[:])
            wait("sync", "sDbg", sems["sDbg"][1])

        ln2_tm = arena3_bf[:, 4096:8192]
        wait("vector", "sV", v_x1)
        def _l2a(v):
            for c in range(2):
                v.reduce_sum(mcol[:, c:c + 1], x_tok[:, c * DIM:(c + 1) * DIM], axis=AXX)
            v.tensor_scalar_mul(mcol[:, 0:2], mcol[:, 0:2], float(1.0 / DIM))
        V(_l2a)
        v_l2a = mark("vector")
        wait("scalar", "sV", v_l2a)
        def _l2b(s):
            for c in range(2):
                s.activation(arena2_bf[:, 4096 + c * DIM: 4096 + (c + 1) * DIM],
                             x_tok[:, c * DIM:(c + 1) * DIM], AF.Square)
        S(_l2b)
        s_l2b = mark("scalar")
        wait("vector", "sS", s_l2b)
        def _l2c(v):
            for c in range(2):
                v.reduce_sum(mcol[:, 2 + c:3 + c], arena2_bf[:, 4096 + c * DIM:4096 + (c + 1) * DIM],
                             axis=AXX)
            v.tensor_scalar_mul(mcol[:, 2:4], mcol[:, 2:4], float(1.0 / DIM))
            v.tensor_mul(rstd_c[:, 0:2], mcol[:, 0:2], mcol[:, 0:2])
            v.tensor_sub(mcol[:, 2:4], mcol[:, 2:4], rstd_c[:, 0:2])
        V(_l2c)
        v_l2c = mark("vector")
        wait("scalar", "sV", v_l2c)
        def _l2d(s):
            s.activation(mcol[:, 2:4], mcol[:, 2:4], AF.Ln, bias=epscol[:, 0:1])
            s.activation(mcol[:, 2:4], mcol[:, 2:4], AF.Exp, scale=-0.5)
        S(_l2d)
        s_l2d = mark("scalar")
        wait("vector", "sS", s_l2d)
        def _l2e(v):
            for c in range(2):
                v.tensor_scalar(ln2_tm[:, c * DIM:(c + 1) * DIM], x_tok[:, c * DIM:(c + 1) * DIM],
                                mcol[:, c:c + 1], mcol[:, 2 + c:3 + c], ALU.subtract, ALU.mult)
        V(_l2e)
        v_l2e = mark("vector")
        wait("sync", "sV", v_l2e)
        l2tr = None
        for c in range(2):
            for k in range(16):
                l2tr = dma("sync", "sTr1", arena2_bf[:, k * T + c * 128: k * T + c * 128 + 128],
                           ln2_tm[:, c * DIM + k * 128:c * DIM + (k + 1) * 128], transpose=True)
        wtok("vector", l2tr)
        V(lambda v: v.tensor_copy(ln2_f8[:], arena2_bf[:, 0:4096]))
        v_ln2f8 = mark("vector")

        gel = arena3_bf[:, 0:1024]
        ff1b = arena3_bf[:, 1024:2048]
        step3 = 0
        f_mark = {}
        f_evac = {}
        for g in range(16):
            for j, (scr, pm) in enumerate(((scr_wg0, pmA), (scr_wg1, pmB))):
                for k in range(16):
                    b = step3 % 2
                    wb = (w8A, w8B)[b]
                    wait("sync", "sT", f_mark.get(step3 - 2, 0))
                    if step3 < 2:
                        wait("sync", "sV", v_ln2f8)
                    wv = dma("sync", ("sWa", "sWb")[b], wb[:, 0:512],
                             scr[k * 128:(k + 1) * 128, g * 512:(g + 1) * 512])
                    wtok("tensor", wv)
                    if k == 0:
                        wait("tensor", "sV", v_ln2f8)
                        wait("tensor", "sS", f_evac.get((g - 1, j), 0))
                        wait("tensor", "sV", f_evac.get((g - 1, "a"), 0))
                    def _fmm(tt, pm=pm, wb=wb, k=k):
                        for mc in range(4):
                            tt.matmul(pm[0:128, mc * 512:mc * 512 + T], wb[:, mc * 128:(mc + 1) * 128],
                                      ln2_f8[:, k * T:(k + 1) * T], start=(k == 0),
                                      stop=(k == 15))
                    TT(_fmm)
                    f_mark[step3] = mark("tensor")
                    step3 += 1
            wait("scalar", "sT", f_mark[step3 - 1])
            def _fe(s, g=g):
                s.activation(gel[:].rearrange("p (a b) -> p a b", a=4),
                             pmA[:, 0:2048].rearrange("p (a b) -> p a b", a=4)[:, :, 0:T],
                             AF.Gelu_apprx_tanh)
                s.copy(ff1b[:].rearrange("p (a b) -> p a b", a=4),
                       pmB[:, 0:2048].rearrange("p (a b) -> p a b", a=4)[:, :, 0:T])
            S(_fe)
            f_evac[(g, 0)] = f_evac[(g, 1)] = mark("scalar")
            wait("vector", "sS", f_evac[(g, 0)])
            def _fa(v, g=g):
                v.tensor_mul(a8[:, g * 1024:(g + 1) * 1024], gel[:], ff1b[:])
            V(_fa)
            f_evac[(g, "a")] = mark("vector")

        ffnout = arena3_bf[:, 4096:8192]
        step4 = 0
        w3_mark = {}
        w3_evac = {}
        for g in range(4):
            pm = PM[g % 2]
            for k in range(64):
                b = step4 % 2
                wb = (w8A, w8B)[b]
                wait("sync", "sT", w3_mark.get(step4 - 2, 0))
                if step4 < 2:
                    wait("sync", "sS", f_evac[(15, 0)])
                    wait("sync", "sV", f_evac[(15, "a")])
                wv = dma("sync", ("sWa", "sWb")[b], wb[:, 0:512],
                         scr_wl[k * 128:(k + 1) * 128, g * 512:(g + 1) * 512])
                wtok("tensor", wv)
                if k == 0:
                    wait("tensor", "sS", f_evac[(15, 0)])
                    wait("tensor", "sV", f_evac[(15, "a")])
                    wait("tensor", "sS", w3_evac.get(g - 2, 0))
                def _f3(tt, pm=pm, wb=wb, k=k):
                    for mc in range(4):
                        tt.matmul(pm[0:128, mc * 512:mc * 512 + T], wb[:, mc * 128:(mc + 1) * 128],
                                  a8[:, k * T:(k + 1) * T], start=(k == 0),
                                  stop=(k == 63))
                TT(_f3)
                w3_mark[step4] = mark("tensor")
                step4 += 1
            wait("scalar", "sT", w3_mark[step4 - 1])
            S(lambda s, pm=pm, g=g: s.copy(
                ffnout[:, g * 1024:(g + 1) * 1024].rearrange("p (a b) -> p a b", a=4),
                pm[:, 0:2048].rearrange("p (a b) -> p a b", a=4)[:, :, 0:T]))
            w3_evac[g] = mark("scalar")

        wait("sync", "sS", w3_evac[3])
        ftr = None
        for kk in range(16):
            for c in range(2):
                ftr = dma("sync", "sTrA", arena2_bf[:, c * 2048 + kk * 128: c * 2048 + (kk + 1) * 128],
                          ffnout[:, kk * T + c * 128: kk * T + c * 128 + 128], transpose=True)
        wtok("vector", ftr)
        wtok("vector", m_prev["out_st"])
        V(lambda v: v.tensor_add(arena_f32[:, 0:2048], x_tok[:, 0:DIM],
                                 arena2_bf[:, 0:2048]))
        v_outa = mark("vector")
        wait("sync", "sV", v_outa)
        o1 = dma("sync", "sOut", y_d[r0:r0 + 128, :], arena_f32[:, 0:2048])
        wtok("vector", o1)
        V(lambda v: v.tensor_add(arena_f32[:, 0:2048], x_tok[:, DIM:2 * DIM],
                                 arena2_bf[:, 2048:4096]))
        v_out = mark("vector")
        wait("sync", "sV", v_out)
        o2 = dma("sync", "sOut", y_d[r0 + 128:r0 + 256, :], arena_f32[:, 0:2048])
        m_prev["out_st"] = o2
        m_prev["tile_done_V"] = v_out
        m_prev["tile_done_S"] = w3_evac[3]
        m_prev["tile_done_T"] = w3_mark[step4 - 1]

    for s in DMA_SEMS:
        if sems[s][1] > 0:
            wait("sync", s, sems[s][1])

    with nc.Block() as block:
        @block.sync
        def _(sync):
            for f in thunks["sync"]:
                f(sync)

        @block.gpsimd
        def _(g):
            for f in thunks["gpsimd"]:
                f(g)

        @block.vector
        def _(v):
            vp = DrainAfter(v)
            for f in thunks["vector"]:
                f(vp)

        @block.scalar
        def _(s):
            sp = DrainAfter(s)
            for f in thunks["scalar"]:
                f(sp)

        @block.tensor
        def _(tt):
            for f in thunks["tensor"]:
                f(tt)

    es.close()
    return nc


def make_inmaps(inputs):
    x = np.ascontiguousarray(inputs["L0_attn_x"], dtype=np.float32)
    cm = {}
    cm["w_in"] = np.ascontiguousarray(inputs["L0_attn_W_in"], np.float32)
    cm["w_lambda"] = np.ascontiguousarray(inputs["L0_attn_W_lambda"], np.float32)
    cm["w_theta"] = np.ascontiguousarray(inputs["L0_attn_W_theta"], np.float32)
    cm["w_out"] = np.ascontiguousarray(inputs["L0_attn_W_out"], np.float32)
    wg = np.ascontiguousarray(inputs["L0_ffn_w_gating"], np.float32)
    cm["wg0"] = np.ascontiguousarray(wg[0])
    cm["wg1"] = np.ascontiguousarray(wg[1])
    cm["wl"] = np.ascontiguousarray(inputs["L0_ffn_w_linear"], np.float32)
    cm["bbias_c"] = np.ascontiguousarray(np.asarray(inputs["L0_attn_B_bias"], np.float32).reshape(64, 1))
    cm["cbias_c"] = np.ascontiguousarray(np.asarray(inputs["L0_attn_C_bias"], np.float32).reshape(64, 1))
    cm["alog_c"] = np.ascontiguousarray(np.asarray(inputs["L0_attn_A_log"], np.float32).reshape(64, 1))
    cm["d_c"] = np.ascontiguousarray(np.asarray(inputs["L0_attn_D_param"], np.float32).reshape(64, 1))
    cm["dtb_c"] = np.ascontiguousarray(np.asarray(inputs["L0_attn_dt_bias"], np.float32).reshape(64, 1))
    cm["i64"] = np.eye(64, dtype=np.float32)
    cm["i128"] = np.eye(128, dtype=np.float32)
    tri = np.zeros((128, 128), np.float32)
    for lp in range(128):
        tri[lp, lp + 1:] = BIGNEG
    cm["trineg"] = tri
    maps = []
    for b in range(8):
        m = dict(cm)
        m["x"] = np.ascontiguousarray(x[b])
        maps.append(m)
    return maps


def kernel(**inputs):
    nc = build_program(debug=False)
    in_maps = make_inmaps(inputs)
    res = run_bass_kernel_spmd(nc, in_maps, core_ids=list(range(8)))
    out = np.stack([np.asarray(r["y"], np.float32) for r in res.results], axis=0)
    return out





# version 02
# speedup vs baseline: 1.0298x; 1.0298x over previous
import numpy as np
import ml_dtypes
import concourse.bass as bass
import concourse.mybir as mybir
from concourse.bass_utils import run_bass_kernel_spmd

dt = mybir.dt
AF = mybir.ActivationFunctionType
ALU = mybir.AluOpType
AXX = mybir.AxisListType.X
ML_BF16 = ml_dtypes.bfloat16

DIM = 2048
DI = 4096
NH = 64
HD = 64
HID = 8192
NPROJ = 8480
T = 256
NT = 2048 // T
Q = 128
NCH = T // Q
BIGNEG = -1.0e30
RMS_EPS = 1.1920929e-07
LN_EPS = 1e-5
INV2PI = float(1.0 / (2.0 * np.pi))
TWOPI = float(2.0 * np.pi)
GCONST = float(np.sqrt(2.0 / np.pi))
GC2 = 0.044715

F32, F32R, BF, FP8, I32 = dt.float32, dt.float32r, dt.bfloat16, dt.float8e4, dt.int32


def build_program(debug=False):
    nc = bass.Bass("TRN2", target_bir_lowering=False)

    def din(name, shape, dtype=F32):
        return nc.dram_tensor(name, shape, dtype, kind="ExternalInput").ap()

    x_d = din("x", [2048, DIM])
    w_in_d = din("w_in", [8384, DIM])
    w_lam_d = din("w_lambda", [64, DIM])
    w_th_d = din("w_theta", [32, DIM])
    w_out_d = din("w_out", [DIM, DI])
    wg0_d = din("wg0", [DIM, HID])
    wg1_d = din("wg1", [DIM, HID])
    wl_d = din("wl", [HID, DIM])
    bbias_d = din("bbias_c", [64, 1])
    cbias_d = din("cbias_c", [64, 1])
    alog_d = din("alog_c", [64, 1])
    dpar_d = din("d_c", [64, 1])
    dtb_d = din("dtb_c", [64, 1])
    i64_d = din("i64", [64, 64])
    i128_d = din("i128", [128, 128])
    trineg_d = din("trineg", [128, 128])

    y_d = nc.dram_tensor("y", [2048, DIM], F32, kind="ExternalOutput").ap()

    scr_wall = nc.dram_tensor("scr_wall", [NPROJ, DIM], BF, kind="Internal").ap()
    scr_wout = nc.dram_tensor("scr_wout", [DIM, DI], BF, kind="Internal").ap()
    scr_wg0 = nc.dram_tensor("scr_wg0", [DIM, HID], FP8, kind="Internal").ap()
    scr_wg1 = nc.dram_tensor("scr_wg1", [DIM, HID], FP8, kind="Internal").ap()
    scr_wl = nc.dram_tensor("scr_wl", [HID, DIM], FP8, kind="Internal").ap()
    scr_line = nc.dram_tensor("scr_line", [1, NH * Q], BF, kind="Internal").ap()
    scr_eline = nc.dram_tensor("scr_eline", [1, DI], BF, kind="Internal").ap()
    scr_xlast = nc.dram_tensor("scr_xlast", [1, DI], BF, kind="Internal").ap()
    scr_row = nc.dram_tensor("scr_row", [1, 256], F32, kind="Internal").ap()

    dbg = {}
    def dout(name, shape, dtype):
        dbg[name] = nc.dram_tensor(name, shape, dtype, kind="ExternalOutput").ap()
        return dbg[name]
    if debug:
        dout("dbg_ln1fm", [128, 16 * T], BF)
        dout("dbg_zfm", [128, 32 * T], BF)
        dout("dbg_bc", [128, T], F32)
        dout("dbg_dt", [64, T], F32)
        dout("dbg_ang", [32, T], F32)
        dout("dbg_bro", [64, T], F32)
        dout("dbg_mask1", [128, 1024], BF)
        dout("dbg_X", [128, 4096], BF)
        dout("dbg_gt", [128, 128], BF)
        dout("dbg_cro", [64, T], F32)
        dout("dbg_gb", [64, 2 * T], F32)
        dout("dbg_ge", [128, 4096], BF)
        dout("dbg_ytok", [128, NCH * DI], BF)
        dout("dbg_x1", [128, 2 * DIM], F32)
        dout("dbg_a8", [128, 8 * T], FP8)

    from contextlib import ExitStack
    es = ExitStack()
    sbt = lambda name, shape, dtype: es.enter_context(nc.sbuf_tensor(name, shape, dtype))
    pst = lambda name, shape: es.enter_context(nc.psum_tensor(name, shape, F32))

    x_tok = sbt("x_tok", [128, 2 * DIM], F32)
    ln1_fm = sbt("ln1_fm", [128, 16 * T], BF)
    z_fm = sbt("z_fm", [128, 32 * T], BF)
    xh_tm = sbt("xh_tm", [128, NCH * DI], BF)
    y_tok = sbt("y_tok", [128, NCH * DI], BF)
    carry = sbt("carry", [64, NH * HD], BF)
    arena_bf = sbt("arena_bf", [128, 8192], BF)
    arena2_bf = sbt("arena2_bf", [128, 8192], BF)
    arena3_bf = sbt("arena3_bf", [128, 8192], BF)
    arena_f32 = sbt("arena_f32", [128, 2048], F32)
    arena_i32 = sbt("arena_i32", [128, 2048], I32)
    wbufA = sbt("wbufA", [128, 1024], BF)
    wbufB = sbt("wbufB", [128, 1024], BF)
    w8A = sbt("w8A", [128, 2048], FP8)
    w8B = sbt("w8B", [128, 2048], FP8)
    a8 = sbt("a8", [128, 64 * T], FP8)
    ln2_f8 = sbt("ln2_f8", [128, 16 * T], FP8)
    trig = sbt("trig", [64, 2 * T], BF)
    zeros_t = sbt("zeros_t", [64, 256], F32)
    bc_t = sbt("bc_t", [128, T], F32)
    dtlam = sbt("dtlam", [128, T], F32)
    s128a = sbt("s128a", [128, T], F32)
    dt_t = sbt("dt_t", [64, T], F32)
    dt_r = sbt("dt_r", [64, T], F32R)
    lam_t = sbt("lam_t", [64, T], F32)
    a_fm_t = sbt("a_fm_t", [64, T], F32)
    gam_r = sbt("gam_r", [64, T], F32R)
    bet_r = sbt("bet_r", [64, T], F32R)
    s64a = sbt("s64a", [64, T], F32)
    s64b = sbt("s64b", [64, T], F32)
    s64c = sbt("s64c", [64, T], F32)
    craw = s64c[0:64, :]
    w1 = s64a[0:32, :]
    w2 = s64b[0:32, :]
    bmn = sbt("bmn", [64, T], F32)
    cmn = sbt("cmn", [64, T], F32)
    bm_bf = sbt("bm_bf", [64, T], BF)
    cm_bf = sbt("cm_bf", [64, T], BF)
    b_tm = sbt("b_tm", [128, NCH * 64], BF)
    theta_t = sbt("theta_t", [32, T], F32)
    ang_t = sbt("ang_t", [32, T], F32)
    ang_carry = sbt("ang_carry", [32, 1], F32)
    ang_i = arena_i32[0:32, 0:T]
    acs_fm = sbt("acs_fm", [64, Q], F32)
    acs_r = sbt("acs_r", [64, Q], F32R)
    nacs_r = sbt("nacs_r", [64, Q], F32R)
    acsline = sbt("acsline", [1, 1024], F32R)
    eacs_bf = sbt("eacs_bf", [64, Q], BF)
    d2e_f = sbt("d2e_f", [64, Q], F32)
    d2e_r = sbt("d2e_r", [64, Q], F32R)
    gt_bf = sbt("gt_bf", [128, 128], BF)
    e_sb = sbt("e_sb", [64, 2048], BF)
    row_f = sbt("row_f", [1, T], F32)
    row_r = sbt("row_r", [1, T], F32R)
    dtavg_row = row_f
    dtavg_r = row_r
    srow2 = row_f
    srow2_r = row_r
    rstd_c = sbt("rstd_c", [128, 2], F32)
    mcol = sbt("mcol", [128, 4], F32)
    i64c = sbt("i64c_s", [64, 64], F32)
    i64r = sbt("i64r", [64, 64], F32R)
    i128r = sbt("i128r", [128, 128], F32R)
    trinr = sbt("trinr", [128, 128], F32R)
    onesf = sbt("onesf_s", [128, 1], F32)
    ones64 = sbt("ones64", [64, 64], F32)
    exp_t = sbt("exp_t", [64, 64], BF)
    e_col = sbt("e_col", [64, 1], F32)
    onesr = sbt("onesr", [128, 128], F32R)
    onesb = sbt("onesb", [128, 128], BF)
    smalls_dc = sbt("smalls_dc", [64, 1], F32)
    bbias = sbt("bbias", [64, 1], F32)
    cbias = sbt("cbias", [64, 1], F32)
    dtb = sbt("dtb", [64, 1], F32)
    alog_s = sbt("alog_s", [64, 1], F32)
    mexpA = sbt("mexpA", [64, 1], F32)
    epscol = sbt("epscol", [128, 1], F32)
    rmseps = sbt("rmseps", [128, 1], F32)
    halfpi = sbt("halfpi", [128, 1], F32)

    pmA = pst("pmA", [128, 2048])
    pmB = pst("pmB", [128, 2048])
    PM = [pmA, pmB]

    thunks = {e: [] for e in ("sync", "gpsimd", "vector", "scalar", "tensor")}
    sems = {}
    DMA_SEMS = ("sInit", "sDl", "sDrT", "sPa", "sPb", "sPstA", "sPstB", "sG8a", "sG8b",
                "sP8a", "sP8b", "sX", "sTr1", "sTrXh", "sTrB", "sTrY", "sTrA",
                "sWa", "sWb", "sLam", "sCm", "sTrg", "sRp1", "sRp2", "sSh", "sEw",
                "sEv", "sE2d", "sRep", "sAcl", "sXl", "sRs", "sOut", "sDbg")
    for s in ("sT", "sV", "sS") + DMA_SEMS:
        sems[s] = [es.enter_context(nc.semaphore(s)), 0]

    def on(eng, fn):
        thunks[eng].append(fn)

    def wait(eng, sname, value):
        if value <= 0:
            return
        h = sems[sname][0]
        on(eng, lambda e, h=h, v=value: e.wait_ge(h, v))

    state = {"last_T": None, "last_SS": None}

    def _is_sbuf(ap):
        try:
            return str(ap.space) in ("MemorySpace.SBUF", "SBUF") or "SB" in str(ap.space)
        except Exception:
            return False

    def dma(eng, sname, out, in_, transpose=False, slow=False):
        h = sems[sname][0]
        if eng == "sync":
            ss = (not transpose) and _is_sbuf(out) and _is_sbuf(in_)
            if transpose and state["last_SS"]:
                sn = state["last_SS"][0]
                wait(eng, sn, sems[sn][1])
            if ss and state["last_T"]:
                sn = state["last_T"][0]
                wait(eng, sn, sems[sn][1])
        sems[sname][1] += 16
        v = sems[sname][1]
        on(eng, lambda e, o=out, i=in_, t=transpose, h=h, sl=slow: e.dma_start(
            o, i, transpose=t, allow_slow_non_contiguous=sl).then_inc(h, 16))
        tok = (sname, v)
        if eng == "sync":
            if transpose:
                state["last_T"] = tok
            elif _is_sbuf(out) and _is_sbuf(in_):
                state["last_SS"] = tok
        return tok

    def wtok(eng, tok):
        if tok is None or tok == 0:
            return
        sname, v = tok
        wait(eng, sname, v)

    ENG_SEM = {"tensor": "sT", "vector": "sV", "scalar": "sS", "gpsimd": "sG_nop"}

    DATAPATH = {"tensor_copy", "tensor_tensor", "tensor_add", "tensor_sub", "tensor_mul",
                "tensor_max", "tensor_scalar", "tensor_single_scalar", "tensor_scalar_add",
                "tensor_scalar_sub", "tensor_scalar_mul", "tensor_scalar_max",
                "tensor_scalar_min", "scalar_tensor_tensor", "tensor_reduce", "reduce_max",
                "reduce_sum", "reciprocal", "memset", "memzero", "tensor_tensor_scan",
                "select", "activation", "copy", "mul", "add", "square", "sqrt", "sign",
                "tensor_relu", "transpose", "matmul", "ldweights"}

    class DrainAfter:

        def __init__(self, e):
            self._e = e
            self._touched = set()
            self.last = None

        @staticmethod
        def _names(args, kw):
            out = set()
            for a in list(args) + list(kw.values()):
                nm = getattr(getattr(a, "tensor", None), "name", None)
                if nm is not None:
                    out.add(nm)
            return out

        def __getattr__(self, n):
            a = getattr(self._e, n)
            if n in DATAPATH:
                if n in ("matmul", "ldweights", "transpose"):
                    def wm(*args, **kw):
                        r = a(*args, **kw)
                        self.last = r
                        return r
                    return wm
                def w(*args, **kw):
                    names = self._names(args, kw)
                    if names & self._touched:
                        self._e.drain()
                        self._touched = set()
                    r = a(*args, **kw)
                    self._touched |= names
                    self.last = r
                    return r
                return w
            if n == "drain":
                def wd(*args, **kw):
                    self._touched = set()
                    r = a(*args, **kw)
                    self.last = r
                    return r
                return wd
            return a

    def mark(eng):
        sname = {"tensor": "sT", "vector": "sV", "scalar": "sS"}[eng]
        h = sems[sname][0]
        sems[sname][1] += 1
        v = sems[sname][1]

        def _m(e, h=h):
            last = getattr(e, "last", None)
            ok = False
            if last is not None:
                try:
                    last.then_inc(h, 1)
                    ok = True
                except Exception:
                    ok = False
                e.last = None
            if not ok:
                e.drain().then_inc(h, 1)
        on(eng, _m)
        return v

    def attach_inc(eng, buildfn):
        sname = {"tensor": "sT", "vector": "sV", "scalar": "sS"}[eng]
        h = sems[sname][0]
        sems[sname][1] += 1
        v = sems[sname][1]
        on(eng, lambda e, h=h, f=buildfn: f(e).then_inc(h, 1))
        return v

    V = lambda fn: on("vector", fn)
    S = lambda fn: on("scalar", fn)
    TT = lambda fn: on("tensor", fn)

    dma("sync", "sInit", i64c[:], i64_d)
    dma("sync", "sInit", bbias[:], bbias_d)
    dma("sync", "sInit", cbias[:], cbias_d)
    dma("sync", "sInit", dtb[:], dtb_d)
    dma("sync", "sInit", alog_s[:], alog_d)
    dma("sync", "sInit", arena_f32[:, 0:128], i128_d)
    dma("sync", "sInit", arena_f32[:, 128:256], trineg_d)
    c_loaded = dcol_v = dma("sync", "sInit", smalls_dc[:], dpar_d)

    wtok("vector", c_loaded)
    def _cinit(v):
        v.tensor_copy(i64r[:], i64c[:])
        v.tensor_copy(i128r[:], arena_f32[:, 0:128])
        v.tensor_copy(trinr[:], arena_f32[:, 128:256])
        v.memset(onesf[:], 1.0)
        v.memset(ones64[:], 1.0)
        v.memset(arena_f32[:, 256:384], 1.0)
        v.tensor_copy(onesr[:], arena_f32[:, 256:384])
        v.tensor_copy(onesb[:], arena_f32[:, 256:384])
        v.memset(epscol[:], LN_EPS)
        v.memset(rmseps[:], RMS_EPS)
        v.memset(halfpi[:], float(np.pi / 2))
        v.memset(zeros_t[:], 0.0)
        v.memset(carry[:], 0.0)
        v.memset(ang_carry[:], 0.0)
    V(_cinit)
    v_cinit = mark("vector")

    wtok("scalar", c_loaded)
    S(lambda s: s.activation(mexpA[:], alog_s[:], AF.Exp))
    S(lambda s: s.mul(mexpA[:], mexpA[:], -1.0))
    s_mexpA = mark("scalar")

    scr_dline = nc.dram_tensor("scr_dline", [1, DI], BF, kind="Internal").ap()
    V(lambda v: v.tensor_scalar(exp_t[:], ones64[:], smalls_dc[:, 0:1], None, ALU.mult))
    v_dline = mark("vector")
    wait("sync", "sV", v_dline)
    drep_v = dma("sync", "sDl", scr_dline[0:1, :].rearrange("o (h p) -> o h p", h=64), exp_t[:])

    inbufs = [arena_f32, x_tok[:, 2048:4096]]
    wq_out = arena3_bf[:, 0:4096]
    prev_q2 = [v_cinit, v_cinit]
    prev_st = [None, None]

    def fq4_chunk(idx, nrows, loads, dst):
        buf = inbufs[idx % 2]
        wq = wq_out[:, (idx % 2) * 2048:(idx % 2) * 2048 + 2048]
        wait("sync", "sV", prev_q2[idx % 2])
        psem = ("sPa", "sPb")[idx % 2]
        vload = None
        for sl, src in loads:
            vload = dma("sync", psem, sl, src)
        wtok("vector", vload)
        def _q1(v, buf=buf, nrows=nrows):
            for s_ in range(16):
                v.reduce_max(s128a[0:nrows, s_:s_ + 1],
                             buf[0:nrows, s_ * 128:(s_ + 1) * 128],
                             axis=AXX, apply_absolute_value=True)
            v.tensor_scalar_max(s128a[0:nrows, 0:16], s128a[0:nrows, 0:16], 1e-10)
            v.reciprocal(s128a[0:nrows, 16:32], s128a[0:nrows, 0:16])
            v.tensor_scalar_mul(s128a[0:nrows, 0:16], s128a[0:nrows, 0:16], float(1.0 / 7.0))
            for s_ in range(16):
                v.tensor_scalar(buf[0:nrows, s_ * 128:(s_ + 1) * 128],
                                buf[0:nrows, s_ * 128:(s_ + 1) * 128],
                                s128a[0:nrows, 16 + s_:17 + s_], None, ALU.mult)
            v.tensor_scalar(buf[0:nrows, :], buf[0:nrows, :], 7.0, 8.0, ALU.mult, ALU.add)
        V(_q1)
        q1 = mark("vector")
        wait("scalar", "sV", q1)
        def _qcast(s, buf=buf, nrows=nrows):
            s.copy(arena_i32[0:nrows, 0:1024], buf[0:nrows, 0:1024])
        S(_qcast)
        qs = mark("scalar")
        wait("vector", "sS", qs)
        def _q2b(v, buf=buf, wq=wq, nrows=nrows):
            v.tensor_copy(x_tok[0:nrows, 0:1024], arena_i32[0:nrows, 0:1024])
        V(_q2b)
        q2a = mark("vector")
        wait("scalar", "sV", q2a)
        def _qcast2(s, buf=buf, nrows=nrows):
            s.copy(arena_i32[0:nrows, 0:1024], buf[0:nrows, 1024:2048])
        S(_qcast2)
        qs2 = mark("scalar")
        wait("vector", "sS", qs2)
        wtok("vector", prev_st[idx % 2])
        def _q2c(v, buf=buf, wq=wq, nrows=nrows):
            v.tensor_copy(x_tok[0:nrows, 1024:2048], arena_i32[0:nrows, 0:1024])
            v.scalar_tensor_tensor(arena_bf[0:nrows, 0:2048], buf[0:nrows, :], 0.5,
                                   x_tok[0:nrows, 0:2048], ALU.subtract, ALU.is_ge)
            v.tensor_add(buf[0:nrows, :], x_tok[0:nrows, 0:2048], arena_bf[0:nrows, 0:2048])
            for s_ in range(16):
                v.tensor_scalar(wq[0:nrows, s_ * 128:(s_ + 1) * 128],
                                buf[0:nrows, s_ * 128:(s_ + 1) * 128],
                                -8.0, s128a[0:nrows, s_:s_ + 1], ALU.add, ALU.mult)
        V(_q2c)
        q2 = mark("vector")
        prev_q2[idx % 2] = q2
        wait("sync", "sV", q2)
        wtok("sync", prev_st[idx % 2])
        prev_st[idx % 2] = dma("sync", ("sPstA", "sPstB")[idx % 2], dst, wq[0:nrows, :])
        return q2


    idx = 0
    for ci in range(67):
        r0 = ci * 128
        nrows = min(128, NPROJ - r0)
        buf = inbufs[idx % 2]
        loads = []
        for srcap, lo, hi in ((w_in_d, 0, 8384), (w_lam_d, 8384, 8448), (w_th_d, 8448, 8480)):
            s0, s1 = max(r0, lo), min(r0 + nrows, hi)
            if s0 < s1:
                loads.append((buf[s0 - r0:s1 - r0, :], srcap[s0 - lo:s1 - lo, :]))
        fq4_chunk(idx, nrows, loads, scr_wall[r0:r0 + nrows, :])
        idx += 1
    for ci in range(32):
        r0 = (ci // 2) * 128
        c0 = (ci % 2) * 2048
        buf = inbufs[idx % 2]
        fq4_chunk(idx, 128, [(buf[:], w_out_d[r0:r0 + 128, c0:c0 + 2048])],
                  scr_wout[r0:r0 + 128, c0:c0 + 2048])
        idx += 1

    prev8 = [None, None]
    j8 = 0
    def f8hop(src_ap, dst_ap):
        nonlocal j8
        b = (w8A, w8B)[j8 % 2]
        wtok("gpsimd", prev8[j8 % 2])
        v8 = dma("gpsimd", ("sG8a", "sG8b")[j8 % 2], b[:], src_ap)
        wtok("sync", v8)
        prev8[j8 % 2] = dma("sync", ("sP8a", "sP8b")[j8 % 2], dst_ap, b[:])
        j8 += 1
    for srcd, dstd in ((wg0_d, scr_wg0), (wg1_d, scr_wg1)):
        for ci in range(16):
            r0 = ci * 128
            for half in range(4):
                c0 = half * 2048
                f8hop(srcd[r0:r0 + 128, c0:c0 + 2048], dstd[r0:r0 + 128, c0:c0 + 2048])
    for ci in range(64):
        r0 = ci * 128
        f8hop(wl_d[r0:r0 + 128, :], scr_wl[r0:r0 + 128, :])

    prep_V = sems["sV"][1]
    prep_toks = [(s, sems[s][1]) for s in ("sPstA", "sPstB", "sP8a", "sP8b", "sDl")
                 if sems[s][1] > 0]

    m_prev = {"tile_done_V": prep_V, "tile_done_S": 0, "tile_done_T": 0,
              "out_st": None, "xl": None, "drt": None}

    for t in range(NT):
        r0 = t * T
        if t == 0:
            for tk in prep_toks:
                wtok("sync", tk)
        wait("sync", "sV", m_prev["tile_done_V"])
        wait("sync", "sS", m_prev["tile_done_S"])
        wait("sync", "sT", m_prev["tile_done_T"])
        wtok("sync", m_prev.get("dbgx1"))
        x_in = None
        for c in range(2):
            x_in = dma("sync", "sX", x_tok[:, c * DIM:(c + 1) * DIM], x_d[r0 + c * 128:r0 + (c + 1) * 128, :])

        ln_tm = arena3_bf[:, 4096:8192]
        wtok("vector", x_in)
        wait("vector", "sS", s_mexpA)
        def _ln(v, src=x_tok):
            for c in range(2):
                xc = src[:, c * DIM:(c + 1) * DIM]
                v.reduce_sum(mcol[:, c:c + 1], xc, axis=AXX)
            v.tensor_scalar_mul(mcol[:, 0:2], mcol[:, 0:2], float(1.0 / DIM))
        V(_ln)
        vln = mark("vector")
        wait("scalar", "sV", vln)
        def _lns(s, src=x_tok):
            for c in range(2):
                s.activation(arena2_bf[:, c * DIM:(c + 1) * DIM], src[:, c * DIM:(c + 1) * DIM],
                             AF.Square)
        S(_lns)
        sln = mark("scalar")
        wait("vector", "sS", sln)
        def _ln2(v, src=x_tok, dstv=ln_tm):
            for c in range(2):
                v.reduce_sum(mcol[:, 2 + c:3 + c], arena2_bf[:, c * DIM:(c + 1) * DIM], axis=AXX)
            v.tensor_scalar_mul(mcol[:, 2:4], mcol[:, 2:4], float(1.0 / DIM))
            v.tensor_mul(rstd_c[:, 0:2], mcol[:, 0:2], mcol[:, 0:2])
            v.tensor_sub(mcol[:, 2:4], mcol[:, 2:4], rstd_c[:, 0:2])
        V(_ln2)
        vln2 = mark("vector")
        wait("scalar", "sV", vln2)
        def _ln3(s):
            s.activation(mcol[:, 2:4], mcol[:, 2:4], AF.Ln, bias=epscol[:, 0:1])
            s.activation(mcol[:, 2:4], mcol[:, 2:4], AF.Exp, scale=-0.5)
        S(_ln3)
        sln3 = mark("scalar")
        wait("vector", "sS", sln3)
        def _ln4(v, src=x_tok, dstv=ln_tm):
            for c in range(2):
                v.tensor_scalar(dstv[:, c * DIM:(c + 1) * DIM], src[:, c * DIM:(c + 1) * DIM],
                                mcol[:, c:c + 1], mcol[:, 2 + c:3 + c], ALU.subtract, ALU.mult)
        V(_ln4)
        vln4 = mark("vector")
        wait("sync", "sV", vln4)
        tr_v = None
        for c in range(2):
            for k in range(16):
                tr_v = dma("sync", "sTr1", ln1_fm[:, k * T + c * 128: k * T + c * 128 + 128],
                           ln_tm[:, c * DIM + k * 128: c * DIM + (k + 1) * 128], transpose=True)
        ln1fm_done = tr_v

        if debug and t == 0:
            wtok("sync", ln1fm_done)
            dma("sync", "sDbg", dbg["dbg_ln1fm"], ln1_fm[:])
            wait("sync", "sDbg", sems["sDbg"][1])

        xh_fm = arena_bf[:, 0:8192]
        wload = [None, None]
        gmark = {}
        evac_mark = {}
        step = 0
        for g in range(17):
            rg0 = g * 512
            mg = min(512, NPROJ - rg0)
            pm = PM[g % 2]
            for k in range(16):
                b = step % 2
                wb = (wbufA, wbufB)[b]
                wait("sync", "sT", gmark.get(("buf", step - 2), 0))
                wload[b] = dma("sync", ("sWa", "sWb")[b], wb[:, 0:mg],
                               scr_wall[rg0:rg0 + mg, k * 128:(k + 1) * 128], transpose=True)
                wtok("tensor", wload[b])
                if k == 0:
                    wait("tensor", "sS", evac_mark.get(g - 2, 0))
                    wtok("tensor", ln1fm_done)
                def _mm(tt, pm=pm, wb=wb, k=k, mg=mg):
                    nmc = (mg + 127) // 128
                    for mc in range(nmc):
                        M = min(128, mg - mc * 128)
                        tt.matmul(pm[0:M, mc * 512:mc * 512 + T],
                                  wb[:, mc * 128:mc * 128 + M],
                                  ln1_fm[:, k * T:(k + 1) * T],
                                  start=(k == 0), stop=(k == 15))
                TT(_mm)
                gmark[("buf", step)] = mark("tensor")
                step += 1
            gm = gmark[("buf", step - 1)]
            wait("scalar", "sT", gm)
            pm_str = pm[:, 0:2048].rearrange("p (a b) -> p a b", a=4)[:, :, 0:T]
            if g < 8:
                S(lambda s, pm_str=pm_str, g=g: s.activation(
                    z_fm[:, g * 1024:(g + 1) * 1024].rearrange("p (a b) -> p a b", a=4),
                    pm_str, AF.Silu))
            elif g < 16:
                S(lambda s, pm_str=pm_str, g=g: s.activation(
                    xh_fm[:, (g - 8) * 1024:(g - 7) * 1024].rearrange("p (a b) -> p a b", a=4),
                    pm_str, AF.Silu))
            else:
                def _e8(s, pm=pm):
                    s.copy(bc_t[:], pm[:, 0:T])
                    s.copy(dtlam[:], pm[:, 512:512 + T])
                    s.copy(theta_t[:], pm[0:32, 1024:1024 + T])
                S(_e8)
            evac_mark[g] = mark("scalar")
        evac_mark[7] = evac_mark[15]
        inproj_S = evac_mark[16]

        if debug and t == 0:
            wait("sync", "sS", evac_mark[7])
            dma("sync", "sDbg", dbg["dbg_zfm"], z_fm[:])
            wait("sync", "sDbg", sems["sDbg"][1])
            wait("sync", "sS", inproj_S)
            dma("sync", "sDbg", dbg["dbg_bc"], bc_t[:])
            wait("sync", "sDbg", sems["sDbg"][1])

        wtok("sync", m_prev["xl"])
        wait("sync", "sS", evac_mark[7])
        xh_tr = None
        for kk in range(32):
            for c in range(2):
                xh_tr = dma("sync", "sTrXh", xh_tm[:, c * DI + kk * 128: c * DI + (kk + 1) * 128],
                            xh_fm[:, kk * T + c * 128: kk * T + c * 128 + 128], transpose=True)
        xh_tm_done = xh_tr

        wait("scalar", "sS", inproj_S)
        def _sm1(s):
            s.activation(s64a[:], dtlam[0:64, :], AF.Abs, bias=dtb[:, 0:1])
            s.activation(s64a[:], s64a[:], AF.Exp, scale=-1.0)
            s.activation(s64a[:], s64a[:], AF.Ln, bias=onesf[0:64, 0:1])
            s.activation(s64b[:], dtlam[0:64, :], AF.Relu, bias=dtb[:, 0:1])
            s.activation(s128a[64:128, :], dtlam[64:128, :], AF.Exp, scale=-1.0)
        S(_sm1)
        sm1 = mark("scalar")
        wait("vector", "sS", sm1)
        def _sm2(v):
            v.tensor_add(dt_t[:], s64a[:], s64b[:])
            v.tensor_copy(dt_r[:], dt_t[:])
            v.tensor_scalar(a_fm_t[:], dt_t[:], mexpA[:, 0:1], None, ALU.mult)
            v.tensor_scalar(s128a[64:128, :], s128a[64:128, :], 1.0, None, ALU.add)
            v.reciprocal(s128a[64:128, :], s128a[64:128, :])
        V(_sm2)
        sm2 = mark("vector")
        wait("sync", "sV", sm2)
        lam_mv = dma("sync", "sLam", lam_t[:], s128a[64:128, :])
        wait("tensor", "sV", sm2)
        wait("tensor", "sS", evac_mark.get(7, 0))
        wait("tensor", "sS", inproj_S)
        TT(lambda tt: tt.matmul(pmA[0:1, 0:T], onesr[0:64, 0:1], dt_r[:], start=True, stop=True))
        t_dtavg = mark("tensor")
        wait("scalar", "sT", t_dtavg)
        S(lambda s: s.activation(dtavg_row[:], pmA[0:1, 0:T], AF.Copy, scale=float(1.0 / NH)))
        s_dtavg = mark("scalar")
        wait("vector", "sS", s_dtavg)
        V(lambda v: v.tensor_copy(dtavg_r[:], dtavg_row[:]))
        v_dtavg = mark("vector")
        wait("tensor", "sV", v_dtavg)
        TT(lambda tt: tt.matmul(pmA[0:32, 0:T], onesr[0:1, 0:32], dtavg_r[:], start=True, stop=True))
        t_dtrep = mark("tensor")
        w3 = s64c[0:32, :]
        wait("vector", "sT", t_dtrep)
        def _ang(v):
            v.tensor_mul(w1[:], theta_t[:], pmA[0:32, 0:T])
            v.tensor_tensor_scan(ang_t[:], w1[:], zeros_t[0:32, 0:T], 0.0, ALU.add, ALU.add)
            v.tensor_scalar(ang_t[:], ang_t[:], ang_carry[:, 0:1], None, ALU.add)
            v.tensor_copy(ang_carry[:], ang_t[:, T - 1:T])
            v.tensor_scalar_mul(w1[:], ang_t[:], INV2PI)
        V(_ang)
        v_ang = mark("vector")
        wait("scalar", "sV", v_ang)
        S(lambda s: s.copy(ang_i[:], w1[:]))
        s_angi = mark("scalar")
        wait("vector", "sS", s_angi)
        def _ang2(v):
            v.tensor_copy(w2[:], ang_i[:])
            v.scalar_tensor_tensor(w3[:], w1[:], 0.5, w2[:], ALU.subtract, ALU.is_ge)
            v.scalar_tensor_tensor(theta_t[:], w1[:], 0.5, w2[:], ALU.add, ALU.is_le)
            v.tensor_add(w2[:], w2[:], w3[:])
            v.tensor_sub(w2[:], w2[:], theta_t[:])
            v.scalar_tensor_tensor(w1[:], w2[:], -TWOPI, ang_t[:], ALU.mult, ALU.add)
            v.tensor_scalar(w2[:], ang_t[:], INV2PI, 0.25, ALU.mult, ALU.add)
        V(_ang2)
        v_ang2 = mark("vector")
        wait("scalar", "sV", v_ang2)
        S(lambda s: s.copy(ang_i[:], w2[:]))
        s_angi2 = mark("scalar")
        wait("vector", "sS", s_angi2)
        def _ang3(v):
            v.tensor_copy(w3[:], ang_i[:])
            v.scalar_tensor_tensor(theta_t[:], w2[:], 0.5, w3[:], ALU.subtract, ALU.is_ge)
            v.scalar_tensor_tensor(w2[:], w2[:], 0.5, w3[:], ALU.add, ALU.is_le)
            v.tensor_add(w3[:], w3[:], theta_t[:])
            v.tensor_sub(w3[:], w3[:], w2[:])
            v.scalar_tensor_tensor(w2[:], w3[:], -TWOPI, ang_t[:], ALU.mult, ALU.add)
            v.tensor_scalar(w2[:], w2[:], float(np.pi / 2), None, ALU.add)
        V(_ang3)
        v_ang3 = mark("vector")
        wait("scalar", "sV", v_ang3)
        def _trig(s):
            s.activation(trig[0:32, 0:T], w1[:], AF.Sin)
            s.activation(trig[0:32, T:2 * T], w2[:], AF.Sin)
        S(_trig)
        s_trig = mark("scalar")
        wait("sync", "sS", s_trig)
        trig_mv = dma("sync", "sTrg", trig[32:64, :], trig[0:32, :])

        if debug and t == 0:
            wait("sync", "sV", v_ang)
            dma("sync", "sDbg", dbg["dbg_ang"], ang_t[:])
            wait("sync", "sDbg", sems["sDbg"][1])
            wait("sync", "sS", inproj_S)
            dma("sync", "sDbg", dbg["dbg_dt"], dt_t[:])
            wait("sync", "sDbg", sems["sDbg"][1])

        wait("sync", "sV", v_ang3)
        cm_mv = dma("sync", "sCm", craw[:], bc_t[64:128, :])
        wait("vector", "sS", inproj_S)
        wait("vector", "sS", s_trig)
        wait("vector", "sT", t_dtrep)
        def _bc1(v):
            v.tensor_mul(s64a[:], bc_t[0:64, :], bc_t[0:64, :])
            v.tensor_copy(dt_r[:], s64a[:])
        V(_bc1)
        v_bc1 = mark("vector")
        wait("tensor", "sV", v_bc1)
        TT(lambda tt: tt.matmul(pmB[0:1, 0:T], onesr[0:64, 0:1], dt_r[:], start=True, stop=True))
        t_bssq = mark("tensor")
        wtok("vector", cm_mv)
        wait("vector", "sT", t_bssq)
        def _bc2(v):
            v.tensor_mul(s64b[:], craw[:], craw[:])
            v.tensor_copy(dt_r[:], s64b[:])
        V(_bc2)
        v_bc2 = mark("vector")
        wait("tensor", "sV", v_bc2)
        TT(lambda tt: tt.matmul(pmB[0:1, T:2 * T], onesr[0:64, 0:1], dt_r[:], start=True, stop=True))
        t_cssq = mark("tensor")
        wait("scalar", "sT", t_cssq)
        def _bc3(s):
            s.activation(srow2[:], pmB[0:1, 0:T], AF.Ln, scale=float(1.0 / 64), bias=rmseps[0:1, 0:1])
            s.activation(srow2[:], srow2[:], AF.Exp, scale=-0.5)
        S(_bc3)
        s_brstd = mark("scalar")
        wait("vector", "sS", s_brstd)
        V(lambda v: v.tensor_copy(srow2_r[:], srow2[:]))
        v_brstd = mark("vector")
        wait("tensor", "sV", v_brstd)
        TT(lambda tt: tt.matmul(pmA[0:64, 0:T], onesr[0:1, 0:64], srow2_r[:], start=True, stop=True))
        t_brep = mark("tensor")
        wait("scalar", "sT", t_brep)
        def _bc4(s):
            s.activation(srow2[:], pmB[0:1, T:2 * T], AF.Ln, scale=float(1.0 / 64), bias=rmseps[0:1, 0:1])
            s.activation(srow2[:], srow2[:], AF.Exp, scale=-0.5)
        S(_bc4)
        s_crstd = mark("scalar")
        wait("vector", "sS", s_crstd)
        wait("vector", "sT", t_brep)
        def _bc5(v):
            v.tensor_mul(bmn[:], bc_t[0:64, :], pmA[0:64, 0:T])
            v.tensor_scalar(bmn[:], bmn[:], bbias[:, 0:1], None, ALU.add)
            v.tensor_copy(srow2_r[:], srow2[:])
        V(_bc5)
        v_bc5 = mark("vector")
        wait("tensor", "sV", v_bc5)
        TT(lambda tt: tt.matmul(pmA[0:64, T:2 * T], onesr[0:1, 0:64], srow2_r[:], start=True, stop=True))
        t_crep = mark("tensor")
        wait("vector", "sT", t_crep)
        def _bc6(v):
            v.tensor_mul(cmn[:], craw[:], pmA[0:64, T:2 * T])
            v.tensor_scalar(cmn[:], cmn[:], cbias[:, 0:1], None, ALU.add)
        V(_bc6)
        v_bc6 = mark("vector")

        def rope(src, v_src_mark, prev_mv2=None):
            wait("sync", "sV", v_src_mark)
            mv1 = dma("sync", "sRp1", w1[:], src[32:64, :])
            wtok("vector", mv1)
            wtok("vector", prev_mv2)
            wtok("vector", trig_mv)
            def _r(v, src=src):
                sin0 = trig[0:32, 0:T]
                cos0 = trig[0:32, T:2 * T]
                v.tensor_mul(w2[:], w1[:], sin0)
                v.tensor_mul(theta_t[:], src[0:32, :], cos0)
                v.tensor_sub(theta_t[:], theta_t[:], w2[:])
                v.tensor_mul(w2[:], src[0:32, :], sin0)
                v.tensor_mul(w1[:], w1[:], cos0)
                v.tensor_add(w2[:], w2[:], w1[:])
                v.tensor_copy(src[0:32, :], theta_t[:])
            V(_r)
            vr = mark("vector")
            wait("sync", "sV", vr)
            mv2 = dma("sync", "sRp2", src[32:64, :], w2[:])
            return mv2
        b_roped = rope(bmn, v_bc5)
        c_roped = rope(cmn, v_bc6, prev_mv2=b_roped)
        if debug and t == 0:
            wtok("sync", c_roped)
            dma("sync", "sDbg", dbg["dbg_cro"], cmn[:])
            wait("sync", "sDbg", sems["sDbg"][1])
        wtok("vector", c_roped)
        def _bc7(v):
            v.tensor_copy(bm_bf[:], bmn[:])
            v.tensor_copy(cm_bf[:], cmn[:])
        V(_bc7)
        v_bcbf = mark("vector")
        wait("sync", "sV", v_bcbf)
        btm_v = None
        for c in range(NCH):
            btm_v = dma("sync", "sTrB", b_tm[:, c * 64:(c + 1) * 64],
                        bm_bf[0:64, c * 128:(c + 1) * 128], transpose=True)

        if debug and t == 0:
            wait("sync", "sV", v_bcbf)
            dma("sync", "sDbg", dbg["dbg_bro"], bmn[:])
            wait("sync", "sDbg", sems["sDbg"][1])

        wait("scalar", "sV", sm2)
        wait("scalar", "sV", v_bc6)
        S(lambda s: s.activation(s64c[:], a_fm_t[:], AF.Exp))
        s_expA = mark("scalar")
        wait("vector", "sS", s_expA)
        wtok("vector", lam_mv)
        wtok("vector", c_roped)
        def _gb(v):
            v.tensor_mul(s64a[:], lam_t[:], dt_t[:])
            v.tensor_copy(gam_r[:], s64a[:])
            v.tensor_scalar(s64b[:], lam_t[:], -1.0, 1.0, ALU.mult, ALU.add)
            v.tensor_mul(s64b[:], s64b[:], dt_t[:])
            v.tensor_mul(s64b[:], s64b[:], s64c[:])
            v.tensor_copy(bet_r[:], s64b[:])
        V(_gb)
        v_gb = mark("vector")
        if debug and t == 0:
            wait("sync", "sV", v_gb)
            dma("sync", "sDbg", dbg["dbg_gb"][0:64, 0:T], s64a[:])
            wait("sync", "sDbg", sems["sDbg"][1])
            dma("sync", "sDbg", dbg["dbg_gb"][0:64, T:2 * T], s64b[:])
            wait("sync", "sDbg", sems["sDbg"][1])

        prev_chunk_V = 0
        for c in range(NCH):
            ccol = c * Q
            xh_c = xh_tm[:, c * DI:(c + 1) * DI]
            xhs = arena_bf[:, 0:4096]
            Xc = arena3_bf[:, 0:4096]
            Xd = arena3_bf[:, 4096:8192]
            ge = arena2_bf[:, 0:4096]
            be = arena2_bf[:, 4096:8192]

            wtok("sync", xh_tm_done)
            wait("sync", "sS", evac_mark[7])
            wait("sync", "sV", prev_chunk_V)
            if c == 0:
                sh1 = dma("sync", "sSh", xhs[1:128, :], xh_c[0:127, :])
                if t == 0:
                    sh2 = sh1
                    wtok("vector", sh1)
                    V(lambda v: v.memset(xhs[0:1, :], 0.0))
                    mark("vector")
                else:
                    wtok("sync", m_prev["xl"])
                    sh2 = dma("sync", "sSh", xhs[0:1, :], scr_xlast[0:1, :])
            else:
                sh1 = dma("sync", "sSh", xhs[1:128, :], xh_c[0:127, :])
                sh2 = dma("sync", "sSh", xhs[0:1, :], xh_tm[127:128, (c - 1) * DI:c * DI])
            wait("tensor", "sV", v_gb)
            wait("tensor", "sV", prev_chunk_V)
            def _gbe(tt, ccol=ccol):
                for half in range(2):
                    pm = PM[half]
                    for blk in range(4):
                        h0 = half * 32 + blk * 8
                        rhs = i64r[:, h0:h0 + 8].unsqueeze(2).broadcast_to([64, 8, 64])
                        tt.matmul(pm[:, blk * 512:(blk + 1) * 512].rearrange("p (h q) -> p h q", h=8),
                                  gam_r[:, ccol:ccol + Q], rhs, start=True, stop=True)
            TT(_gbe)
            t_ge = mark("tensor")
            wait("scalar", "sT", t_ge)
            def _gee(s):
                s.copy(ge[:, 0:2048], pmA[:, 0:2048])
                s.copy(ge[:, 2048:4096], pmB[:, 0:2048])
            S(_gee)
            s_ge = mark("scalar")
            if debug and t == 0 and c == 0:
                wait("sync", "sS", s_ge)
                dma("sync", "sDbg", dbg["dbg_ge"], ge[:])
                wait("sync", "sDbg", sems["sDbg"][1])
            wait("tensor", "sS", s_ge)
            def _bbe(tt, ccol=ccol):
                for half in range(2):
                    pm = PM[half]
                    for blk in range(4):
                        h0 = half * 32 + blk * 8
                        rhs = i64r[:, h0:h0 + 8].unsqueeze(2).broadcast_to([64, 8, 64])
                        tt.matmul(pm[:, blk * 512:(blk + 1) * 512].rearrange("p (h q) -> p h q", h=8),
                                  bet_r[:, ccol:ccol + Q], rhs, start=True, stop=True)
            TT(_bbe)
            t_be = mark("tensor")
            wait("scalar", "sT", t_be)
            def _bee(s):
                s.copy(be[:, 0:2048], pmA[:, 0:2048])
                s.copy(be[:, 2048:4096], pmB[:, 0:2048])
            S(_bee)
            s_be = mark("scalar")
            wait("vector", "sS", s_be)
            if debug:
                wait("vector", "sDbg", sems["sDbg"][1])
            wtok("vector", sh2)
            def _X(v, xh_c=xh_c):
                v.tensor_mul(Xc[:], xh_c[:], ge[:])
                v.tensor_mul(Xd[:], xhs[:], be[:])
                v.tensor_add(Xc[:], Xc[:], Xd[:])
            V(_X)
            v_X = mark("vector")

            wait("vector", "sV", v_gb)
            def _acs(v, ccol=ccol):
                v.tensor_tensor_scan(acs_fm[:], a_fm_t[:, ccol:ccol + Q],
                                     zeros_t[0:64, 0:Q], 0.0, ALU.add, ALU.add)
                v.tensor_copy(acs_r[:], acs_fm[:])
                v.tensor_scalar(nacs_r[:], acs_fm[:], -1.0, None, ALU.mult)
            V(_acs)
            v_acs = mark("vector")
            wait("scalar", "sV", v_acs)
            def _acs2(s):
                s.activation(eacs_bf[:], acs_fm[:], AF.Exp)
            S(_acs2)
            s_acs2 = mark("scalar")
            wait("vector", "sS", s_acs2)
            def _vd2e(v):
                v.tensor_scalar(d2e_f[:], acs_fm[:], acs_fm[:, Q - 1:Q], -1.0,
                                ALU.subtract, ALU.mult)
                v.tensor_copy(d2e_r[:], d2e_f[:])
            V(_vd2e)
            v_d2e = mark("vector")
            wait("scalar", "sV", v_acs)
            S(lambda s: s.activation(e_col[:], acs_fm[:, Q - 1:Q], AF.Exp))
            s_erow = mark("scalar")
            wait("vector", "sS", s_erow)
            V(lambda v: v.tensor_scalar(exp_t[:], ones64[:], e_col[:, 0:1], None, ALU.mult))
            v_erow = mark("vector")
            wait("sync", "sV", v_erow)
            el_w = dma("sync", "sEw", scr_eline[0:1, :].rearrange("o (h p) -> o h p", h=64), exp_t[:])

            wait("sync", "sS", s_acs2)
            e2d_v = dma("sync", "sE2d", scr_line[:].rearrange("o (h q) -> o h q", h=64), eacs_bf[:])
            if debug and t == 0 and c == 0:
                wait("sync", "sV", v_X)
                m_prev["dbgX"] = dma("sync", "sDbg", dbg["dbg_X"], Xc[:])
                wait("sync", "sDbg", sems["sDbg"][1])
            wait("tensor", "sV", v_bcbf)
            wait("tensor", "sS", s_be)
            TT(lambda tt, ccol=ccol: tt.matmul(pmA[0:128, 0:128], bm_bf[:, ccol:ccol + Q],
                                               cm_bf[:, ccol:ccol + Q], start=True, stop=True))
            t_G = mark("tensor")
            wait("vector", "sT", t_G)
            V(lambda v: v.tensor_copy(gt_bf[:], pmA[0:128, 0:128]))
            v_G = mark("vector")

            mp = arena_bf[:, 4096:5120]
            m1a = arena_bf[:, 5120:6144]
            m1b = arena_bf[:, 6144:7168]
            grep = arena_bf[:, 0:1024]
            m2 = arena_bf[:, 1024:2048]
            cmrep = arena_bf[:, 2048:3072]
            earep = arena_bf[:, 3072:4096]
            ydone = []
            prev_evac = 0
            prev_m1 = {}
            prev_diff = {}
            prev_mk = {}
            for e8 in range(8):
                h0 = e8 * 8
                qpm = PM[e8 % 2]
                aslot = acsline[0:1, 0:1024]
                wait("sync", "sT", prev_diff.get(e8 - 1, 0))
                wait("sync", "sV", v_acs)
                acl_v = dma("sync", "sAcl", aslot.rearrange("o (h q) -> o h q", h=8),
                            acs_r[h0:h0 + 8, :])
                wtok("tensor", acl_v)
                wait("tensor", "sV", v_acs)
                wait("tensor", "sS", prev_m1.get(e8 - 2, 0))
                if e8 < 2:
                    wait("tensor", "sS", s_be)
                    wait("tensor", "sV", v_G)
                def _dmm(tt, h0=h0, qpm=qpm, aslot=aslot):
                    for half in range(2):
                        sl = qpm[:, half * 512:(half + 1) * 512]
                        h1 = h0 + half * 4
                        tt.matmul(sl, onesr[0:1, 0:128],
                                  aslot[0:1, half * 512:(half + 1) * 512],
                                  start=True, stop=False)
                        rhs = i64r[:, h1:h1 + 4].unsqueeze(2).broadcast_to([64, 4, 128])
                        tt.matmul(sl.rearrange("p (h q) -> p h q", h=4),
                                  nacs_r[:], rhs, start=False, stop=False)
                        rhs2 = i128r[:, :].unsqueeze(1).broadcast_to([128, 4, 128])
                        tt.matmul(sl.rearrange("p (h q) -> p h q", h=4),
                                  trinr[:], rhs2, start=False, stop=True)
                TT(_dmm)
                t_diff = mark("tensor")
                prev_diff[e8] = t_diff
                wait("sync", "sV", v_X if e8 == 0 else 0)
                wait("sync", "sV", prev_mk.get(e8 - 1, 0))
                wait("sync", "sV", v_G)
                g_v = dma("sync", "sRep", grep[:].rearrange("p (h q) -> p h q", h=8),
                          gt_bf[:].unsqueeze(1).broadcast_to([128, 8, 128]))
                c_v = dma("sync", "sRep", cmrep[0:64, :].rearrange("p (h q) -> p h q", h=8),
                          cm_bf[0:64, ccol:ccol + Q].unsqueeze(1).broadcast_to([64, 8, 128]))
                wtok("sync", e2d_v)
                e_v = dma("sync", "sRep", earep[0:64, :],
                          scr_line[0:1, h0 * 128:(h0 + 8) * 128].broadcast_to([64, 1024]))
                wait("scalar", "sT", t_diff)
                S(lambda s, qpm=qpm: s.activation(mp[:], qpm[:, 0:1024], AF.Exp))
                prev_m1[e8] = mark("scalar")
                m1 = (m1a, m1b)[e8 % 2]
                wait("vector", "sS", prev_m1[e8])
                if debug and e8 == 2:
                    wait("vector", "sDbg", sems["sDbg"][1])
                wtok("vector", e_v)
                def _mk(v, m1=m1):
                    v.tensor_mul(m1[:], mp[:], grep[:])
                    v.tensor_mul(m2[0:64, :], earep[0:64, :], cmrep[0:64, :])
                V(_mk)
                v_m1 = mark("vector")
                prev_mk[e8] = v_m1
                if debug and t == 0 and c == 0 and e8 == 0:
                    wait("sync", "sV", v_m1)
                    dma("sync", "sDbg", dbg["dbg_mask1"], m1[:])
                    wait("sync", "sDbg", sems["sDbg"][1])
                    wait("sync", "sV", v_G)
                    m_prev["dbgm1"] = dma("sync", "sDbg", dbg["dbg_gt"], gt_bf[:])
                    wait("sync", "sDbg", sems["sDbg"][1])
                ypm = PM[(e8 + 1) % 2]
                wait("tensor", "sV", v_m1)
                wait("tensor", "sV", v_X)
                if e8 >= 2:
                    wait("tensor", "sV", ydone[e8 - 2])
                def _ymm(tt, m1=m1, ypm=ypm, h0=h0, c=c):
                    for j in range(8):
                        h = h0 + j
                        tt.matmul(ypm[:, 1024 + j * 64: 1024 + (j + 1) * 64],
                                  m1[:, j * 128:(j + 1) * 128],
                                  Xc[:, h * 64:(h + 1) * 64], start=True, stop=False)
                        tt.matmul(ypm[:, 1024 + j * 64: 1024 + (j + 1) * 64],
                                  m2[0:64, j * 128:(j + 1) * 128],
                                  carry[0:64, h * 64:(h + 1) * 64], start=False, stop=True)
                TT(_ymm)
                t_y = mark("tensor")
                wait("vector", "sT", t_y)
                def _yev(v, ypm=ypm, h0=h0, c=c):
                    v.tensor_copy(y_tok[:, c * DI + h0 * 64: c * DI + (h0 + 8) * 64],
                                  ypm[:, 1024:1536])
                V(_yev)
                ydone.append(mark("vector"))
                prev_evac = ydone[-1]
            y_chunk_done = ydone[-1]

            d2ex = arena2_bf[:, 0:4096]
            wait("tensor", "sV", y_chunk_done)
            def _d2mm(tt):
                for half in range(2):
                    pm = PM[half]
                    for blk in range(4):
                        h0 = half * 32 + blk * 8
                        rhs = i64r[:, h0:h0 + 8].unsqueeze(2).broadcast_to([64, 8, 64])
                        tt.matmul(pm[:, blk * 512:(blk + 1) * 512].rearrange("p (h q) -> p h q", h=8),
                                  d2e_r[:], rhs, start=True, stop=True)
            TT(_d2mm)
            t_d2 = mark("tensor")
            wait("scalar", "sT", t_d2)
            def _d2ev(s):
                s.activation(d2ex[:, 0:2048], pmA[:, 0:2048], AF.Exp)
                s.activation(d2ex[:, 2048:4096], pmB[:, 0:2048], AF.Exp)
            S(_d2ev)
            s_d2 = mark("scalar")
            wait("vector", "sS", s_d2)
            V(lambda v: v.tensor_mul(Xd[:], Xc[:], d2ex[:]))
            v_Xd = mark("vector")
            wtok("tensor", btm_v)
            prev_cup = 0
            for half in range(2):
                hw0 = half * 2048
                wtok("sync", el_w)
                wait("sync", "sV", prev_cup)
                el_v = dma("sync", "sEv", e_sb[:], scr_eline[0:1, hw0:hw0 + 2048].broadcast_to([64, 2048]))
                wait("tensor", "sV", v_Xd)
                wait("tensor", "sV", prev_cup)
                def _sthalf(tt, hw0=hw0, c=c):
                    for blk in range(4):
                        tt.matmul(pmA[0:64, blk * 512:(blk + 1) * 512],
                                  b_tm[:, c * 64:(c + 1) * 64],
                                  Xd[:, hw0 + blk * 512: hw0 + (blk + 1) * 512],
                                  start=True, stop=True)
                TT(_sthalf)
                t_sth = mark("tensor")
                wait("vector", "sT", t_sth)
                wtok("vector", el_v)
                def _cup(v, hw0=hw0):
                    cu = arena3_bf[0:64, 4096 + hw0: 4096 + hw0 + 2048]
                    v.tensor_mul(cu, carry[0:64, hw0:hw0 + 2048], e_sb[:])
                    v.tensor_add(carry[0:64, hw0:hw0 + 2048], cu, pmA[0:64, 0:2048])
                V(_cup)
                prev_cup = mark("vector")
            v_carry = prev_cup
            prev_chunk_V = v_carry

        wait("sync", "sV", v_carry)
        xhl_v = dma("sync", "sXl", scr_xlast[0:1, :], xh_tm[127:128, (NCH - 1) * DI:NCH * DI])
        m_prev["xl"] = xhl_v

        dbg_ytok_tok = None
        if debug and t == 0:
            wait("sync", "sV", v_dterm) if False else None
            dbg_ytok_tok = dma("sync", "sDbg", dbg["dbg_ytok"], y_tok[:])
            wait("sync", "sDbg", sems["sDbg"][1])

        wait("sync", "sV", v_carry)
        if t == 0:
            wtok("sync", drep_v)
        wtok("sync", m_prev["drt"])
        drep_t = dma("sync", "sDrT", arena2_bf[:, 0:4096], scr_dline[0:1, :].broadcast_to([128, DI]))
        m_prev["drt"] = drep_t
        wtok("vector", drep_t)
        wtok("vector", dbg_ytok_tok)
        def _dterm(v):
            for c in range(NCH):
                v.tensor_mul(arena3_bf[:, 0:4096], xh_tm[:, c * DI:(c + 1) * DI],
                             arena2_bf[:, 0:4096])
                v.tensor_add(y_tok[:, c * DI:(c + 1) * DI], y_tok[:, c * DI:(c + 1) * DI],
                             arena3_bf[:, 0:4096])
        V(_dterm)
        v_dterm = mark("vector")

        y_fm = arena2_bf[:, 0:8192]
        wait("sync", "sV", v_dterm)
        ytr = None
        for kk in range(32):
            for c in range(NCH):
                ytr = dma("sync", "sTrY", y_fm[:, kk * T + c * 128: kk * T + c * 128 + 128],
                          y_tok[:, c * DI + kk * 128: c * DI + (kk + 1) * 128], transpose=True)
        wtok("vector", ytr)
        def _gate(v):
            v.tensor_mul(y_tok[:, 0:8192], y_fm[:], z_fm[:])
            v.tensor_mul(arena3_bf[:, 0:8192], y_tok[:, 0:8192], y_tok[:, 0:8192])
        V(_gate)
        v_gate = mark("vector")
        g_fm = y_tok[:, 0:8192]
        sq_fm = arena3_bf[:, 0:8192]
        wait("tensor", "sV", v_gate)
        def _ssq(tt):
            for kk in range(32):
                tt.matmul(pmA[0:1, 0:T], onesb[0:128, 0:1], sq_fm[:, kk * T:(kk + 1) * T],
                          start=(kk == 0), stop=(kk == 31))
        TT(_ssq)
        t_ssq = mark("tensor")
        wait("scalar", "sT", t_ssq)
        S(lambda s: s.copy(srow2[:], pmA[0:1, 0:T]))
        s_ssq = mark("scalar")
        wait("sync", "sS", s_ssq)
        r1 = dma("sync", "sRs", scr_row[0:1, 0:T], srow2[:])
        wtok("sync", r1)
        r2 = dma("sync", "sRs", rstd_c[:, 0:2], scr_row[0:1, 0:T].rearrange("o (c p) -> p (o c)", p=128), slow=True)
        wtok("scalar", r2)
        def _rstd(s):
            s.activation(rstd_c[:, 0:2], rstd_c[:, 0:2], AF.Ln, scale=float(1.0 / DI),
                         bias=rmseps[:, 0:1])
            s.activation(rstd_c[:, 0:2], rstd_c[:, 0:2], AF.Exp, scale=-0.5)
        S(_rstd)
        s_rstd = mark("scalar")

        attnout = arena3_bf[:, 0:4096]
        step2 = 0
        wo_mark = {}
        wo_evac = {}
        for g in range(4):
            pm = PM[g % 2]
            for k in range(32):
                b = step2 % 2
                wb = (wbufA, wbufB)[b]
                wait("sync", "sT", wo_mark.get(step2 - 2, 0))
                if step2 < 2:
                    wait("sync", "sT", t_ssq)
                wl_v = dma("sync", ("sWa", "sWb")[b], wb[:, 0:512],
                           scr_wout[g * 512:(g + 1) * 512, k * 128:(k + 1) * 128], transpose=True)
                wtok("tensor", wl_v)
                if k == 0:
                    wait("tensor", "sV", v_gate)
                    wait("tensor", "sS", s_ssq)
                    wait("tensor", "sS", wo_evac.get(g - 2, 0))
                def _wmm(tt, pm=pm, wb=wb, k=k):
                    for mc in range(4):
                        tt.matmul(pm[0:128, mc * 512:mc * 512 + T], wb[:, mc * 128:(mc + 1) * 128],
                                  g_fm[:, k * T:(k + 1) * T], start=(k == 0), stop=(k == 31))
                TT(_wmm)
                wo_mark[step2] = mark("tensor")
                step2 += 1
            wait("scalar", "sT", wo_mark[step2 - 1])
            S(lambda s, pm=pm, g=g: s.copy(
                attnout[:, g * 1024:(g + 1) * 1024].rearrange("p (a b) -> p a b", a=4),
                pm[:, 0:2048].rearrange("p (a b) -> p a b", a=4)[:, :, 0:T]))
            wo_evac[g] = mark("scalar")

        attn_tok = arena2_bf[:, 0:4096]
        wait("sync", "sS", wo_evac[3])
        atr = None
        for kk in range(16):
            for c in range(2):
                atr = dma("sync", "sTrA", attn_tok[:, c * 2048 + kk * 128: c * 2048 + (kk + 1) * 128],
                          attnout[:, kk * T + c * 128: kk * T + c * 128 + 128], transpose=True)
        wtok("vector", atr)
        wait("vector", "sS", s_rstd)
        def _res1(v):
            for c in range(2):
                v.scalar_tensor_tensor(x_tok[:, c * DIM:(c + 1) * DIM],
                                       attn_tok[:, c * 2048:(c + 1) * 2048],
                                       rstd_c[:, c:c + 1], x_tok[:, c * DIM:(c + 1) * DIM],
                                       ALU.mult, ALU.add)
        V(_res1)
        v_x1 = mark("vector")

        if debug and t == 0:
            wait("sync", "sV", v_x1)
            m_prev["dbgx1"] = dma("sync", "sDbg", dbg["dbg_x1"], x_tok[:])
            wait("sync", "sDbg", sems["sDbg"][1])

        ln2_tm = arena3_bf[:, 4096:8192]
        wait("vector", "sV", v_x1)
        def _l2a(v):
            for c in range(2):
                v.reduce_sum(mcol[:, c:c + 1], x_tok[:, c * DIM:(c + 1) * DIM], axis=AXX)
            v.tensor_scalar_mul(mcol[:, 0:2], mcol[:, 0:2], float(1.0 / DIM))
        V(_l2a)
        v_l2a = mark("vector")
        wait("scalar", "sV", v_l2a)
        def _l2b(s):
            for c in range(2):
                s.activation(arena2_bf[:, 4096 + c * DIM: 4096 + (c + 1) * DIM],
                             x_tok[:, c * DIM:(c + 1) * DIM], AF.Square)
        S(_l2b)
        s_l2b = mark("scalar")
        wait("vector", "sS", s_l2b)
        def _l2c(v):
            for c in range(2):
                v.reduce_sum(mcol[:, 2 + c:3 + c], arena2_bf[:, 4096 + c * DIM:4096 + (c + 1) * DIM],
                             axis=AXX)
            v.tensor_scalar_mul(mcol[:, 2:4], mcol[:, 2:4], float(1.0 / DIM))
            v.tensor_mul(rstd_c[:, 0:2], mcol[:, 0:2], mcol[:, 0:2])
            v.tensor_sub(mcol[:, 2:4], mcol[:, 2:4], rstd_c[:, 0:2])
        V(_l2c)
        v_l2c = mark("vector")
        wait("scalar", "sV", v_l2c)
        def _l2d(s):
            s.activation(mcol[:, 2:4], mcol[:, 2:4], AF.Ln, bias=epscol[:, 0:1])
            s.activation(mcol[:, 2:4], mcol[:, 2:4], AF.Exp, scale=-0.5)
        S(_l2d)
        s_l2d = mark("scalar")
        wait("vector", "sS", s_l2d)
        def _l2e(v):
            for c in range(2):
                v.tensor_scalar(ln2_tm[:, c * DIM:(c + 1) * DIM], x_tok[:, c * DIM:(c + 1) * DIM],
                                mcol[:, c:c + 1], mcol[:, 2 + c:3 + c], ALU.subtract, ALU.mult)
        V(_l2e)
        v_l2e = mark("vector")
        wait("sync", "sV", v_l2e)
        l2tr = None
        for c in range(2):
            for k in range(16):
                l2tr = dma("sync", "sTr1", arena2_bf[:, k * T + c * 128: k * T + c * 128 + 128],
                           ln2_tm[:, c * DIM + k * 128:c * DIM + (k + 1) * 128], transpose=True)
        wtok("vector", l2tr)
        V(lambda v: v.tensor_copy(ln2_f8[:], arena2_bf[:, 0:4096]))
        v_ln2f8 = mark("vector")

        gel = arena3_bf[:, 0:1024]
        ff1b = arena3_bf[:, 1024:2048]
        step3 = 0
        f_mark = {}
        f_evac = {}
        for g in range(16):
            for j, (scr, pm) in enumerate(((scr_wg0, pmA), (scr_wg1, pmB))):
                for k in range(16):
                    b = step3 % 2
                    wb = (w8A, w8B)[b]
                    wait("sync", "sT", f_mark.get(step3 - 2, 0))
                    if step3 < 2:
                        wait("sync", "sV", v_ln2f8)
                    wv = dma("sync", ("sWa", "sWb")[b], wb[:, 0:512],
                             scr[k * 128:(k + 1) * 128, g * 512:(g + 1) * 512])
                    wtok("tensor", wv)
                    if k == 0:
                        wait("tensor", "sV", v_ln2f8)
                        wait("tensor", "sS", f_evac.get((g - 1, j), 0))
                        wait("tensor", "sV", f_evac.get((g - 1, "a"), 0))
                    def _fmm(tt, pm=pm, wb=wb, k=k):
                        for mc in range(4):
                            tt.matmul(pm[0:128, mc * 512:mc * 512 + T], wb[:, mc * 128:(mc + 1) * 128],
                                      ln2_f8[:, k * T:(k + 1) * T], start=(k == 0),
                                      stop=(k == 15))
                    TT(_fmm)
                    f_mark[step3] = mark("tensor")
                    step3 += 1
            wait("scalar", "sT", f_mark[step3 - 1])
            def _fe(s, g=g):
                s.activation(gel[:].rearrange("p (a b) -> p a b", a=4),
                             pmA[:, 0:2048].rearrange("p (a b) -> p a b", a=4)[:, :, 0:T],
                             AF.Gelu_apprx_tanh)
                s.copy(ff1b[:].rearrange("p (a b) -> p a b", a=4),
                       pmB[:, 0:2048].rearrange("p (a b) -> p a b", a=4)[:, :, 0:T])
            S(_fe)
            f_evac[(g, 0)] = f_evac[(g, 1)] = mark("scalar")
            wait("vector", "sS", f_evac[(g, 0)])
            def _fa(v, g=g):
                v.tensor_mul(a8[:, g * 1024:(g + 1) * 1024], gel[:], ff1b[:])
            V(_fa)
            f_evac[(g, "a")] = mark("vector")

        ffnout = arena3_bf[:, 4096:8192]
        step4 = 0
        w3_mark = {}
        w3_evac = {}
        for g in range(4):
            pm = PM[g % 2]
            for k in range(64):
                b = step4 % 2
                wb = (w8A, w8B)[b]
                wait("sync", "sT", w3_mark.get(step4 - 2, 0))
                if step4 < 2:
                    wait("sync", "sS", f_evac[(15, 0)])
                    wait("sync", "sV", f_evac[(15, "a")])
                wv = dma("sync", ("sWa", "sWb")[b], wb[:, 0:512],
                         scr_wl[k * 128:(k + 1) * 128, g * 512:(g + 1) * 512])
                wtok("tensor", wv)
                if k == 0:
                    wait("tensor", "sS", f_evac[(15, 0)])
                    wait("tensor", "sV", f_evac[(15, "a")])
                    wait("tensor", "sS", w3_evac.get(g - 2, 0))
                def _f3(tt, pm=pm, wb=wb, k=k):
                    for mc in range(4):
                        tt.matmul(pm[0:128, mc * 512:mc * 512 + T], wb[:, mc * 128:(mc + 1) * 128],
                                  a8[:, k * T:(k + 1) * T], start=(k == 0),
                                  stop=(k == 63))
                TT(_f3)
                w3_mark[step4] = mark("tensor")
                step4 += 1
            wait("scalar", "sT", w3_mark[step4 - 1])
            S(lambda s, pm=pm, g=g: s.copy(
                ffnout[:, g * 1024:(g + 1) * 1024].rearrange("p (a b) -> p a b", a=4),
                pm[:, 0:2048].rearrange("p (a b) -> p a b", a=4)[:, :, 0:T]))
            w3_evac[g] = mark("scalar")

        wait("sync", "sS", w3_evac[3])
        ftr = None
        for kk in range(16):
            for c in range(2):
                ftr = dma("sync", "sTrA", arena2_bf[:, c * 2048 + kk * 128: c * 2048 + (kk + 1) * 128],
                          ffnout[:, kk * T + c * 128: kk * T + c * 128 + 128], transpose=True)
        wtok("vector", ftr)
        wtok("vector", m_prev["out_st"])
        V(lambda v: v.tensor_add(arena_f32[:, 0:2048], x_tok[:, 0:DIM],
                                 arena2_bf[:, 0:2048]))
        v_outa = mark("vector")
        wait("sync", "sV", v_outa)
        o1 = dma("sync", "sOut", y_d[r0:r0 + 128, :], arena_f32[:, 0:2048])
        wtok("vector", o1)
        V(lambda v: v.tensor_add(arena_f32[:, 0:2048], x_tok[:, DIM:2 * DIM],
                                 arena2_bf[:, 2048:4096]))
        v_out = mark("vector")
        wait("sync", "sV", v_out)
        o2 = dma("sync", "sOut", y_d[r0 + 128:r0 + 256, :], arena_f32[:, 0:2048])
        m_prev["out_st"] = o2
        m_prev["tile_done_V"] = v_out
        m_prev["tile_done_S"] = w3_evac[3]
        m_prev["tile_done_T"] = w3_mark[step4 - 1]

    for s in DMA_SEMS:
        if sems[s][1] > 0:
            wait("sync", s, sems[s][1])

    with nc.Block() as block:
        @block.sync
        def _(sync):
            for f in thunks["sync"]:
                f(sync)

        @block.gpsimd
        def _(g):
            for f in thunks["gpsimd"]:
                f(g)

        @block.vector
        def _(v):
            vp = DrainAfter(v)
            for f in thunks["vector"]:
                f(vp)

        @block.scalar
        def _(s):
            sp = DrainAfter(s)
            for f in thunks["scalar"]:
                f(sp)

        @block.tensor
        def _(tt):
            tp_ = DrainAfter(tt)
            for f in thunks["tensor"]:
                f(tp_)

    es.close()
    return nc


def make_inmaps(inputs):
    x = np.ascontiguousarray(inputs["L0_attn_x"], dtype=np.float32)
    cm = {}
    cm["w_in"] = np.ascontiguousarray(inputs["L0_attn_W_in"], np.float32)
    cm["w_lambda"] = np.ascontiguousarray(inputs["L0_attn_W_lambda"], np.float32)
    cm["w_theta"] = np.ascontiguousarray(inputs["L0_attn_W_theta"], np.float32)
    cm["w_out"] = np.ascontiguousarray(inputs["L0_attn_W_out"], np.float32)
    wg = np.ascontiguousarray(inputs["L0_ffn_w_gating"], np.float32)
    cm["wg0"] = np.ascontiguousarray(wg[0])
    cm["wg1"] = np.ascontiguousarray(wg[1])
    cm["wl"] = np.ascontiguousarray(inputs["L0_ffn_w_linear"], np.float32)
    cm["bbias_c"] = np.ascontiguousarray(np.asarray(inputs["L0_attn_B_bias"], np.float32).reshape(64, 1))
    cm["cbias_c"] = np.ascontiguousarray(np.asarray(inputs["L0_attn_C_bias"], np.float32).reshape(64, 1))
    cm["alog_c"] = np.ascontiguousarray(np.asarray(inputs["L0_attn_A_log"], np.float32).reshape(64, 1))
    cm["d_c"] = np.ascontiguousarray(np.asarray(inputs["L0_attn_D_param"], np.float32).reshape(64, 1))
    cm["dtb_c"] = np.ascontiguousarray(np.asarray(inputs["L0_attn_dt_bias"], np.float32).reshape(64, 1))
    cm["i64"] = np.eye(64, dtype=np.float32)
    cm["i128"] = np.eye(128, dtype=np.float32)
    tri = np.zeros((128, 128), np.float32)
    for lp in range(128):
        tri[lp, lp + 1:] = BIGNEG
    cm["trineg"] = tri
    maps = []
    for b in range(8):
        m = dict(cm)
        m["x"] = np.ascontiguousarray(x[b])
        maps.append(m)
    return maps


def kernel(**inputs):
    nc = build_program(debug=False)
    in_maps = make_inmaps(inputs)
    res = run_bass_kernel_spmd(nc, in_maps, core_ids=list(range(8)))
    out = np.stack([np.asarray(r["y"], np.float32) for r in res.results], axis=0)
    return out




# version 03
# speedup vs baseline: 1.4213x; 1.3801x over previous
import numpy as np
import ml_dtypes
import concourse.bass as bass
import concourse.mybir as mybir
from concourse.bass_utils import run_bass_kernel_spmd

dt = mybir.dt
AF = mybir.ActivationFunctionType
ALU = mybir.AluOpType
AXX = mybir.AxisListType.X
ML_BF16 = ml_dtypes.bfloat16

DIM = 2048
DI = 4096
NH = 64
HD = 64
HID = 8192
NPROJ = 8480
T = 256
NT = 2048 // T
Q = 128
NCH = T // Q
BIGNEG = -1.0e30
RMS_EPS = 1.1920929e-07
LN_EPS = 1e-5
INV2PI = float(1.0 / (2.0 * np.pi))
TWOPI = float(2.0 * np.pi)
GCONST = float(np.sqrt(2.0 / np.pi))
GC2 = 0.044715

F32, F32R, BF, FP8, I32 = dt.float32, dt.float32r, dt.bfloat16, dt.float8e4, dt.int32


def build_program(debug=False):
    nc = bass.Bass("TRN2", target_bir_lowering=False)

    def din(name, shape, dtype=F32):
        return nc.dram_tensor(name, shape, dtype, kind="ExternalInput").ap()

    x_d = din("x", [2048, DIM])
    w_in_d = din("w_in", [8384, DIM])
    w_lam_d = din("w_lambda", [64, DIM])
    w_th_d = din("w_theta", [32, DIM])
    w_out_d = din("w_out", [DIM, DI])
    wg0_d = din("wg0", [DIM, HID])
    wg1_d = din("wg1", [DIM, HID])
    wl_d = din("wl", [HID, DIM])
    bbias_d = din("bbias_c", [64, 1])
    cbias_d = din("cbias_c", [64, 1])
    alog_d = din("alog_c", [64, 1])
    dpar_d = din("d_c", [64, 1])
    dtb_d = din("dtb_c", [64, 1])
    i64_d = din("i64", [64, 64])
    i128_d = din("i128", [128, 128])
    trineg_d = din("trineg", [128, 128])

    y_d = nc.dram_tensor("y", [2048, DIM], F32, kind="ExternalOutput").ap()

    scr_wallT = nc.dram_tensor("scr_wallT", [DIM, NPROJ], BF, kind="Internal").ap()
    scr_woutT = nc.dram_tensor("scr_woutT", [DI, DIM], BF, kind="Internal").ap()
    scr_wg0 = nc.dram_tensor("scr_wg0", [DIM, HID], FP8, kind="Internal").ap()
    scr_wg1 = nc.dram_tensor("scr_wg1", [DIM, HID], FP8, kind="Internal").ap()
    scr_wl = nc.dram_tensor("scr_wl", [HID, DIM], FP8, kind="Internal").ap()
    scr_line = nc.dram_tensor("scr_line", [1, NH * Q], BF, kind="Internal").ap()
    scr_eline = nc.dram_tensor("scr_eline", [1, DI], BF, kind="Internal").ap()
    scr_xlast = nc.dram_tensor("scr_xlast", [1, DI], BF, kind="Internal").ap()
    scr_row = nc.dram_tensor("scr_row", [1, 256], F32, kind="Internal").ap()

    dbg = {}
    def dout(name, shape, dtype):
        dbg[name] = nc.dram_tensor(name, shape, dtype, kind="ExternalOutput").ap()
        return dbg[name]
    if debug:
        dout("dbg_ln1fm", [128, 16 * T], BF)
        dout("dbg_zfm", [128, 32 * T], BF)
        dout("dbg_bc", [128, T], F32)
        dout("dbg_dt", [64, T], F32)
        dout("dbg_ang", [32, T], F32)
        dout("dbg_bro", [64, T], F32)
        dout("dbg_mask1", [128, 1024], BF)
        dout("dbg_X", [128, 4096], BF)
        dout("dbg_gt", [128, 128], BF)
        dout("dbg_cro", [64, T], F32)
        dout("dbg_gb", [64, 2 * T], F32)
        dout("dbg_ge", [128, 4096], BF)
        dout("dbg_ytok", [128, NCH * DI], BF)
        dout("dbg_x1", [128, 2 * DIM], F32)
        dout("dbg_a8", [128, 8 * T], FP8)

    from contextlib import ExitStack
    es = ExitStack()
    sbt = lambda name, shape, dtype: es.enter_context(nc.sbuf_tensor(name, shape, dtype))
    pst = lambda name, shape: es.enter_context(nc.psum_tensor(name, shape, F32))

    x_tok = sbt("x_tok", [128, 2 * DIM], F32)
    ln1_fm = sbt("ln1_fm", [128, 16 * T], BF)
    z_fm = sbt("z_fm", [128, 32 * T], BF)
    xh_tm = sbt("xh_tm", [128, NCH * DI], BF)
    y_tok = sbt("y_tok", [128, NCH * DI], BF)
    carry = sbt("carry", [64, NH * HD], BF)
    arena_bf = sbt("arena_bf", [128, 8192], BF)
    arena2_bf = sbt("arena2_bf", [128, 8192], BF)
    arena3_bf = sbt("arena3_bf", [128, 8192], BF)
    arena_f32 = sbt("arena_f32", [128, 2048], F32)
    arena_i32 = sbt("arena_i32", [128, 2048], I32)
    wbufA = sbt("wbufA", [128, 512], BF)
    wbufB = sbt("wbufB", [128, 512], BF)
    wbufC = sbt("wbufC", [128, 512], BF)
    wbufD = sbt("wbufD", [128, 512], BF)
    w8A = sbt("w8A", [128, 2048], FP8)
    w8B = sbt("w8B", [128, 2048], FP8)
    w8C = sbt("w8C", [128, 512], FP8)
    a8 = sbt("a8", [128, 64 * T], FP8)
    ln2_f8 = sbt("ln2_f8", [128, 16 * T], FP8)
    trig = sbt("trig", [64, 2 * T], BF)
    zeros_t = sbt("zeros_t", [64, 256], F32)
    bc_t = sbt("bc_t", [128, T], F32)
    dtlam = sbt("dtlam", [128, T], F32)
    s128a = sbt("s128a", [128, T], F32)
    dt_t = sbt("dt_t", [64, T], F32)
    dt_r = sbt("dt_r", [64, T], F32R)
    lam_t = sbt("lam_t", [64, T], F32)
    a_fm_t = sbt("a_fm_t", [64, T], F32)
    gam_r = sbt("gam_r", [64, T], F32R)
    bet_r = sbt("bet_r", [64, T], F32R)
    s64a = sbt("s64a", [64, T], F32)
    s64b = sbt("s64b", [64, T], F32)
    s64c = sbt("s64c", [64, T], F32)
    craw = s64c[0:64, :]
    w1 = s64a[0:32, :]
    w2 = s64b[0:32, :]
    bmn = sbt("bmn", [64, T], F32)
    cmn = sbt("cmn", [64, T], F32)
    bm_bf = sbt("bm_bf", [64, T], BF)
    cm_bf = sbt("cm_bf", [64, T], BF)
    b_tm = sbt("b_tm", [128, NCH * 64], BF)
    theta_t = sbt("theta_t", [32, T], F32)
    ang_t = sbt("ang_t", [32, T], F32)
    ang_carry = sbt("ang_carry", [32, 1], F32)
    ang_i = arena_i32[0:32, 0:T]
    acs_fm = sbt("acs_fm", [64, Q], F32)
    acs_r = sbt("acs_r", [64, Q], F32R)
    nacs_r = sbt("nacs_r", [64, Q], F32R)
    acsline = sbt("acsline", [1, 1024], F32R)
    eacs_bf = sbt("eacs_bf", [64, Q], BF)
    d2e_f = sbt("d2e_f", [64, Q], F32)
    d2e_r = sbt("d2e_r", [64, Q], F32R)
    gt_bf = sbt("gt_bf", [128, 128], BF)
    e_sb = sbt("e_sb", [64, 2048], BF)
    row_f = sbt("row_f", [1, T], F32)
    row_r = sbt("row_r", [1, T], F32R)
    dtavg_row = row_f
    dtavg_r = row_r
    srow2 = row_f
    srow2_r = row_r
    rstd_c = sbt("rstd_c", [128, 2], F32)
    mcol = sbt("mcol", [128, 4], F32)
    i64c = sbt("i64c_s", [64, 64], F32)
    i64r = sbt("i64r", [64, 64], F32R)
    i128r = sbt("i128r", [128, 128], F32R)
    trinr = sbt("trinr", [128, 128], F32R)
    onesf = sbt("onesf_s", [128, 1], F32)
    ones64 = sbt("ones64", [64, 64], F32)
    exp_t = sbt("exp_t", [64, 64], BF)
    e_col = sbt("e_col", [64, 1], F32)
    onesr = sbt("onesr", [128, 128], F32R)
    onesb = sbt("onesb", [128, 128], BF)
    smalls_dc = sbt("smalls_dc", [64, 1], F32)
    bbias = sbt("bbias", [64, 1], F32)
    cbias = sbt("cbias", [64, 1], F32)
    dtb = sbt("dtb", [64, 1], F32)
    alog_s = sbt("alog_s", [64, 1], F32)
    mexpA = sbt("mexpA", [64, 1], F32)
    epscol = sbt("epscol", [128, 1], F32)
    rmseps = sbt("rmseps", [128, 1], F32)
    halfpi = sbt("halfpi", [128, 1], F32)

    pmA = pst("pmA", [128, 2048])
    pmB = pst("pmB", [128, 2048])
    PM = [pmA, pmB]

    thunks = {e: [] for e in ("sync", "gpsimd", "vector", "scalar", "tensor")}
    sems = {}
    DMA_SEMS = ("sInit", "sDl", "sDrT", "sPa", "sPb", "sPstA", "sPstB", "sG8a", "sG8b",
                "sP8a", "sP8b", "sX", "sTr1", "sTrXh", "sTrB", "sTrY", "sTrA",
                "sWa", "sWb", "sWc", "sWd", "sTq", "sLam", "sCm", "sTrg", "sRp1", "sRp2", "sSh", "sEw",
                "sEv", "sE2d", "sRep", "sAcl", "sXl", "sRs", "sOut", "sDbg")
    for s in ("sT", "sV", "sS") + DMA_SEMS:
        sems[s] = [es.enter_context(nc.semaphore(s)), 0]

    def on(eng, fn):
        thunks[eng].append(fn)

    def wait(eng, sname, value):
        if value <= 0:
            return
        h = sems[sname][0]
        on(eng, lambda e, h=h, v=value: e.wait_ge(h, v))

    state = {"last_T": None, "last_SS": None}

    def _is_sbuf(ap):
        try:
            return str(ap.space) in ("MemorySpace.SBUF", "SBUF") or "SB" in str(ap.space)
        except Exception:
            return False

    def dma(eng, sname, out, in_, transpose=False, slow=False):
        h = sems[sname][0]
        if eng == "sync":
            ss = (not transpose) and _is_sbuf(out) and _is_sbuf(in_)
            if transpose and state["last_SS"]:
                sn = state["last_SS"][0]
                wait(eng, sn, sems[sn][1])
            if ss and state["last_T"]:
                sn = state["last_T"][0]
                wait(eng, sn, sems[sn][1])
        sems[sname][1] += 16
        v = sems[sname][1]
        on(eng, lambda e, o=out, i=in_, t=transpose, h=h, sl=slow: e.dma_start(
            o, i, transpose=t, allow_slow_non_contiguous=sl).then_inc(h, 16))
        tok = (sname, v)
        if eng == "sync":
            if transpose:
                state["last_T"] = tok
            elif _is_sbuf(out) and _is_sbuf(in_):
                state["last_SS"] = tok
        return tok

    def wtok(eng, tok):
        if tok is None or tok == 0:
            return
        sname, v = tok
        wait(eng, sname, v)

    ENG_SEM = {"tensor": "sT", "vector": "sV", "scalar": "sS", "gpsimd": "sG_nop"}

    DATAPATH = {"tensor_copy", "tensor_tensor", "tensor_add", "tensor_sub", "tensor_mul",
                "tensor_max", "tensor_scalar", "tensor_single_scalar", "tensor_scalar_add",
                "tensor_scalar_sub", "tensor_scalar_mul", "tensor_scalar_max",
                "tensor_scalar_min", "scalar_tensor_tensor", "tensor_reduce", "reduce_max",
                "reduce_sum", "reciprocal", "memset", "memzero", "tensor_tensor_scan",
                "select", "activation", "copy", "mul", "add", "square", "sqrt", "sign",
                "tensor_relu", "transpose", "matmul", "ldweights"}

    class DrainAfter:

        def __init__(self, e):
            self._e = e
            self._touched = set()
            self.last = None

        @staticmethod
        def _names(args, kw):
            out = set()
            for a in list(args) + list(kw.values()):
                nm = getattr(getattr(a, "tensor", None), "name", None)
                if nm is not None:
                    out.add(nm)
            return out

        def __getattr__(self, n):
            a = getattr(self._e, n)
            if n in DATAPATH:
                if n in ("matmul", "ldweights", "transpose"):
                    def wm(*args, **kw):
                        r = a(*args, **kw)
                        self.last = r
                        return r
                    return wm
                def w(*args, **kw):
                    names = self._names(args, kw)
                    if names & self._touched:
                        self._e.drain()
                        self._touched = set()
                    r = a(*args, **kw)
                    self._touched |= names
                    self.last = r
                    return r
                return w
            if n == "drain":
                def wd(*args, **kw):
                    self._touched = set()
                    r = a(*args, **kw)
                    self.last = r
                    return r
                return wd
            return a

    def mark(eng):
        sname = {"tensor": "sT", "vector": "sV", "scalar": "sS"}[eng]
        h = sems[sname][0]
        sems[sname][1] += 1
        v = sems[sname][1]

        def _m(e, h=h):
            last = getattr(e, "last", None)
            ok = False
            if last is not None:
                try:
                    last.then_inc(h, 1)
                    ok = True
                except Exception:
                    ok = False
                e.last = None
            if not ok:
                e.drain().then_inc(h, 1)
        on(eng, _m)
        return v

    def attach_inc(eng, buildfn):
        sname = {"tensor": "sT", "vector": "sV", "scalar": "sS"}[eng]
        h = sems[sname][0]
        sems[sname][1] += 1
        v = sems[sname][1]
        on(eng, lambda e, h=h, f=buildfn: f(e).then_inc(h, 1))
        return v

    V = lambda fn: on("vector", fn)
    S = lambda fn: on("scalar", fn)
    TT = lambda fn: on("tensor", fn)

    dma("sync", "sInit", i64c[:], i64_d)
    dma("sync", "sInit", bbias[:], bbias_d)
    dma("sync", "sInit", cbias[:], cbias_d)
    dma("sync", "sInit", dtb[:], dtb_d)
    dma("sync", "sInit", alog_s[:], alog_d)
    dma("sync", "sInit", arena_f32[:, 0:128], i128_d)
    dma("sync", "sInit", arena_f32[:, 128:256], trineg_d)
    c_loaded = dcol_v = dma("sync", "sInit", smalls_dc[:], dpar_d)

    wtok("vector", c_loaded)
    def _cinit(v):
        v.tensor_copy(i64r[:], i64c[:])
        v.tensor_copy(i128r[:], arena_f32[:, 0:128])
        v.tensor_copy(trinr[:], arena_f32[:, 128:256])
        v.memset(onesf[:], 1.0)
        v.memset(ones64[:], 1.0)
        v.memset(arena_f32[:, 256:384], 1.0)
        v.tensor_copy(onesr[:], arena_f32[:, 256:384])
        v.tensor_copy(onesb[:], arena_f32[:, 256:384])
        v.memset(epscol[:], LN_EPS)
        v.memset(rmseps[:], RMS_EPS)
        v.memset(halfpi[:], float(np.pi / 2))
        v.memset(zeros_t[:], 0.0)
        v.memset(carry[:], 0.0)
        v.memset(ang_carry[:], 0.0)
    V(_cinit)
    v_cinit = mark("vector")

    wtok("scalar", c_loaded)
    S(lambda s: s.activation(mexpA[:], alog_s[:], AF.Exp))
    S(lambda s: s.mul(mexpA[:], mexpA[:], -1.0))
    s_mexpA = mark("scalar")

    scr_dline = nc.dram_tensor("scr_dline", [1, DI], BF, kind="Internal").ap()
    V(lambda v: v.tensor_scalar(exp_t[:], ones64[:], smalls_dc[:, 0:1], None, ALU.mult))
    v_dline = mark("vector")
    wait("sync", "sV", v_dline)
    drep_v = dma("sync", "sDl", scr_dline[0:1, :].rearrange("o (h p) -> o h p", h=64), exp_t[:])

    inbufs = [arena_f32, x_tok[:, 2048:4096]]
    wq_out = arena3_bf[:, 0:4096]
    wqT_out = arena3_bf[:, 4096:8192]
    prev_q2 = [v_cinit, v_cinit]
    prev_tr = [None, None]
    prev_trst = [None, None]

    def fq_load(ch):
        idx, nrows, loads, dstf = ch
        p = idx % 2
        wait("sync", "sV", prev_q2[p])
        vload = None
        for sl, srcap in loads:
            vload = dma("sync", ("sPa", "sPb")[p], sl, srcap)
        ch.append(vload)

    def fq_q1(ch):
        idx, nrows, loads, dstf, vload = ch
        p = idx % 2
        buf = inbufs[p]
        sc = s128a[0:nrows, p * 32:p * 32 + 32]
        wtok("vector", vload)
        def _q1(v, buf=buf, nrows=nrows, sc=sc):
            for s_ in range(16):
                v.reduce_max(sc[:, s_:s_ + 1],
                             buf[0:nrows, s_ * 128:(s_ + 1) * 128],
                             axis=AXX, apply_absolute_value=True)
            v.tensor_scalar_max(sc[:, 0:16], sc[:, 0:16], 1e-10)
            v.reciprocal(sc[:, 16:32], sc[:, 0:16])
            v.tensor_scalar_mul(sc[:, 16:32], sc[:, 16:32], 7.0)
            v.tensor_scalar_mul(sc[:, 0:16], sc[:, 0:16], float(1.0 / 7.0))
            for s_ in range(16):
                v.tensor_scalar(buf[0:nrows, s_ * 128:(s_ + 1) * 128],
                                buf[0:nrows, s_ * 128:(s_ + 1) * 128],
                                sc[:, 16 + s_:17 + s_], 8.0, ALU.mult, ALU.add)
        V(_q1)
        ch.append(mark("vector"))

    def fq_castA(ch):
        idx, nrows, loads, dstf, vload, q1 = ch
        p = idx % 2
        buf = inbufs[p]
        wait("scalar", "sV", q1)
        S(lambda s, buf=buf, nrows=nrows, p=p: s.copy(
            arena_i32[0:nrows, p * 1024:p * 1024 + 1024], buf[0:nrows, 0:1024]))
        ch.append(mark("scalar"))

    def fq_backA(ch):
        idx, nrows, loads, dstf, vload, q1, qsA = ch
        p = idx % 2
        c1f = arena_bf[0:nrows, p * 2048:p * 2048 + 2048]
        wait("vector", "sS", qsA)
        V(lambda v, c1f=c1f, nrows=nrows, p=p: v.tensor_copy(
            c1f[:, 0:1024], arena_i32[0:nrows, p * 1024:p * 1024 + 1024]))
        ch.append(mark("vector"))

    def fq_castB(ch):
        idx, nrows, loads, dstf, vload, q1, qsA, q2a = ch
        p = idx % 2
        buf = inbufs[p]
        wait("scalar", "sV", q2a)
        S(lambda s, buf=buf, nrows=nrows, p=p: s.copy(
            arena_i32[0:nrows, p * 1024:p * 1024 + 1024], buf[0:nrows, 1024:2048]))
        ch.append(mark("scalar"))

    def fq_final(ch):
        idx, nrows, loads, dstf, vload, q1, qsA, q2a, qsB = ch
        p = idx % 2
        buf = inbufs[p]
        sc = s128a[0:nrows, p * 32:p * 32 + 32]
        c1f = arena_bf[0:nrows, p * 2048:p * 2048 + 2048]
        gp = arena_bf[0:nrows, 4096 + p * 2048:4096 + p * 2048 + 2048]
        wq = wq_out[:, p * 2048:p * 2048 + 2048]
        wait("vector", "sS", qsB)
        wtok("vector", prev_tr[p])
        def _q2c(v, buf=buf, wq=wq, nrows=nrows, sc=sc, c1f=c1f, gp=gp, p=p):
            v.tensor_copy(c1f[:, 1024:2048], arena_i32[0:nrows, p * 1024:p * 1024 + 1024])
            v.scalar_tensor_tensor(gp[:], buf[0:nrows, :], 0.5, c1f[:],
                                   ALU.subtract, ALU.is_ge)
            v.tensor_add(buf[0:nrows, :], c1f[:], gp[:])
            for s_ in range(16):
                v.tensor_scalar(wq[0:nrows, s_ * 128:(s_ + 1) * 128],
                                buf[0:nrows, s_ * 128:(s_ + 1) * 128],
                                -8.0, sc[:, s_:s_ + 1], ALU.add, ALU.mult)
        V(_q2c)
        q2 = mark("vector")
        prev_q2[p] = q2
        ch.append(q2)

    def fq_store(ch):
        idx, nrows, loads, dstf, vload, q1, qsA, q2a, qsB, q2 = ch
        p = idx % 2
        wq = wq_out[:, p * 2048:p * 2048 + 2048]
        wqT = wqT_out[:, p * 2048:p * 2048 + 2048]
        wait("sync", "sV", q2)
        wtok("sync", prev_trst[p])
        tr = None
        for k in range(16):
            tr = dma("sync", "sTq", wqT[:, k * 128:k * 128 + nrows],
                     wq[0:nrows, k * 128:(k + 1) * 128], transpose=True)
        prev_tr[p] = tr
        wtok("sync", tr)
        st = None
        for k in range(16):
            st = dma("sync", ("sPstA", "sPstB")[p], dstf(k),
                     wqT[0:128, k * 128:k * 128 + nrows])
        prev_trst[p] = st

    STAGES = [fq_load, fq_q1, fq_castA, fq_backA, fq_castB, fq_final, fq_store]

    chunks = []
    idx = 0
    for ci in range(67):
        r0 = ci * 128
        nrows = min(128, NPROJ - r0)
        buf = inbufs[idx % 2]
        loads = []
        for srcap, lo, hi in ((w_in_d, 0, 8384), (w_lam_d, 8384, 8448), (w_th_d, 8448, 8480)):
            s0, s1 = max(r0, lo), min(r0 + nrows, hi)
            if s0 < s1:
                loads.append((buf[s0 - r0:s1 - r0, :], srcap[s0 - lo:s1 - lo, :]))
        chunks.append([idx, nrows, loads,
                       (lambda k, r0=r0, nrows=nrows: scr_wallT[k * 128:(k + 1) * 128, r0:r0 + nrows])])
        idx += 1
    for ci in range(32):
        r0 = (ci // 2) * 128
        c0 = (ci % 2) * 2048
        buf = inbufs[idx % 2]
        chunks.append([idx, 128, [(buf[:], w_out_d[r0:r0 + 128, c0:c0 + 2048])],
                       (lambda k, r0=r0, c0=c0: scr_woutT[c0 + k * 128:c0 + (k + 1) * 128, r0:r0 + 128])])
        idx += 1

    NCHK = len(chunks)
    f8_pending = []

    def emit_pair(j):
        for st in STAGES[1:6]:
            for i in (2 * j, 2 * j + 1):
                if i < NCHK:
                    st(chunks[i])
        for i in (2 * (j + 1), 2 * (j + 1) + 1):
            if i < NCHK:
                fq_load(chunks[i])
        for i in (2 * j, 2 * j + 1):
            if i < NCHK:
                fq_store(chunks[i])
        for _ in range(4):
            if f8_pending:
                f8_pending.pop(0)()

    for i in (0, 1):
        fq_load(chunks[i])
    prev8 = [None, None]
    j8 = 0
    def f8hop(src_ap, dst_ap):
        nonlocal j8
        b = (w8A, w8B)[j8 % 2]
        wtok("gpsimd", prev8[j8 % 2])
        v8 = dma("gpsimd", ("sG8a", "sG8b")[j8 % 2], b[:], src_ap)
        wtok("sync", v8)
        prev8[j8 % 2] = dma("sync", ("sP8a", "sP8b")[j8 % 2], dst_ap, b[:])
        j8 += 1
    for srcd, dstd in ((wg0_d, scr_wg0), (wg1_d, scr_wg1)):
        for ci in range(16):
            r0 = ci * 128
            for half in range(4):
                c0 = half * 2048
                f8_pending.append(lambda s=srcd[r0:r0 + 128, c0:c0 + 2048],
                                  d=dstd[r0:r0 + 128, c0:c0 + 2048]: f8hop(s, d))
    for ci in range(64):
        r0 = ci * 128
        f8_pending.append(lambda s=wl_d[r0:r0 + 128, :],
                          d=scr_wl[r0:r0 + 128, :]: f8hop(s, d))

    for j in range((NCHK + 1) // 2):
        emit_pair(j)
    while f8_pending:
        f8_pending.pop(0)()

    prep_V = sems["sV"][1]
    prep_toks = [(s, sems[s][1]) for s in ("sPstA", "sPstB", "sP8a", "sP8b", "sDl")
                 if sems[s][1] > 0]

    m_prev = {"tile_done_V": prep_V, "tile_done_S": 0, "tile_done_T": 0,
              "out_st": None, "xl": None, "drt": None}

    for t in range(NT):
        r0 = t * T
        if t == 0:
            for tk in prep_toks:
                wtok("sync", tk)
        wait("sync", "sV", m_prev["tile_done_V"])
        wait("sync", "sS", m_prev["tile_done_S"])
        wait("sync", "sT", m_prev["tile_done_T"])
        wtok("sync", m_prev.get("dbgx1"))
        x_in = None
        for c in range(2):
            x_in = dma("sync", "sX", x_tok[:, c * DIM:(c + 1) * DIM], x_d[r0 + c * 128:r0 + (c + 1) * 128, :])

        ln_tm = arena3_bf[:, 4096:8192]
        wtok("vector", x_in)
        wait("vector", "sS", s_mexpA)
        def _ln(v, src=x_tok):
            for c in range(2):
                xc = src[:, c * DIM:(c + 1) * DIM]
                v.reduce_sum(mcol[:, c:c + 1], xc, axis=AXX)
            v.tensor_scalar_mul(mcol[:, 0:2], mcol[:, 0:2], float(1.0 / DIM))
        V(_ln)
        vln = mark("vector")
        wait("scalar", "sV", vln)
        def _lns(s, src=x_tok):
            for c in range(2):
                s.activation(arena2_bf[:, c * DIM:(c + 1) * DIM], src[:, c * DIM:(c + 1) * DIM],
                             AF.Square)
        S(_lns)
        sln = mark("scalar")
        wait("vector", "sS", sln)
        def _ln2(v, src=x_tok, dstv=ln_tm):
            for c in range(2):
                v.reduce_sum(mcol[:, 2 + c:3 + c], arena2_bf[:, c * DIM:(c + 1) * DIM], axis=AXX)
            v.tensor_scalar_mul(mcol[:, 2:4], mcol[:, 2:4], float(1.0 / DIM))
            v.tensor_mul(rstd_c[:, 0:2], mcol[:, 0:2], mcol[:, 0:2])
            v.tensor_sub(mcol[:, 2:4], mcol[:, 2:4], rstd_c[:, 0:2])
        V(_ln2)
        vln2 = mark("vector")
        wait("scalar", "sV", vln2)
        def _ln3(s):
            s.activation(mcol[:, 2:4], mcol[:, 2:4], AF.Ln, bias=epscol[:, 0:1])
            s.activation(mcol[:, 2:4], mcol[:, 2:4], AF.Exp, scale=-0.5)
        S(_ln3)
        sln3 = mark("scalar")
        wait("vector", "sS", sln3)
        def _ln4(v, src=x_tok, dstv=ln_tm):
            for c in range(2):
                v.tensor_scalar(dstv[:, c * DIM:(c + 1) * DIM], src[:, c * DIM:(c + 1) * DIM],
                                mcol[:, c:c + 1], mcol[:, 2 + c:3 + c], ALU.subtract, ALU.mult)
        V(_ln4)
        vln4 = mark("vector")
        wait("sync", "sV", vln4)
        tr_v = None
        for c in range(2):
            for k in range(16):
                tr_v = dma("sync", "sTr1", ln1_fm[:, k * T + c * 128: k * T + c * 128 + 128],
                           ln_tm[:, c * DIM + k * 128: c * DIM + (k + 1) * 128], transpose=True)
        ln1fm_done = tr_v

        if debug and t == 0:
            wtok("sync", ln1fm_done)
            dma("sync", "sDbg", dbg["dbg_ln1fm"], ln1_fm[:])
            wait("sync", "sDbg", sems["sDbg"][1])

        xh_fm = arena_bf[:, 0:8192]
        wload = [None, None]
        gmark = {}
        evac_mark = {}
        step = 0
        for g in range(17):
            rg0 = g * 512
            mg = min(512, NPROJ - rg0)
            pm = PM[g % 2]
            for k in range(16):
                b = step % 4
                wb = (wbufA, wbufB, wbufC, wbufD)[b]
                wait("sync", "sT", gmark.get(("buf", step - 4), 0))
                wload[b % 2] = dma("sync", ("sWa", "sWb", "sWc", "sWd")[b], wb[:, 0:mg],
                               scr_wallT[k * 128:(k + 1) * 128, rg0:rg0 + mg])
                wtok("tensor", wload[b % 2])
                if k == 0:
                    wait("tensor", "sS", evac_mark.get(g - 2, 0))
                    wtok("tensor", ln1fm_done)
                def _mm(tt, pm=pm, wb=wb, k=k, mg=mg):
                    nmc = (mg + 127) // 128
                    for mc in range(nmc):
                        M = min(128, mg - mc * 128)
                        tt.matmul(pm[0:M, mc * 512:mc * 512 + T],
                                  wb[:, mc * 128:mc * 128 + M],
                                  ln1_fm[:, k * T:(k + 1) * T],
                                  start=(k == 0), stop=(k == 15))
                TT(_mm)
                gmark[("buf", step)] = mark("tensor")
                step += 1
            gm = gmark[("buf", step - 1)]
            wait("scalar", "sT", gm)
            pm_str = pm[:, 0:2048].rearrange("p (a b) -> p a b", a=4)[:, :, 0:T]
            if g < 8:
                S(lambda s, pm_str=pm_str, g=g: s.activation(
                    z_fm[:, g * 1024:(g + 1) * 1024].rearrange("p (a b) -> p a b", a=4),
                    pm_str, AF.Silu))
            elif g < 16:
                S(lambda s, pm_str=pm_str, g=g: s.activation(
                    xh_fm[:, (g - 8) * 1024:(g - 7) * 1024].rearrange("p (a b) -> p a b", a=4),
                    pm_str, AF.Silu))
            else:
                def _e8(s, pm=pm):
                    s.copy(bc_t[:], pm[:, 0:T])
                    s.copy(dtlam[:], pm[:, 512:512 + T])
                    s.copy(theta_t[:], pm[0:32, 1024:1024 + T])
                S(_e8)
            evac_mark[g] = mark("scalar")
        evac_mark[7] = evac_mark[15]
        inproj_S = evac_mark[16]

        if debug and t == 0:
            wait("sync", "sS", evac_mark[7])
            dma("sync", "sDbg", dbg["dbg_zfm"], z_fm[:])
            wait("sync", "sDbg", sems["sDbg"][1])
            wait("sync", "sS", inproj_S)
            dma("sync", "sDbg", dbg["dbg_bc"], bc_t[:])
            wait("sync", "sDbg", sems["sDbg"][1])

        wtok("sync", m_prev["xl"])
        wait("sync", "sS", evac_mark[7])
        xh_tr = None
        for kk in range(32):
            for c in range(2):
                xh_tr = dma("sync", "sTrXh", xh_tm[:, c * DI + kk * 128: c * DI + (kk + 1) * 128],
                            xh_fm[:, kk * T + c * 128: kk * T + c * 128 + 128], transpose=True)
        xh_tm_done = xh_tr

        wait("scalar", "sS", inproj_S)
        def _sm1(s):
            s.activation(s64a[:], dtlam[0:64, :], AF.Abs, bias=dtb[:, 0:1])
            s.activation(s64a[:], s64a[:], AF.Exp, scale=-1.0)
            s.activation(s64a[:], s64a[:], AF.Ln, bias=onesf[0:64, 0:1])
            s.activation(s64b[:], dtlam[0:64, :], AF.Relu, bias=dtb[:, 0:1])
            s.activation(s128a[64:128, :], dtlam[64:128, :], AF.Exp, scale=-1.0)
        S(_sm1)
        sm1 = mark("scalar")
        wait("vector", "sS", sm1)
        def _sm2(v):
            v.tensor_add(dt_t[:], s64a[:], s64b[:])
            v.tensor_copy(dt_r[:], dt_t[:])
            v.tensor_scalar(a_fm_t[:], dt_t[:], mexpA[:, 0:1], None, ALU.mult)
            v.tensor_scalar(s128a[64:128, :], s128a[64:128, :], 1.0, None, ALU.add)
            v.reciprocal(s128a[64:128, :], s128a[64:128, :])
        V(_sm2)
        sm2 = mark("vector")
        wait("sync", "sV", sm2)
        lam_mv = dma("sync", "sLam", lam_t[:], s128a[64:128, :])
        wait("tensor", "sV", sm2)
        wait("tensor", "sS", evac_mark.get(7, 0))
        wait("tensor", "sS", inproj_S)
        TT(lambda tt: tt.matmul(pmA[0:1, 0:T], onesr[0:64, 0:1], dt_r[:], start=True, stop=True))
        t_dtavg = mark("tensor")
        wait("scalar", "sT", t_dtavg)
        S(lambda s: s.activation(dtavg_row[:], pmA[0:1, 0:T], AF.Copy, scale=float(1.0 / NH)))
        s_dtavg = mark("scalar")
        wait("vector", "sS", s_dtavg)
        V(lambda v: v.tensor_copy(dtavg_r[:], dtavg_row[:]))
        v_dtavg = mark("vector")
        wait("tensor", "sV", v_dtavg)
        TT(lambda tt: tt.matmul(pmA[0:32, 0:T], onesr[0:1, 0:32], dtavg_r[:], start=True, stop=True))
        t_dtrep = mark("tensor")
        w3 = s64c[0:32, :]
        wait("vector", "sT", t_dtrep)
        def _ang(v):
            v.tensor_mul(w1[:], theta_t[:], pmA[0:32, 0:T])
            v.tensor_tensor_scan(ang_t[:], w1[:], zeros_t[0:32, 0:T], 0.0, ALU.add, ALU.add)
            v.tensor_scalar(ang_t[:], ang_t[:], ang_carry[:, 0:1], None, ALU.add)
            v.tensor_copy(ang_carry[:], ang_t[:, T - 1:T])
            v.tensor_scalar_mul(w1[:], ang_t[:], INV2PI)
        V(_ang)
        v_ang = mark("vector")
        wait("scalar", "sV", v_ang)
        S(lambda s: s.copy(ang_i[:], w1[:]))
        s_angi = mark("scalar")
        wait("vector", "sS", s_angi)
        def _ang2(v):
            v.tensor_copy(w2[:], ang_i[:])
            v.scalar_tensor_tensor(w3[:], w1[:], 0.5, w2[:], ALU.subtract, ALU.is_ge)
            v.scalar_tensor_tensor(theta_t[:], w1[:], 0.5, w2[:], ALU.add, ALU.is_le)
            v.tensor_add(w2[:], w2[:], w3[:])
            v.tensor_sub(w2[:], w2[:], theta_t[:])
            v.scalar_tensor_tensor(w1[:], w2[:], -TWOPI, ang_t[:], ALU.mult, ALU.add)
            v.tensor_scalar(w2[:], ang_t[:], INV2PI, 0.25, ALU.mult, ALU.add)
        V(_ang2)
        v_ang2 = mark("vector")
        wait("scalar", "sV", v_ang2)
        S(lambda s: s.copy(ang_i[:], w2[:]))
        s_angi2 = mark("scalar")
        wait("vector", "sS", s_angi2)
        def _ang3(v):
            v.tensor_copy(w3[:], ang_i[:])
            v.scalar_tensor_tensor(theta_t[:], w2[:], 0.5, w3[:], ALU.subtract, ALU.is_ge)
            v.scalar_tensor_tensor(w2[:], w2[:], 0.5, w3[:], ALU.add, ALU.is_le)
            v.tensor_add(w3[:], w3[:], theta_t[:])
            v.tensor_sub(w3[:], w3[:], w2[:])
            v.scalar_tensor_tensor(w2[:], w3[:], -TWOPI, ang_t[:], ALU.mult, ALU.add)
            v.tensor_scalar(w2[:], w2[:], float(np.pi / 2), None, ALU.add)
        V(_ang3)
        v_ang3 = mark("vector")
        wait("scalar", "sV", v_ang3)
        def _trig(s):
            s.activation(trig[0:32, 0:T], w1[:], AF.Sin)
            s.activation(trig[0:32, T:2 * T], w2[:], AF.Sin)
        S(_trig)
        s_trig = mark("scalar")
        wait("sync", "sS", s_trig)
        trig_mv = dma("sync", "sTrg", trig[32:64, :], trig[0:32, :])

        if debug and t == 0:
            wait("sync", "sV", v_ang)
            dma("sync", "sDbg", dbg["dbg_ang"], ang_t[:])
            wait("sync", "sDbg", sems["sDbg"][1])
            wait("sync", "sS", inproj_S)
            dma("sync", "sDbg", dbg["dbg_dt"], dt_t[:])
            wait("sync", "sDbg", sems["sDbg"][1])

        wait("sync", "sV", v_ang3)
        cm_mv = dma("sync", "sCm", craw[:], bc_t[64:128, :])
        wait("vector", "sS", inproj_S)
        wait("vector", "sS", s_trig)
        wait("vector", "sT", t_dtrep)
        def _bc1(v):
            v.tensor_mul(s64a[:], bc_t[0:64, :], bc_t[0:64, :])
            v.tensor_copy(dt_r[:], s64a[:])
        V(_bc1)
        v_bc1 = mark("vector")
        wait("tensor", "sV", v_bc1)
        TT(lambda tt: tt.matmul(pmB[0:1, 0:T], onesr[0:64, 0:1], dt_r[:], start=True, stop=True))
        t_bssq = mark("tensor")
        wtok("vector", cm_mv)
        wait("vector", "sT", t_bssq)
        def _bc2(v):
            v.tensor_mul(s64b[:], craw[:], craw[:])
            v.tensor_copy(dt_r[:], s64b[:])
        V(_bc2)
        v_bc2 = mark("vector")
        wait("tensor", "sV", v_bc2)
        TT(lambda tt: tt.matmul(pmB[0:1, T:2 * T], onesr[0:64, 0:1], dt_r[:], start=True, stop=True))
        t_cssq = mark("tensor")
        wait("scalar", "sT", t_cssq)
        def _bc3(s):
            s.activation(srow2[:], pmB[0:1, 0:T], AF.Ln, scale=float(1.0 / 64), bias=rmseps[0:1, 0:1])
            s.activation(srow2[:], srow2[:], AF.Exp, scale=-0.5)
        S(_bc3)
        s_brstd = mark("scalar")
        wait("vector", "sS", s_brstd)
        V(lambda v: v.tensor_copy(srow2_r[:], srow2[:]))
        v_brstd = mark("vector")
        wait("tensor", "sV", v_brstd)
        TT(lambda tt: tt.matmul(pmA[0:64, 0:T], onesr[0:1, 0:64], srow2_r[:], start=True, stop=True))
        t_brep = mark("tensor")
        wait("scalar", "sT", t_brep)
        def _bc4(s):
            s.activation(srow2[:], pmB[0:1, T:2 * T], AF.Ln, scale=float(1.0 / 64), bias=rmseps[0:1, 0:1])
            s.activation(srow2[:], srow2[:], AF.Exp, scale=-0.5)
        S(_bc4)
        s_crstd = mark("scalar")
        wait("vector", "sS", s_crstd)
        wait("vector", "sT", t_brep)
        def _bc5(v):
            v.tensor_mul(bmn[:], bc_t[0:64, :], pmA[0:64, 0:T])
            v.tensor_scalar(bmn[:], bmn[:], bbias[:, 0:1], None, ALU.add)
            v.tensor_copy(srow2_r[:], srow2[:])
        V(_bc5)
        v_bc5 = mark("vector")
        wait("tensor", "sV", v_bc5)
        TT(lambda tt: tt.matmul(pmA[0:64, T:2 * T], onesr[0:1, 0:64], srow2_r[:], start=True, stop=True))
        t_crep = mark("tensor")
        wait("vector", "sT", t_crep)
        def _bc6(v):
            v.tensor_mul(cmn[:], craw[:], pmA[0:64, T:2 * T])
            v.tensor_scalar(cmn[:], cmn[:], cbias[:, 0:1], None, ALU.add)
        V(_bc6)
        v_bc6 = mark("vector")

        def rope(src, v_src_mark, prev_mv2=None):
            wait("sync", "sV", v_src_mark)
            mv1 = dma("sync", "sRp1", w1[:], src[32:64, :])
            wtok("vector", mv1)
            wtok("vector", prev_mv2)
            wtok("vector", trig_mv)
            def _r(v, src=src):
                sin0 = trig[0:32, 0:T]
                cos0 = trig[0:32, T:2 * T]
                v.tensor_mul(w2[:], w1[:], sin0)
                v.tensor_mul(theta_t[:], src[0:32, :], cos0)
                v.tensor_sub(theta_t[:], theta_t[:], w2[:])
                v.tensor_mul(w2[:], src[0:32, :], sin0)
                v.tensor_mul(w1[:], w1[:], cos0)
                v.tensor_add(w2[:], w2[:], w1[:])
                v.tensor_copy(src[0:32, :], theta_t[:])
            V(_r)
            vr = mark("vector")
            wait("sync", "sV", vr)
            mv2 = dma("sync", "sRp2", src[32:64, :], w2[:])
            return mv2
        b_roped = rope(bmn, v_bc5)
        c_roped = rope(cmn, v_bc6, prev_mv2=b_roped)
        if debug and t == 0:
            wtok("sync", c_roped)
            dma("sync", "sDbg", dbg["dbg_cro"], cmn[:])
            wait("sync", "sDbg", sems["sDbg"][1])
        wtok("vector", c_roped)
        def _bc7(v):
            v.tensor_copy(bm_bf[:], bmn[:])
            v.tensor_copy(cm_bf[:], cmn[:])
        V(_bc7)
        v_bcbf = mark("vector")
        wait("sync", "sV", v_bcbf)
        btm_v = None
        for c in range(NCH):
            btm_v = dma("sync", "sTrB", b_tm[:, c * 64:(c + 1) * 64],
                        bm_bf[0:64, c * 128:(c + 1) * 128], transpose=True)

        if debug and t == 0:
            wait("sync", "sV", v_bcbf)
            dma("sync", "sDbg", dbg["dbg_bro"], bmn[:])
            wait("sync", "sDbg", sems["sDbg"][1])

        wait("scalar", "sV", sm2)
        wait("scalar", "sV", v_bc6)
        S(lambda s: s.activation(s64c[:], a_fm_t[:], AF.Exp))
        s_expA = mark("scalar")
        wait("vector", "sS", s_expA)
        wtok("vector", lam_mv)
        wtok("vector", c_roped)
        def _gb(v):
            v.tensor_mul(s64a[:], lam_t[:], dt_t[:])
            v.tensor_copy(gam_r[:], s64a[:])
            v.tensor_scalar(s64b[:], lam_t[:], -1.0, 1.0, ALU.mult, ALU.add)
            v.tensor_mul(s64b[:], s64b[:], dt_t[:])
            v.tensor_mul(s64b[:], s64b[:], s64c[:])
            v.tensor_copy(bet_r[:], s64b[:])
        V(_gb)
        v_gb = mark("vector")
        if debug and t == 0:
            wait("sync", "sV", v_gb)
            dma("sync", "sDbg", dbg["dbg_gb"][0:64, 0:T], s64a[:])
            wait("sync", "sDbg", sems["sDbg"][1])
            dma("sync", "sDbg", dbg["dbg_gb"][0:64, T:2 * T], s64b[:])
            wait("sync", "sDbg", sems["sDbg"][1])

        prev_chunk_V = 0
        for c in range(NCH):
            ccol = c * Q
            xh_c = xh_tm[:, c * DI:(c + 1) * DI]
            xhs = arena_bf[:, 0:4096]
            Xc = arena3_bf[:, 0:4096]
            Xd = arena3_bf[:, 4096:8192]
            ge = arena2_bf[:, 0:4096]
            be = arena2_bf[:, 4096:8192]

            wtok("sync", xh_tm_done)
            wait("sync", "sS", evac_mark[7])
            wait("sync", "sV", prev_chunk_V)
            if c == 0:
                sh1 = dma("sync", "sSh", xhs[1:128, :], xh_c[0:127, :])
                if t == 0:
                    sh2 = sh1
                    wtok("vector", sh1)
                    V(lambda v: v.memset(xhs[0:1, :], 0.0))
                    mark("vector")
                else:
                    wtok("sync", m_prev["xl"])
                    sh2 = dma("sync", "sSh", xhs[0:1, :], scr_xlast[0:1, :])
            else:
                sh1 = dma("sync", "sSh", xhs[1:128, :], xh_c[0:127, :])
                sh2 = dma("sync", "sSh", xhs[0:1, :], xh_tm[127:128, (c - 1) * DI:c * DI])
            wait("tensor", "sV", v_gb)
            wait("tensor", "sV", prev_chunk_V)
            def _gbe(tt, ccol=ccol):
                for half in range(2):
                    pm = PM[half]
                    for blk in range(4):
                        h0 = half * 32 + blk * 8
                        rhs = i64r[:, h0:h0 + 8].unsqueeze(2).broadcast_to([64, 8, 64])
                        tt.matmul(pm[:, blk * 512:(blk + 1) * 512].rearrange("p (h q) -> p h q", h=8),
                                  gam_r[:, ccol:ccol + Q], rhs, start=True, stop=True)
            TT(_gbe)
            t_ge = mark("tensor")
            wait("scalar", "sT", t_ge)
            def _gee(s):
                s.copy(ge[:, 0:2048], pmA[:, 0:2048])
                s.copy(ge[:, 2048:4096], pmB[:, 0:2048])
            S(_gee)
            s_ge = mark("scalar")
            if debug and t == 0 and c == 0:
                wait("sync", "sS", s_ge)
                dma("sync", "sDbg", dbg["dbg_ge"], ge[:])
                wait("sync", "sDbg", sems["sDbg"][1])
            wait("tensor", "sS", s_ge)
            def _bbe(tt, ccol=ccol):
                for half in range(2):
                    pm = PM[half]
                    for blk in range(4):
                        h0 = half * 32 + blk * 8
                        rhs = i64r[:, h0:h0 + 8].unsqueeze(2).broadcast_to([64, 8, 64])
                        tt.matmul(pm[:, blk * 512:(blk + 1) * 512].rearrange("p (h q) -> p h q", h=8),
                                  bet_r[:, ccol:ccol + Q], rhs, start=True, stop=True)
            TT(_bbe)
            t_be = mark("tensor")
            wait("scalar", "sT", t_be)
            def _bee(s):
                s.copy(be[:, 0:2048], pmA[:, 0:2048])
                s.copy(be[:, 2048:4096], pmB[:, 0:2048])
            S(_bee)
            s_be = mark("scalar")
            wait("vector", "sS", s_be)
            if debug:
                wait("vector", "sDbg", sems["sDbg"][1])
            wtok("vector", sh2)
            def _X(v, xh_c=xh_c):
                v.tensor_mul(Xc[:], xh_c[:], ge[:])
                v.tensor_mul(Xd[:], xhs[:], be[:])
                v.tensor_add(Xc[:], Xc[:], Xd[:])
            V(_X)
            v_X = mark("vector")

            wait("vector", "sV", v_gb)
            def _acs(v, ccol=ccol):
                v.tensor_tensor_scan(acs_fm[:], a_fm_t[:, ccol:ccol + Q],
                                     zeros_t[0:64, 0:Q], 0.0, ALU.add, ALU.add)
                v.tensor_copy(acs_r[:], acs_fm[:])
                v.tensor_scalar(nacs_r[:], acs_fm[:], -1.0, None, ALU.mult)
            V(_acs)
            v_acs = mark("vector")
            wait("scalar", "sV", v_acs)
            def _acs2(s):
                s.activation(eacs_bf[:], acs_fm[:], AF.Exp)
            S(_acs2)
            s_acs2 = mark("scalar")
            wait("vector", "sS", s_acs2)
            def _vd2e(v):
                v.tensor_scalar(d2e_f[:], acs_fm[:], acs_fm[:, Q - 1:Q], -1.0,
                                ALU.subtract, ALU.mult)
                v.tensor_copy(d2e_r[:], d2e_f[:])
            V(_vd2e)
            v_d2e = mark("vector")
            wait("scalar", "sV", v_acs)
            S(lambda s: s.activation(e_col[:], acs_fm[:, Q - 1:Q], AF.Exp))
            s_erow = mark("scalar")
            wait("vector", "sS", s_erow)
            V(lambda v: v.tensor_scalar(exp_t[:], ones64[:], e_col[:, 0:1], None, ALU.mult))
            v_erow = mark("vector")
            wait("sync", "sV", v_erow)
            el_w = dma("sync", "sEw", scr_eline[0:1, :].rearrange("o (h p) -> o h p", h=64), exp_t[:])

            wait("sync", "sS", s_acs2)
            e2d_v = dma("sync", "sE2d", scr_line[:].rearrange("o (h q) -> o h q", h=64), eacs_bf[:])
            if debug and t == 0 and c == 0:
                wait("sync", "sV", v_X)
                m_prev["dbgX"] = dma("sync", "sDbg", dbg["dbg_X"], Xc[:])
                wait("sync", "sDbg", sems["sDbg"][1])
            wait("tensor", "sV", v_bcbf)
            wait("tensor", "sS", s_be)
            TT(lambda tt, ccol=ccol: tt.matmul(pmA[0:128, 0:128], bm_bf[:, ccol:ccol + Q],
                                               cm_bf[:, ccol:ccol + Q], start=True, stop=True))
            t_G = mark("tensor")
            wait("vector", "sT", t_G)
            V(lambda v: v.tensor_copy(gt_bf[:], pmA[0:128, 0:128]))
            v_G = mark("vector")

            mp = arena_bf[:, 4096:5120]
            m1a = arena_bf[:, 5120:6144]
            m1b = arena_bf[:, 6144:7168]
            grep = arena_bf[:, 0:1024]
            m2 = arena_bf[:, 1024:2048]
            cmrep = arena_bf[:, 2048:3072]
            earep = arena_bf[:, 3072:4096]
            ydone = []
            prev_evac = 0
            prev_m1 = {}
            prev_diff = {}
            prev_mk = {}
            for e8 in range(8):
                h0 = e8 * 8
                qpm = PM[e8 % 2]
                aslot = acsline[0:1, 0:1024]
                wait("sync", "sT", prev_diff.get(e8 - 1, 0))
                wait("sync", "sV", v_acs)
                acl_v = dma("sync", "sAcl", aslot.rearrange("o (h q) -> o h q", h=8),
                            acs_r[h0:h0 + 8, :])
                wtok("tensor", acl_v)
                wait("tensor", "sV", v_acs)
                wait("tensor", "sS", prev_m1.get(e8 - 2, 0))
                if e8 < 2:
                    wait("tensor", "sS", s_be)
                    wait("tensor", "sV", v_G)
                def _dmm(tt, h0=h0, qpm=qpm, aslot=aslot):
                    for half in range(2):
                        sl = qpm[:, half * 512:(half + 1) * 512]
                        h1 = h0 + half * 4
                        tt.matmul(sl, onesr[0:1, 0:128],
                                  aslot[0:1, half * 512:(half + 1) * 512],
                                  start=True, stop=False)
                        rhs = i64r[:, h1:h1 + 4].unsqueeze(2).broadcast_to([64, 4, 128])
                        tt.matmul(sl.rearrange("p (h q) -> p h q", h=4),
                                  nacs_r[:], rhs, start=False, stop=False)
                        rhs2 = i128r[:, :].unsqueeze(1).broadcast_to([128, 4, 128])
                        tt.matmul(sl.rearrange("p (h q) -> p h q", h=4),
                                  trinr[:], rhs2, start=False, stop=True)
                TT(_dmm)
                t_diff = mark("tensor")
                prev_diff[e8] = t_diff
                wait("sync", "sV", v_X if e8 == 0 else 0)
                wait("sync", "sV", prev_mk.get(e8 - 1, 0))
                wait("sync", "sV", v_G)
                g_v = dma("sync", "sRep", grep[:].rearrange("p (h q) -> p h q", h=8),
                          gt_bf[:].unsqueeze(1).broadcast_to([128, 8, 128]))
                c_v = dma("sync", "sRep", cmrep[0:64, :].rearrange("p (h q) -> p h q", h=8),
                          cm_bf[0:64, ccol:ccol + Q].unsqueeze(1).broadcast_to([64, 8, 128]))
                wtok("sync", e2d_v)
                e_v = dma("sync", "sRep", earep[0:64, :],
                          scr_line[0:1, h0 * 128:(h0 + 8) * 128].broadcast_to([64, 1024]))
                wait("scalar", "sT", t_diff)
                S(lambda s, qpm=qpm: s.activation(mp[:], qpm[:, 0:1024], AF.Exp))
                prev_m1[e8] = mark("scalar")
                m1 = (m1a, m1b)[e8 % 2]
                wait("vector", "sS", prev_m1[e8])
                if debug and e8 == 2:
                    wait("vector", "sDbg", sems["sDbg"][1])
                wtok("vector", e_v)
                def _mk(v, m1=m1):
                    v.tensor_mul(m1[:], mp[:], grep[:])
                    v.tensor_mul(m2[0:64, :], earep[0:64, :], cmrep[0:64, :])
                V(_mk)
                v_m1 = mark("vector")
                prev_mk[e8] = v_m1
                if debug and t == 0 and c == 0 and e8 == 0:
                    wait("sync", "sV", v_m1)
                    dma("sync", "sDbg", dbg["dbg_mask1"], m1[:])
                    wait("sync", "sDbg", sems["sDbg"][1])
                    wait("sync", "sV", v_G)
                    m_prev["dbgm1"] = dma("sync", "sDbg", dbg["dbg_gt"], gt_bf[:])
                    wait("sync", "sDbg", sems["sDbg"][1])
                ypm = PM[(e8 + 1) % 2]
                wait("tensor", "sV", v_m1)
                wait("tensor", "sV", v_X)
                if e8 >= 2:
                    wait("tensor", "sV", ydone[e8 - 2])
                def _ymm(tt, m1=m1, ypm=ypm, h0=h0, c=c):
                    for j in range(8):
                        h = h0 + j
                        tt.matmul(ypm[:, 1024 + j * 64: 1024 + (j + 1) * 64],
                                  m1[:, j * 128:(j + 1) * 128],
                                  Xc[:, h * 64:(h + 1) * 64], start=True, stop=False)
                        tt.matmul(ypm[:, 1024 + j * 64: 1024 + (j + 1) * 64],
                                  m2[0:64, j * 128:(j + 1) * 128],
                                  carry[0:64, h * 64:(h + 1) * 64], start=False, stop=True)
                TT(_ymm)
                t_y = mark("tensor")
                wait("vector", "sT", t_y)
                def _yev(v, ypm=ypm, h0=h0, c=c):
                    v.tensor_copy(y_tok[:, c * DI + h0 * 64: c * DI + (h0 + 8) * 64],
                                  ypm[:, 1024:1536])
                V(_yev)
                ydone.append(mark("vector"))
                prev_evac = ydone[-1]
            y_chunk_done = ydone[-1]

            d2ex = arena2_bf[:, 0:4096]
            wait("tensor", "sV", y_chunk_done)
            def _d2mm(tt):
                for half in range(2):
                    pm = PM[half]
                    for blk in range(4):
                        h0 = half * 32 + blk * 8
                        rhs = i64r[:, h0:h0 + 8].unsqueeze(2).broadcast_to([64, 8, 64])
                        tt.matmul(pm[:, blk * 512:(blk + 1) * 512].rearrange("p (h q) -> p h q", h=8),
                                  d2e_r[:], rhs, start=True, stop=True)
            TT(_d2mm)
            t_d2 = mark("tensor")
            wait("scalar", "sT", t_d2)
            def _d2ev(s):
                s.activation(d2ex[:, 0:2048], pmA[:, 0:2048], AF.Exp)
                s.activation(d2ex[:, 2048:4096], pmB[:, 0:2048], AF.Exp)
            S(_d2ev)
            s_d2 = mark("scalar")
            wait("vector", "sS", s_d2)
            V(lambda v: v.tensor_mul(Xd[:], Xc[:], d2ex[:]))
            v_Xd = mark("vector")
            wtok("tensor", btm_v)
            prev_cup = 0
            for half in range(2):
                hw0 = half * 2048
                wtok("sync", el_w)
                wait("sync", "sV", prev_cup)
                el_v = dma("sync", "sEv", e_sb[:], scr_eline[0:1, hw0:hw0 + 2048].broadcast_to([64, 2048]))
                wait("tensor", "sV", v_Xd)
                wait("tensor", "sV", prev_cup)
                def _sthalf(tt, hw0=hw0, c=c):
                    for blk in range(4):
                        tt.matmul(pmA[0:64, blk * 512:(blk + 1) * 512],
                                  b_tm[:, c * 64:(c + 1) * 64],
                                  Xd[:, hw0 + blk * 512: hw0 + (blk + 1) * 512],
                                  start=True, stop=True)
                TT(_sthalf)
                t_sth = mark("tensor")
                wait("vector", "sT", t_sth)
                wtok("vector", el_v)
                def _cup(v, hw0=hw0):
                    cu = arena3_bf[0:64, 4096 + hw0: 4096 + hw0 + 2048]
                    v.tensor_mul(cu, carry[0:64, hw0:hw0 + 2048], e_sb[:])
                    v.tensor_add(carry[0:64, hw0:hw0 + 2048], cu, pmA[0:64, 0:2048])
                V(_cup)
                prev_cup = mark("vector")
            v_carry = prev_cup
            prev_chunk_V = v_carry

        wait("sync", "sV", v_carry)
        xhl_v = dma("sync", "sXl", scr_xlast[0:1, :], xh_tm[127:128, (NCH - 1) * DI:NCH * DI])
        m_prev["xl"] = xhl_v

        dbg_ytok_tok = None
        if debug and t == 0:
            wait("sync", "sV", v_dterm) if False else None
            dbg_ytok_tok = dma("sync", "sDbg", dbg["dbg_ytok"], y_tok[:])
            wait("sync", "sDbg", sems["sDbg"][1])

        wait("sync", "sV", v_carry)
        if t == 0:
            wtok("sync", drep_v)
        wtok("sync", m_prev["drt"])
        drep_t = dma("sync", "sDrT", arena2_bf[:, 0:4096], scr_dline[0:1, :].broadcast_to([128, DI]))
        m_prev["drt"] = drep_t
        wtok("vector", drep_t)
        wtok("vector", dbg_ytok_tok)
        def _dterm(v):
            for c in range(NCH):
                v.tensor_mul(arena3_bf[:, 0:4096], xh_tm[:, c * DI:(c + 1) * DI],
                             arena2_bf[:, 0:4096])
                v.tensor_add(y_tok[:, c * DI:(c + 1) * DI], y_tok[:, c * DI:(c + 1) * DI],
                             arena3_bf[:, 0:4096])
        V(_dterm)
        v_dterm = mark("vector")

        y_fm = arena2_bf[:, 0:8192]
        wait("sync", "sV", v_dterm)
        ytr = None
        for kk in range(32):
            for c in range(NCH):
                ytr = dma("sync", "sTrY", y_fm[:, kk * T + c * 128: kk * T + c * 128 + 128],
                          y_tok[:, c * DI + kk * 128: c * DI + (kk + 1) * 128], transpose=True)
        wtok("vector", ytr)
        def _gate(v):
            v.tensor_mul(y_tok[:, 0:8192], y_fm[:], z_fm[:])
            v.tensor_mul(arena3_bf[:, 0:8192], y_tok[:, 0:8192], y_tok[:, 0:8192])
        V(_gate)
        v_gate = mark("vector")
        g_fm = y_tok[:, 0:8192]
        sq_fm = arena3_bf[:, 0:8192]
        wait("tensor", "sV", v_gate)
        def _ssq(tt):
            for kk in range(32):
                tt.matmul(pmA[0:1, 0:T], onesb[0:128, 0:1], sq_fm[:, kk * T:(kk + 1) * T],
                          start=(kk == 0), stop=(kk == 31))
        TT(_ssq)
        t_ssq = mark("tensor")
        wait("scalar", "sT", t_ssq)
        S(lambda s: s.copy(srow2[:], pmA[0:1, 0:T]))
        s_ssq = mark("scalar")
        wait("sync", "sS", s_ssq)
        r1 = dma("sync", "sRs", scr_row[0:1, 0:T], srow2[:])
        wtok("sync", r1)
        r2 = dma("sync", "sRs", rstd_c[:, 0:2], scr_row[0:1, 0:T].rearrange("o (c p) -> p (o c)", p=128), slow=True)
        wtok("scalar", r2)
        def _rstd(s):
            s.activation(rstd_c[:, 0:2], rstd_c[:, 0:2], AF.Ln, scale=float(1.0 / DI),
                         bias=rmseps[:, 0:1])
            s.activation(rstd_c[:, 0:2], rstd_c[:, 0:2], AF.Exp, scale=-0.5)
        S(_rstd)
        s_rstd = mark("scalar")

        attnout = arena3_bf[:, 0:4096]
        step2 = 0
        wo_mark = {}
        wo_evac = {}
        for g in range(4):
            pm = PM[g % 2]
            for k in range(32):
                b = step2 % 4
                wb = (wbufA, wbufB, wbufC, wbufD)[b]
                wait("sync", "sT", wo_mark.get(step2 - 4, 0))
                if step2 < 4:
                    wait("sync", "sT", t_ssq)
                wl_v = dma("sync", ("sWa", "sWb", "sWc", "sWd")[b], wb[:, 0:512],
                           scr_woutT[k * 128:(k + 1) * 128, g * 512:(g + 1) * 512])
                wtok("tensor", wl_v)
                if k == 0:
                    wait("tensor", "sV", v_gate)
                    wait("tensor", "sS", s_ssq)
                    wait("tensor", "sS", wo_evac.get(g - 2, 0))
                def _wmm(tt, pm=pm, wb=wb, k=k):
                    for mc in range(4):
                        tt.matmul(pm[0:128, mc * 512:mc * 512 + T], wb[:, mc * 128:(mc + 1) * 128],
                                  g_fm[:, k * T:(k + 1) * T], start=(k == 0), stop=(k == 31))
                TT(_wmm)
                wo_mark[step2] = mark("tensor")
                step2 += 1
            wait("scalar", "sT", wo_mark[step2 - 1])
            S(lambda s, pm=pm, g=g: s.copy(
                attnout[:, g * 1024:(g + 1) * 1024].rearrange("p (a b) -> p a b", a=4),
                pm[:, 0:2048].rearrange("p (a b) -> p a b", a=4)[:, :, 0:T]))
            wo_evac[g] = mark("scalar")

        attn_tok = arena2_bf[:, 0:4096]
        wait("sync", "sS", wo_evac[3])
        atr = None
        for kk in range(16):
            for c in range(2):
                atr = dma("sync", "sTrA", attn_tok[:, c * 2048 + kk * 128: c * 2048 + (kk + 1) * 128],
                          attnout[:, kk * T + c * 128: kk * T + c * 128 + 128], transpose=True)
        wtok("vector", atr)
        wait("vector", "sS", s_rstd)
        def _res1(v):
            for c in range(2):
                v.scalar_tensor_tensor(x_tok[:, c * DIM:(c + 1) * DIM],
                                       attn_tok[:, c * 2048:(c + 1) * 2048],
                                       rstd_c[:, c:c + 1], x_tok[:, c * DIM:(c + 1) * DIM],
                                       ALU.mult, ALU.add)
        V(_res1)
        v_x1 = mark("vector")

        if debug and t == 0:
            wait("sync", "sV", v_x1)
            m_prev["dbgx1"] = dma("sync", "sDbg", dbg["dbg_x1"], x_tok[:])
            wait("sync", "sDbg", sems["sDbg"][1])

        ln2_tm = arena3_bf[:, 4096:8192]
        wait("vector", "sV", v_x1)
        def _l2a(v):
            for c in range(2):
                v.reduce_sum(mcol[:, c:c + 1], x_tok[:, c * DIM:(c + 1) * DIM], axis=AXX)
            v.tensor_scalar_mul(mcol[:, 0:2], mcol[:, 0:2], float(1.0 / DIM))
        V(_l2a)
        v_l2a = mark("vector")
        wait("scalar", "sV", v_l2a)
        def _l2b(s):
            for c in range(2):
                s.activation(arena2_bf[:, 4096 + c * DIM: 4096 + (c + 1) * DIM],
                             x_tok[:, c * DIM:(c + 1) * DIM], AF.Square)
        S(_l2b)
        s_l2b = mark("scalar")
        wait("vector", "sS", s_l2b)
        def _l2c(v):
            for c in range(2):
                v.reduce_sum(mcol[:, 2 + c:3 + c], arena2_bf[:, 4096 + c * DIM:4096 + (c + 1) * DIM],
                             axis=AXX)
            v.tensor_scalar_mul(mcol[:, 2:4], mcol[:, 2:4], float(1.0 / DIM))
            v.tensor_mul(rstd_c[:, 0:2], mcol[:, 0:2], mcol[:, 0:2])
            v.tensor_sub(mcol[:, 2:4], mcol[:, 2:4], rstd_c[:, 0:2])
        V(_l2c)
        v_l2c = mark("vector")
        wait("scalar", "sV", v_l2c)
        def _l2d(s):
            s.activation(mcol[:, 2:4], mcol[:, 2:4], AF.Ln, bias=epscol[:, 0:1])
            s.activation(mcol[:, 2:4], mcol[:, 2:4], AF.Exp, scale=-0.5)
        S(_l2d)
        s_l2d = mark("scalar")
        wait("vector", "sS", s_l2d)
        def _l2e(v):
            for c in range(2):
                v.tensor_scalar(ln2_tm[:, c * DIM:(c + 1) * DIM], x_tok[:, c * DIM:(c + 1) * DIM],
                                mcol[:, c:c + 1], mcol[:, 2 + c:3 + c], ALU.subtract, ALU.mult)
        V(_l2e)
        v_l2e = mark("vector")
        wait("sync", "sV", v_l2e)
        l2tr = None
        for c in range(2):
            for k in range(16):
                l2tr = dma("sync", "sTr1", arena2_bf[:, k * T + c * 128: k * T + c * 128 + 128],
                           ln2_tm[:, c * DIM + k * 128:c * DIM + (k + 1) * 128], transpose=True)
        wtok("vector", l2tr)
        V(lambda v: v.tensor_copy(ln2_f8[:], arena2_bf[:, 0:4096]))
        v_ln2f8 = mark("vector")

        gel = arena3_bf[:, 0:1024]
        ff1b = arena3_bf[:, 1024:2048]
        step3 = 0
        f_mark = {}
        f_evac = {}
        for g in range(16):
            for j, (scr, pm) in enumerate(((scr_wg0, pmA), (scr_wg1, pmB))):
                for k in range(16):
                    b = step3 % 3
                    wb = (w8A, w8B, w8C)[b]
                    wait("sync", "sT", f_mark.get(step3 - 3, 0))
                    if step3 < 3:
                        wait("sync", "sV", v_ln2f8)
                    wv = dma("sync", ("sWa", "sWb", "sWc")[b], wb[:, 0:512],
                             scr[k * 128:(k + 1) * 128, g * 512:(g + 1) * 512])
                    wtok("tensor", wv)
                    if k == 0:
                        wait("tensor", "sV", v_ln2f8)
                        wait("tensor", "sS", f_evac.get((g - 1, j), 0))
                        wait("tensor", "sV", f_evac.get((g - 1, "a"), 0))
                    def _fmm(tt, pm=pm, wb=wb, k=k):
                        for mc in range(4):
                            tt.matmul(pm[0:128, mc * 512:mc * 512 + T], wb[:, mc * 128:(mc + 1) * 128],
                                      ln2_f8[:, k * T:(k + 1) * T], start=(k == 0),
                                      stop=(k == 15))
                    TT(_fmm)
                    f_mark[step3] = mark("tensor")
                    step3 += 1
            wait("scalar", "sT", f_mark[step3 - 1])
            def _fe(s, g=g):
                s.activation(gel[:].rearrange("p (a b) -> p a b", a=4),
                             pmA[:, 0:2048].rearrange("p (a b) -> p a b", a=4)[:, :, 0:T],
                             AF.Gelu_apprx_tanh)
                s.copy(ff1b[:].rearrange("p (a b) -> p a b", a=4),
                       pmB[:, 0:2048].rearrange("p (a b) -> p a b", a=4)[:, :, 0:T])
            S(_fe)
            f_evac[(g, 0)] = f_evac[(g, 1)] = mark("scalar")
            wait("vector", "sS", f_evac[(g, 0)])
            def _fa(v, g=g):
                v.tensor_mul(a8[:, g * 1024:(g + 1) * 1024], gel[:], ff1b[:])
            V(_fa)
            f_evac[(g, "a")] = mark("vector")

        ffnout = arena3_bf[:, 4096:8192]
        step4 = 0
        w3_mark = {}
        w3_evac = {}
        for g in range(4):
            pm = PM[g % 2]
            for k in range(64):
                b = step4 % 3
                wb = (w8A, w8B, w8C)[b]
                wait("sync", "sT", w3_mark.get(step4 - 3, 0))
                if step4 < 3:
                    wait("sync", "sS", f_evac[(15, 0)])
                    wait("sync", "sV", f_evac[(15, "a")])
                wv = dma("sync", ("sWa", "sWb", "sWc")[b], wb[:, 0:512],
                         scr_wl[k * 128:(k + 1) * 128, g * 512:(g + 1) * 512])
                wtok("tensor", wv)
                if k == 0:
                    wait("tensor", "sS", f_evac[(15, 0)])
                    wait("tensor", "sV", f_evac[(15, "a")])
                    wait("tensor", "sS", w3_evac.get(g - 2, 0))
                def _f3(tt, pm=pm, wb=wb, k=k):
                    for mc in range(4):
                        tt.matmul(pm[0:128, mc * 512:mc * 512 + T], wb[:, mc * 128:(mc + 1) * 128],
                                  a8[:, k * T:(k + 1) * T], start=(k == 0),
                                  stop=(k == 63))
                TT(_f3)
                w3_mark[step4] = mark("tensor")
                step4 += 1
            wait("scalar", "sT", w3_mark[step4 - 1])
            S(lambda s, pm=pm, g=g: s.copy(
                ffnout[:, g * 1024:(g + 1) * 1024].rearrange("p (a b) -> p a b", a=4),
                pm[:, 0:2048].rearrange("p (a b) -> p a b", a=4)[:, :, 0:T]))
            w3_evac[g] = mark("scalar")

        wait("sync", "sS", w3_evac[3])
        ftr = None
        for kk in range(16):
            for c in range(2):
                ftr = dma("sync", "sTrA", arena2_bf[:, c * 2048 + kk * 128: c * 2048 + (kk + 1) * 128],
                          ffnout[:, kk * T + c * 128: kk * T + c * 128 + 128], transpose=True)
        wtok("vector", ftr)
        wtok("vector", m_prev["out_st"])
        V(lambda v: v.tensor_add(arena_f32[:, 0:2048], x_tok[:, 0:DIM],
                                 arena2_bf[:, 0:2048]))
        v_outa = mark("vector")
        wait("sync", "sV", v_outa)
        o1 = dma("sync", "sOut", y_d[r0:r0 + 128, :], arena_f32[:, 0:2048])
        wtok("vector", o1)
        V(lambda v: v.tensor_add(arena_f32[:, 0:2048], x_tok[:, DIM:2 * DIM],
                                 arena2_bf[:, 2048:4096]))
        v_out = mark("vector")
        wait("sync", "sV", v_out)
        o2 = dma("sync", "sOut", y_d[r0 + 128:r0 + 256, :], arena_f32[:, 0:2048])
        m_prev["out_st"] = o2
        m_prev["tile_done_V"] = v_out
        m_prev["tile_done_S"] = w3_evac[3]
        m_prev["tile_done_T"] = w3_mark[step4 - 1]

    for s in DMA_SEMS:
        if sems[s][1] > 0:
            wait("sync", s, sems[s][1])

    with nc.Block() as block:
        @block.sync
        def _(sync):
            for f in thunks["sync"]:
                f(sync)

        @block.gpsimd
        def _(g):
            for f in thunks["gpsimd"]:
                f(g)

        @block.vector
        def _(v):
            vp = DrainAfter(v)
            for f in thunks["vector"]:
                f(vp)

        @block.scalar
        def _(s):
            sp = DrainAfter(s)
            for f in thunks["scalar"]:
                f(sp)

        @block.tensor
        def _(tt):
            tp_ = DrainAfter(tt)
            for f in thunks["tensor"]:
                f(tp_)

    es.close()
    return nc


def make_inmaps(inputs):
    x = np.ascontiguousarray(inputs["L0_attn_x"], dtype=np.float32)
    cm = {}
    cm["w_in"] = np.ascontiguousarray(inputs["L0_attn_W_in"], np.float32)
    cm["w_lambda"] = np.ascontiguousarray(inputs["L0_attn_W_lambda"], np.float32)
    cm["w_theta"] = np.ascontiguousarray(inputs["L0_attn_W_theta"], np.float32)
    cm["w_out"] = np.ascontiguousarray(inputs["L0_attn_W_out"], np.float32)
    wg = np.ascontiguousarray(inputs["L0_ffn_w_gating"], np.float32)
    cm["wg0"] = np.ascontiguousarray(wg[0])
    cm["wg1"] = np.ascontiguousarray(wg[1])
    cm["wl"] = np.ascontiguousarray(inputs["L0_ffn_w_linear"], np.float32)
    cm["bbias_c"] = np.ascontiguousarray(np.asarray(inputs["L0_attn_B_bias"], np.float32).reshape(64, 1))
    cm["cbias_c"] = np.ascontiguousarray(np.asarray(inputs["L0_attn_C_bias"], np.float32).reshape(64, 1))
    cm["alog_c"] = np.ascontiguousarray(np.asarray(inputs["L0_attn_A_log"], np.float32).reshape(64, 1))
    cm["d_c"] = np.ascontiguousarray(np.asarray(inputs["L0_attn_D_param"], np.float32).reshape(64, 1))
    cm["dtb_c"] = np.ascontiguousarray(np.asarray(inputs["L0_attn_dt_bias"], np.float32).reshape(64, 1))
    cm["i64"] = np.eye(64, dtype=np.float32)
    cm["i128"] = np.eye(128, dtype=np.float32)
    tri = np.zeros((128, 128), np.float32)
    for lp in range(128):
        tri[lp, lp + 1:] = BIGNEG
    cm["trineg"] = tri
    maps = []
    for b in range(8):
        m = dict(cm)
        m["x"] = np.ascontiguousarray(x[b])
        maps.append(m)
    return maps


def kernel(**inputs):
    nc = build_program(debug=False)
    in_maps = make_inmaps(inputs)
    res = run_bass_kernel_spmd(nc, in_maps, core_ids=list(range(8)))
    out = np.stack([np.asarray(r["y"], np.float32) for r in res.results], axis=0)
    return out




# version 04
# speedup vs baseline: 1.5873x; 1.1168x over previous
import numpy as np
import ml_dtypes
import concourse.bass as bass
import concourse.mybir as mybir
from concourse.bass_utils import run_bass_kernel_spmd

dt = mybir.dt
AF = mybir.ActivationFunctionType
ALU = mybir.AluOpType
AXX = mybir.AxisListType.X
ML_BF16 = ml_dtypes.bfloat16

DIM = 2048
DI = 4096
NH = 64
HD = 64
HID = 8192
NPROJ = 8480
T = 256
NT = 2048 // T
Q = 128
NCH = T // Q
BIGNEG = -1.0e30
RMS_EPS = 1.1920929e-07
LN_EPS = 1e-5
INV2PI = float(1.0 / (2.0 * np.pi))
TWOPI = float(2.0 * np.pi)
GCONST = float(np.sqrt(2.0 / np.pi))
GC2 = 0.044715

F32, F32R, BF, FP8, I32 = dt.float32, dt.float32r, dt.bfloat16, dt.float8e4, dt.int32


def build_program(debug=False):
    nc = bass.Bass("TRN2", target_bir_lowering=False)

    def din(name, shape, dtype=F32):
        return nc.dram_tensor(name, shape, dtype, kind="ExternalInput").ap()

    x_d = din("x", [2048, DIM])
    w_in_d = din("w_in", [8384, DIM])
    w_lam_d = din("w_lambda", [64, DIM])
    w_th_d = din("w_theta", [32, DIM])
    w_out_d = din("w_out", [DIM, DI])
    wg0_d = din("wg0", [DIM, HID])
    wg1_d = din("wg1", [DIM, HID])
    wl_d = din("wl", [HID, DIM])
    bbias_d = din("bbias_c", [64, 1])
    cbias_d = din("cbias_c", [64, 1])
    alog_d = din("alog_c", [64, 1])
    dpar_d = din("d_c", [64, 1])
    dtb_d = din("dtb_c", [64, 1])
    i64_d = din("i64", [64, 64])
    i128_d = din("i128", [128, 128])
    trineg_d = din("trineg", [128, 128])

    y_d = nc.dram_tensor("y", [2048, DIM], F32, kind="ExternalOutput").ap()

    scr_wallT = nc.dram_tensor("scr_wallT", [DIM, NPROJ], BF, kind="Internal").ap()
    scr_woutT = nc.dram_tensor("scr_woutT", [DI, DIM], BF, kind="Internal").ap()
    scr_wg0 = nc.dram_tensor("scr_wg0", [DIM, HID], FP8, kind="Internal").ap()
    scr_wg1 = nc.dram_tensor("scr_wg1", [DIM, HID], FP8, kind="Internal").ap()
    scr_wl = nc.dram_tensor("scr_wl", [HID, DIM], FP8, kind="Internal").ap()
    scr_line = nc.dram_tensor("scr_line", [1, NH * Q], BF, kind="Internal").ap()
    scr_eline = nc.dram_tensor("scr_eline", [1, DI], BF, kind="Internal").ap()
    scr_xlast = nc.dram_tensor("scr_xlast", [1, DI], BF, kind="Internal").ap()
    scr_row = nc.dram_tensor("scr_row", [1, 256], F32, kind="Internal").ap()

    dbg = {}
    def dout(name, shape, dtype):
        dbg[name] = nc.dram_tensor(name, shape, dtype, kind="ExternalOutput").ap()
        return dbg[name]
    if debug:
        dout("dbg_ln1fm", [128, 16 * T], BF)
        dout("dbg_zfm", [128, 32 * T], BF)
        dout("dbg_bc", [128, T], F32)
        dout("dbg_dt", [64, T], F32)
        dout("dbg_ang", [32, T], F32)
        dout("dbg_bro", [64, T], F32)
        dout("dbg_mask1", [128, 1024], BF)
        dout("dbg_X", [128, 4096], BF)
        dout("dbg_gt", [128, 128], BF)
        dout("dbg_cro", [64, T], F32)
        dout("dbg_gb", [64, 2 * T], F32)
        dout("dbg_ge", [128, 4096], BF)
        dout("dbg_ytok", [128, NCH * DI], BF)
        dout("dbg_x1", [128, 2 * DIM], F32)
        dout("dbg_a8", [128, 8 * T], FP8)

    from contextlib import ExitStack
    es = ExitStack()
    sbt = lambda name, shape, dtype: es.enter_context(nc.sbuf_tensor(name, shape, dtype))
    pst = lambda name, shape: es.enter_context(nc.psum_tensor(name, shape, F32))

    x_tok = sbt("x_tok", [128, 2 * DIM], F32)
    ln1_fm = sbt("ln1_fm", [128, 16 * T], BF)
    z_fm = sbt("z_fm", [128, 32 * T], BF)
    xh_tm = sbt("xh_tm", [128, NCH * DI], BF)
    y_tok = sbt("y_tok", [128, NCH * DI], BF)
    carry = sbt("carry", [64, NH * HD], BF)
    arena_bf = sbt("arena_bf", [128, 8192], BF)
    arena2_bf = sbt("arena2_bf", [128, 8192], BF)
    arena3_bf = sbt("arena3_bf", [128, 8192], BF)
    arena_f32 = sbt("arena_f32", [128, 2048], F32)
    arena_i32 = sbt("arena_i32", [128, 2048], I32)
    wbufA = sbt("wbufA", [128, 512], BF)
    wbufB = sbt("wbufB", [128, 512], BF)
    wbufC = sbt("wbufC", [128, 512], BF)
    wbufD = sbt("wbufD", [128, 512], BF)
    w8A = sbt("w8A", [128, 2048], FP8)
    w8B = sbt("w8B", [128, 2048], FP8)
    w8C = sbt("w8C", [128, 512], FP8)
    a8 = sbt("a8", [128, 64 * T], FP8)
    ln2_f8 = sbt("ln2_f8", [128, 16 * T], FP8)
    trig = sbt("trig", [64, 2 * T], BF)
    zeros_t = sbt("zeros_t", [64, 256], F32)
    bc_t = sbt("bc_t", [128, T], F32)
    dtlam = sbt("dtlam", [128, T], F32)
    s128a = sbt("s128a", [128, T], F32)
    dt_t = sbt("dt_t", [64, T], F32)
    dt_r = sbt("dt_r", [64, T], F32R)
    lam_t = sbt("lam_t", [64, T], F32)
    a_fm_t = sbt("a_fm_t", [64, T], F32)
    gam_r = sbt("gam_r", [64, T], F32R)
    bet_r = sbt("bet_r", [64, T], F32R)
    s64a = sbt("s64a", [64, T], F32)
    s64b = sbt("s64b", [64, T], F32)
    s64c = sbt("s64c", [64, T], F32)
    craw = s64c[0:64, :]
    w1 = s64a[0:32, :]
    w2 = s64b[0:32, :]
    bmn = sbt("bmn", [64, T], F32)
    cmn = sbt("cmn", [64, T], F32)
    bm_bf = sbt("bm_bf", [64, T], BF)
    cm_bf = sbt("cm_bf", [64, T], BF)
    b_tm = sbt("b_tm", [128, NCH * 64], BF)
    theta_t = sbt("theta_t", [32, T], F32)
    ang_t = sbt("ang_t", [32, T], F32)
    ang_carry = sbt("ang_carry", [32, 1], F32)
    ang_i = arena_i32[0:32, 0:T]
    acs_fm = sbt("acs_fm", [64, Q], F32)
    acs_r = sbt("acs_r", [64, Q], F32R)
    nacs_r = sbt("nacs_r", [64, Q], F32R)
    acsline = sbt("acsline", [1, 1024], F32R)
    eacs_bf = sbt("eacs_bf", [64, Q], BF)
    d2e_f = sbt("d2e_f", [64, Q], F32)
    d2e_r = sbt("d2e_r", [64, Q], F32R)
    gt_bf = sbt("gt_bf", [128, 128], BF)
    e_sb = sbt("e_sb", [64, 2048], BF)
    row_f = sbt("row_f", [1, T], F32)
    row_r = sbt("row_r", [1, T], F32R)
    dtavg_row = row_f
    dtavg_r = row_r
    srow2 = row_f
    srow2_r = row_r
    rstd_c = sbt("rstd_c", [128, 2], F32)
    mcol = sbt("mcol", [128, 4], F32)
    i64c = sbt("i64c_s", [64, 64], F32)
    i64r = sbt("i64r", [64, 64], F32R)
    i128r = sbt("i128r", [128, 128], F32R)
    trinr = sbt("trinr", [128, 128], F32R)
    onesf = sbt("onesf_s", [128, 1], F32)
    ones64 = sbt("ones64", [64, 64], F32)
    exp_t = sbt("exp_t", [64, 64], BF)
    e_col = sbt("e_col", [64, 1], F32)
    onesr = sbt("onesr", [128, 128], F32R)
    onesb = sbt("onesb", [128, 2], BF)
    identb = sbt("identb", [128, 128], BF)
    smalls_dc = sbt("smalls_dc", [64, 1], F32)
    bbias = sbt("bbias", [64, 1], F32)
    cbias = sbt("cbias", [64, 1], F32)
    dtb = sbt("dtb", [64, 1], F32)
    alog_s = sbt("alog_s", [64, 1], F32)
    mexpA = sbt("mexpA", [64, 1], F32)
    epscol = sbt("epscol", [128, 1], F32)
    rmseps = sbt("rmseps", [128, 1], F32)
    halfpi = sbt("halfpi", [128, 1], F32)

    pmA = pst("pmA", [128, 2048])
    pmB = pst("pmB", [128, 2048])
    PM = [pmA, pmB]

    thunks = {e: [] for e in ("sync", "gpsimd", "vector", "scalar", "tensor")}
    sems = {}
    DMA_SEMS = ("sInit", "sDl", "sDrT", "sPa", "sPb", "sPstA", "sPstB", "sG8a", "sG8b",
                "sP8a", "sP8b", "sX", "sTr1", "sTrXh", "sTrB", "sTrY", "sTrA",
                "sWa", "sWb", "sWc", "sWd", "sTq", "sLam", "sCm", "sTrg", "sRp1", "sRp2", "sSh", "sEw",
                "sEv", "sE2d", "sRep", "sAcl", "sXl", "sRs", "sOut", "sDbg")
    for s in ("sT", "sV", "sS") + DMA_SEMS:
        sems[s] = [es.enter_context(nc.semaphore(s)), 0]

    def on(eng, fn):
        thunks[eng].append(fn)

    def wait(eng, sname, value):
        if value <= 0:
            return
        h = sems[sname][0]
        on(eng, lambda e, h=h, v=value: e.wait_ge(h, v))

    state = {"last_T": None, "last_SS": None}

    def _is_sbuf(ap):
        try:
            return str(ap.space) in ("MemorySpace.SBUF", "SBUF") or "SB" in str(ap.space)
        except Exception:
            return False

    def dma(eng, sname, out, in_, transpose=False, slow=False):
        h = sems[sname][0]
        if eng == "sync":
            ss = (not transpose) and _is_sbuf(out) and _is_sbuf(in_)
            if transpose and state["last_SS"]:
                sn = state["last_SS"][0]
                wait(eng, sn, sems[sn][1])
            if ss and state["last_T"]:
                sn = state["last_T"][0]
                wait(eng, sn, sems[sn][1])
        sems[sname][1] += 16
        v = sems[sname][1]
        on(eng, lambda e, o=out, i=in_, t=transpose, h=h, sl=slow: e.dma_start(
            o, i, transpose=t, allow_slow_non_contiguous=sl).then_inc(h, 16))
        tok = (sname, v)
        if eng == "sync":
            if transpose:
                state["last_T"] = tok
            elif _is_sbuf(out) and _is_sbuf(in_):
                state["last_SS"] = tok
        return tok

    def wtok(eng, tok):
        if tok is None or tok == 0:
            return
        sname, v = tok
        wait(eng, sname, v)

    ENG_SEM = {"tensor": "sT", "vector": "sV", "scalar": "sS", "gpsimd": "sG_nop"}

    DATAPATH = {"tensor_copy", "tensor_tensor", "tensor_add", "tensor_sub", "tensor_mul",
                "tensor_max", "tensor_scalar", "tensor_single_scalar", "tensor_scalar_add",
                "tensor_scalar_sub", "tensor_scalar_mul", "tensor_scalar_max",
                "tensor_scalar_min", "scalar_tensor_tensor", "tensor_reduce", "reduce_max",
                "reduce_sum", "reciprocal", "memset", "memzero", "tensor_tensor_scan",
                "select", "activation", "copy", "mul", "add", "square", "sqrt", "sign",
                "tensor_relu", "transpose", "matmul", "ldweights"}

    class DrainAfter:

        def __init__(self, e):
            self._e = e
            self._W = []
            self._R = []
            self.last = None

        @staticmethod
        def _fp(a):
            t = getattr(a, "tensor", None)
            nm = getattr(t, "name", None)
            if nm is None:
                return None
            try:
                off = a.offset
                dims = [(d[0], d[1]) for d in a.ap]
                rowlen = int(t.shape[-1]) if len(t.shape) > 1 else 1
                st0, sz0 = dims[0]
                p_lo = off // rowlen
                p_step = max(st0 // rowlen, 1)
                p_hi = p_lo + (sz0 - 1) * p_step + 1
                f_lo = off % rowlen
                f_hi = f_lo + sum((sz - 1) * st for st, sz in dims[1:]) + 1
                return (nm, p_lo, p_hi, f_lo, f_hi)
            except Exception:
                return (nm, 0, 1 << 30, 0, 1 << 30)

        @staticmethod
        def _ov(a, b):
            return (a[0] == b[0] and a[1] < b[2] and b[1] < a[2]
                    and a[3] < b[4] and b[3] < a[4])

        def _conflict(self, reads, writes):
            for w in self._W:
                for x in reads:
                    if self._ov(w, x):
                        return True
                for x in writes:
                    if self._ov(w, x):
                        return True
            for r in self._R:
                for x in writes:
                    if self._ov(r, x):
                        return True
            return False

        def __getattr__(self, n):
            a = getattr(self._e, n)
            if n in DATAPATH:
                if n in ("matmul", "ldweights", "transpose"):
                    def wm(*args, **kw):
                        r = a(*args, **kw)
                        self.last = r
                        return r
                    return wm
                def w(*args, **kw):
                    fps = [self._fp(x) for x in list(args) + list(kw.values())]
                    fps = [(i, f) for i, f in enumerate(fps) if f is not None]
                    writes = [f for i, f in fps if i == 0]
                    reads = [f for i, f in fps if i > 0]
                    if self._conflict(reads, writes):
                        self._e.drain()
                        self._W, self._R = [], []
                    r = a(*args, **kw)
                    self._W += writes
                    self._R += reads
                    self.last = r
                    return r
                return w
            if n == "drain":
                def wd(*args, **kw):
                    self._W, self._R = [], []
                    r = a(*args, **kw)
                    self.last = r
                    return r
                return wd
            return a

    def mark(eng):
        sname = {"tensor": "sT", "vector": "sV", "scalar": "sS"}[eng]
        h = sems[sname][0]
        sems[sname][1] += 1
        v = sems[sname][1]

        def _m(e, h=h):
            last = getattr(e, "last", None)
            ok = False
            if last is not None:
                try:
                    last.then_inc(h, 1)
                    ok = True
                except Exception:
                    ok = False
                e.last = None
            if not ok:
                e.drain().then_inc(h, 1)
        on(eng, _m)
        return v

    def attach_inc(eng, buildfn):
        sname = {"tensor": "sT", "vector": "sV", "scalar": "sS"}[eng]
        h = sems[sname][0]
        sems[sname][1] += 1
        v = sems[sname][1]
        on(eng, lambda e, h=h, f=buildfn: f(e).then_inc(h, 1))
        return v

    V = lambda fn: on("vector", fn)
    S = lambda fn: on("scalar", fn)
    TT = lambda fn: on("tensor", fn)

    dma("sync", "sInit", i64c[:], i64_d)
    dma("sync", "sInit", bbias[:], bbias_d)
    dma("sync", "sInit", cbias[:], cbias_d)
    dma("sync", "sInit", dtb[:], dtb_d)
    dma("sync", "sInit", alog_s[:], alog_d)
    dma("sync", "sInit", arena_f32[:, 0:128], i128_d)
    dma("sync", "sInit", arena_f32[:, 128:256], trineg_d)
    c_loaded = dcol_v = dma("sync", "sInit", smalls_dc[:], dpar_d)

    wtok("vector", c_loaded)
    def _cinit(v):
        v.tensor_copy(i64r[:], i64c[:])
        v.tensor_copy(i128r[:], arena_f32[:, 0:128])
        v.tensor_copy(identb[:], arena_f32[:, 0:128])
        v.tensor_copy(trinr[:], arena_f32[:, 128:256])
        v.memset(onesf[:], 1.0)
        v.memset(ones64[:], 1.0)
        v.memset(arena_f32[:, 256:384], 1.0)
        v.tensor_copy(onesr[:], arena_f32[:, 256:384])
        v.tensor_copy(onesb[:], arena_f32[:, 256:258])
        v.memset(epscol[:], LN_EPS)
        v.memset(rmseps[:], RMS_EPS)
        v.memset(halfpi[:], float(np.pi / 2))
        v.memset(zeros_t[:], 0.0)
        v.memset(carry[:], 0.0)
        v.memset(ang_carry[:], 0.0)
    V(_cinit)
    v_cinit = mark("vector")

    wtok("scalar", c_loaded)
    S(lambda s: s.activation(mexpA[:], alog_s[:], AF.Exp))
    S(lambda s: s.mul(mexpA[:], mexpA[:], -1.0))
    s_mexpA = mark("scalar")

    scr_dline = nc.dram_tensor("scr_dline", [1, DI], BF, kind="Internal").ap()
    V(lambda v: v.tensor_scalar(exp_t[:], ones64[:], smalls_dc[:, 0:1], None, ALU.mult))
    v_dline = mark("vector")
    wait("sync", "sV", v_dline)
    drep_v = dma("sync", "sDl", scr_dline[0:1, :].rearrange("o (h p) -> o h p", h=64), exp_t[:])

    inbufs = [arena_f32, x_tok[:, 2048:4096]]
    wq_out = arena3_bf[:, 0:4096]
    wqT_out = arena3_bf[:, 4096:8192]
    prev_q2 = [v_cinit, v_cinit]
    prev_tr = [None, None]
    prev_trst = [None, None]

    def fq_load(ch):
        idx, nrows, loads, dstf = ch
        p = idx % 2
        wait("sync", "sV", prev_q2[p])
        vload = None
        for sl, srcap in loads:
            vload = dma("sync", ("sPa", "sPb")[p], sl, srcap)
        ch.append(vload)

    def fq_q1(ch):
        idx, nrows, loads, dstf, vload = ch
        p = idx % 2
        buf = inbufs[p]
        sc = s128a[0:nrows, p * 32:p * 32 + 32]
        wtok("vector", vload)
        def _q1(v, buf=buf, nrows=nrows, sc=sc):
            for s_ in range(16):
                v.reduce_max(sc[:, s_:s_ + 1],
                             buf[0:nrows, s_ * 128:(s_ + 1) * 128],
                             axis=AXX, apply_absolute_value=True)
            v.tensor_scalar_max(sc[:, 0:16], sc[:, 0:16], 1e-10)
            v.reciprocal(sc[:, 16:32], sc[:, 0:16])
            v.tensor_scalar_mul(sc[:, 16:32], sc[:, 16:32], 7.0)
            v.tensor_scalar_mul(sc[:, 0:16], sc[:, 0:16], float(1.0 / 7.0))
            for s_ in range(16):
                v.tensor_scalar(buf[0:nrows, s_ * 128:(s_ + 1) * 128],
                                buf[0:nrows, s_ * 128:(s_ + 1) * 128],
                                sc[:, 16 + s_:17 + s_], 8.0, ALU.mult, ALU.add)
        V(_q1)
        ch.append(mark("vector"))

    def fq_castA(ch):
        idx, nrows, loads, dstf, vload, q1 = ch
        p = idx % 2
        buf = inbufs[p]
        wait("scalar", "sV", q1)
        S(lambda s, buf=buf, nrows=nrows, p=p: s.copy(
            arena_i32[0:nrows, p * 1024:p * 1024 + 1024], buf[0:nrows, 0:1024]))
        ch.append(mark("scalar"))

    def fq_backA(ch):
        idx, nrows, loads, dstf, vload, q1, qsA = ch
        p = idx % 2
        c1f = arena_bf[0:nrows, p * 2048:p * 2048 + 2048]
        wait("vector", "sS", qsA)
        V(lambda v, c1f=c1f, nrows=nrows, p=p: v.tensor_copy(
            c1f[:, 0:1024], arena_i32[0:nrows, p * 1024:p * 1024 + 1024]))
        ch.append(mark("vector"))

    def fq_castB(ch):
        idx, nrows, loads, dstf, vload, q1, qsA, q2a = ch
        p = idx % 2
        buf = inbufs[p]
        wait("scalar", "sV", q2a)
        S(lambda s, buf=buf, nrows=nrows, p=p: s.copy(
            arena_i32[0:nrows, p * 1024:p * 1024 + 1024], buf[0:nrows, 1024:2048]))
        ch.append(mark("scalar"))

    def fq_final(ch):
        idx, nrows, loads, dstf, vload, q1, qsA, q2a, qsB = ch
        p = idx % 2
        buf = inbufs[p]
        sc = s128a[0:nrows, p * 32:p * 32 + 32]
        c1f = arena_bf[0:nrows, p * 2048:p * 2048 + 2048]
        gp = arena_bf[0:nrows, 4096 + p * 2048:4096 + p * 2048 + 2048]
        wq = wq_out[:, p * 2048:p * 2048 + 2048]
        wait("vector", "sS", qsB)
        wtok("vector", prev_tr[p])
        def _q2c(v, buf=buf, wq=wq, nrows=nrows, sc=sc, c1f=c1f, gp=gp, p=p):
            v.tensor_copy(c1f[:, 1024:2048], arena_i32[0:nrows, p * 1024:p * 1024 + 1024])
            v.scalar_tensor_tensor(gp[:], buf[0:nrows, :], 0.5, c1f[:],
                                   ALU.subtract, ALU.is_ge)
            v.tensor_add(buf[0:nrows, :], c1f[:], gp[:])
            for s_ in range(16):
                v.tensor_scalar(wq[0:nrows, s_ * 128:(s_ + 1) * 128],
                                buf[0:nrows, s_ * 128:(s_ + 1) * 128],
                                -8.0, sc[:, s_:s_ + 1], ALU.add, ALU.mult)
        V(_q2c)
        q2 = mark("vector")
        prev_q2[p] = q2
        ch.append(q2)

    prev_evac = [0, 0]

    def fq_ptr(ch):
        idx, nrows, loads, dstf, vload, q1, qsA, q2a, qsB, q2 = ch
        p = idx % 2
        wq = wq_out[:, p * 2048:p * 2048 + 2048]
        pm = PM[p]
        wait("tensor", "sV", q2)
        wait("tensor", "sS", prev_evac[p])
        if idx < 2:
            wait("tensor", "sV", v_cinit)
        def _tr(tt, pm=pm, wq=wq, nrows=nrows):
            for k in range(16):
                tt.matmul(pm[0:128, k * 128:k * 128 + nrows],
                          wq[0:nrows, k * 128:(k + 1) * 128],
                          identb[0:nrows, 0:nrows], start=True, stop=True)
        TT(_tr)
        tm = mark("tensor")
        prev_tr[p] = ("sT", tm)
        ch.append(tm)

    def fq_pevac(ch):
        idx, nrows, loads, dstf, vload, q1, qsA, q2a, qsB, q2, tm = ch
        p = idx % 2
        wqT = wqT_out[:, p * 2048:p * 2048 + 2048]
        pm = PM[p]
        wait("scalar", "sT", tm)
        wtok("scalar", prev_trst[p])
        if nrows == 128:
            S(lambda s, wqT=wqT, pm=pm: s.copy(wqT[:], pm[:, 0:2048]))
        else:
            def _ev(s, wqT=wqT, pm=pm, nrows=nrows):
                for k in range(16):
                    s.copy(wqT[:, k * 128:k * 128 + nrows],
                           pm[:, k * 128:k * 128 + nrows])
            S(_ev)
        ev = mark("scalar")
        prev_evac[p] = ev
        ch.append(ev)

    def fq_pst(ch):
        idx, nrows, loads, dstf, vload, q1, qsA, q2a, qsB, q2, tm, ev = ch
        p = idx % 2
        wqT = wqT_out[:, p * 2048:p * 2048 + 2048]
        wait("sync", "sS", ev)
        st = None
        for k in range(16):
            st = dma("sync", ("sPstA", "sPstB")[p], dstf(k),
                     wqT[0:128, k * 128:k * 128 + nrows])
        prev_trst[p] = st

    STAGES = [fq_load, fq_q1, fq_castA, fq_backA, fq_castB, fq_final,
              fq_ptr, fq_pevac, fq_pst]

    chunks = []
    idx = 0
    for ci in range(67):
        r0 = ci * 128
        nrows = min(128, NPROJ - r0)
        buf = inbufs[idx % 2]
        loads = []
        for srcap, lo, hi in ((w_in_d, 0, 8384), (w_lam_d, 8384, 8448), (w_th_d, 8448, 8480)):
            s0, s1 = max(r0, lo), min(r0 + nrows, hi)
            if s0 < s1:
                loads.append((buf[s0 - r0:s1 - r0, :], srcap[s0 - lo:s1 - lo, :]))
        chunks.append([idx, nrows, loads,
                       (lambda k, r0=r0, nrows=nrows: scr_wallT[k * 128:(k + 1) * 128, r0:r0 + nrows])])
        idx += 1
    for ci in range(32):
        r0 = (ci // 2) * 128
        c0 = (ci % 2) * 2048
        buf = inbufs[idx % 2]
        chunks.append([idx, 128, [(buf[:], w_out_d[r0:r0 + 128, c0:c0 + 2048])],
                       (lambda k, r0=r0, c0=c0: scr_woutT[c0 + k * 128:c0 + (k + 1) * 128, r0:r0 + 128])])
        idx += 1

    NCHK = len(chunks)
    f8_pending = []

    def emit_pair(j):
        for st in STAGES[1:6]:
            for i in (2 * j, 2 * j + 1):
                if i < NCHK:
                    st(chunks[i])
        for i in (2 * (j + 1), 2 * (j + 1) + 1):
            if i < NCHK:
                fq_load(chunks[i])
        for st in STAGES[6:]:
            for i in (2 * j, 2 * j + 1):
                if i < NCHK:
                    st(chunks[i])
        for _ in range(4):
            if f8_pending:
                f8_pending.pop(0)()

    for i in (0, 1):
        fq_load(chunks[i])
    prev8 = [None, None]
    j8 = 0
    def f8hop(src_ap, dst_ap):
        nonlocal j8
        b = (w8A, w8B)[j8 % 2]
        wtok("gpsimd", prev8[j8 % 2])
        v8 = dma("gpsimd", ("sG8a", "sG8b")[j8 % 2], b[:], src_ap)
        wtok("sync", v8)
        prev8[j8 % 2] = dma("sync", ("sP8a", "sP8b")[j8 % 2], dst_ap, b[:])
        j8 += 1
    for srcd, dstd in ((wg0_d, scr_wg0), (wg1_d, scr_wg1)):
        for ci in range(16):
            r0 = ci * 128
            for half in range(4):
                c0 = half * 2048
                f8_pending.append(lambda s=srcd[r0:r0 + 128, c0:c0 + 2048],
                                  d=dstd[r0:r0 + 128, c0:c0 + 2048]: f8hop(s, d))
    for ci in range(64):
        r0 = ci * 128
        f8_pending.append(lambda s=wl_d[r0:r0 + 128, :],
                          d=scr_wl[r0:r0 + 128, :]: f8hop(s, d))

    for j in range((NCHK + 1) // 2):
        emit_pair(j)
    while f8_pending:
        f8_pending.pop(0)()

    prep_V = sems["sV"][1]
    prep_S = sems["sS"][1]
    prep_toks = [(s, sems[s][1]) for s in ("sPstA", "sPstB", "sP8a", "sP8b", "sDl")
                 if sems[s][1] > 0]

    m_prev = {"tile_done_V": prep_V, "tile_done_S": 0, "tile_done_T": 0,
              "out_st": None, "xl": None, "drt": None}

    for t in range(NT):
        r0 = t * T
        if t == 0:
            for tk in prep_toks:
                wtok("sync", tk)
        wait("sync", "sV", m_prev["tile_done_V"])
        wait("sync", "sS", m_prev["tile_done_S"])
        wait("sync", "sT", m_prev["tile_done_T"])
        wtok("sync", m_prev.get("dbgx1"))
        x_in = None
        for c in range(2):
            x_in = dma("sync", "sX", x_tok[:, c * DIM:(c + 1) * DIM], x_d[r0 + c * 128:r0 + (c + 1) * 128, :])

        ln_tm = arena3_bf[:, 4096:8192]
        wtok("vector", x_in)
        wait("vector", "sS", s_mexpA)
        def _ln(v, src=x_tok):
            for c in range(2):
                xc = src[:, c * DIM:(c + 1) * DIM]
                v.reduce_sum(mcol[:, c:c + 1], xc, axis=AXX)
            v.tensor_scalar_mul(mcol[:, 0:2], mcol[:, 0:2], float(1.0 / DIM))
        V(_ln)
        vln = mark("vector")
        wait("scalar", "sV", vln)
        def _lns(s, src=x_tok):
            for c in range(2):
                s.activation(arena2_bf[:, c * DIM:(c + 1) * DIM], src[:, c * DIM:(c + 1) * DIM],
                             AF.Square)
        S(_lns)
        sln = mark("scalar")
        wait("vector", "sS", sln)
        def _ln2(v, src=x_tok, dstv=ln_tm):
            for c in range(2):
                v.reduce_sum(mcol[:, 2 + c:3 + c], arena2_bf[:, c * DIM:(c + 1) * DIM], axis=AXX)
            v.tensor_scalar_mul(mcol[:, 2:4], mcol[:, 2:4], float(1.0 / DIM))
            v.tensor_mul(rstd_c[:, 0:2], mcol[:, 0:2], mcol[:, 0:2])
            v.tensor_sub(mcol[:, 2:4], mcol[:, 2:4], rstd_c[:, 0:2])
        V(_ln2)
        vln2 = mark("vector")
        wait("scalar", "sV", vln2)
        def _ln3(s):
            s.activation(mcol[:, 2:4], mcol[:, 2:4], AF.Ln, bias=epscol[:, 0:1])
            s.activation(mcol[:, 2:4], mcol[:, 2:4], AF.Exp, scale=-0.5)
        S(_ln3)
        sln3 = mark("scalar")
        wait("vector", "sS", sln3)
        def _ln4(v, src=x_tok, dstv=ln_tm):
            for c in range(2):
                v.tensor_scalar(dstv[:, c * DIM:(c + 1) * DIM], src[:, c * DIM:(c + 1) * DIM],
                                mcol[:, c:c + 1], mcol[:, 2 + c:3 + c], ALU.subtract, ALU.mult)
        V(_ln4)
        vln4 = mark("vector")
        wait("sync", "sV", vln4)
        tr_v = None
        for c in range(2):
            for k in range(16):
                tr_v = dma("sync", "sTr1", ln1_fm[:, k * T + c * 128: k * T + c * 128 + 128],
                           ln_tm[:, c * DIM + k * 128: c * DIM + (k + 1) * 128], transpose=True)
        ln1fm_done = tr_v

        if debug and t == 0:
            wtok("sync", ln1fm_done)
            dma("sync", "sDbg", dbg["dbg_ln1fm"], ln1_fm[:])
            wait("sync", "sDbg", sems["sDbg"][1])

        xh_fm = arena_bf[:, 0:8192]
        wload = [None, None]
        gmark = {}
        evac_mark = {}
        step = 0
        for g in range(17):
            rg0 = g * 512
            mg = min(512, NPROJ - rg0)
            pm = PM[g % 2]
            for k in range(16):
                b = step % 4
                wb = (wbufA, wbufB, wbufC, wbufD)[b]
                wait("sync", "sT", gmark.get(("buf", step - 4), 0))
                wload[b % 2] = dma("sync", ("sWa", "sWb", "sWc", "sWd")[b], wb[:, 0:mg],
                               scr_wallT[k * 128:(k + 1) * 128, rg0:rg0 + mg])
                wtok("tensor", wload[b % 2])
                if k == 0:
                    wait("tensor", "sS", evac_mark.get(g - 2, prep_S if (t == 0 and g < 2) else 0))
                    wtok("tensor", ln1fm_done)
                def _mm(tt, pm=pm, wb=wb, k=k, mg=mg):
                    nmc = (mg + 127) // 128
                    for mc in range(nmc):
                        M = min(128, mg - mc * 128)
                        tt.matmul(pm[0:M, mc * 512:mc * 512 + T],
                                  wb[:, mc * 128:mc * 128 + M],
                                  ln1_fm[:, k * T:(k + 1) * T],
                                  start=(k == 0), stop=(k == 15))
                TT(_mm)
                gmark[("buf", step)] = mark("tensor")
                step += 1
            gm = gmark[("buf", step - 1)]
            wait("scalar", "sT", gm)
            pm_str = pm[:, 0:2048].rearrange("p (a b) -> p a b", a=4)[:, :, 0:T]
            if g < 8:
                S(lambda s, pm_str=pm_str, g=g: s.activation(
                    z_fm[:, g * 1024:(g + 1) * 1024].rearrange("p (a b) -> p a b", a=4),
                    pm_str, AF.Silu))
            elif g < 16:
                S(lambda s, pm_str=pm_str, g=g: s.activation(
                    xh_fm[:, (g - 8) * 1024:(g - 7) * 1024].rearrange("p (a b) -> p a b", a=4),
                    pm_str, AF.Silu))
            else:
                def _e8(s, pm=pm):
                    s.copy(bc_t[:], pm[:, 0:T])
                    s.copy(dtlam[:], pm[:, 512:512 + T])
                    s.copy(theta_t[:], pm[0:32, 1024:1024 + T])
                S(_e8)
            evac_mark[g] = mark("scalar")
        evac_mark[7] = evac_mark[15]
        inproj_S = evac_mark[16]

        if debug and t == 0:
            wait("sync", "sS", evac_mark[7])
            dma("sync", "sDbg", dbg["dbg_zfm"], z_fm[:])
            wait("sync", "sDbg", sems["sDbg"][1])
            wait("sync", "sS", inproj_S)
            dma("sync", "sDbg", dbg["dbg_bc"], bc_t[:])
            wait("sync", "sDbg", sems["sDbg"][1])

        wtok("sync", m_prev["xl"])
        wait("sync", "sS", evac_mark[7])
        xh_tr = None
        for kk in range(32):
            for c in range(2):
                xh_tr = dma("sync", "sTrXh", xh_tm[:, c * DI + kk * 128: c * DI + (kk + 1) * 128],
                            xh_fm[:, kk * T + c * 128: kk * T + c * 128 + 128], transpose=True)
        xh_tm_done = xh_tr

        wait("scalar", "sS", inproj_S)
        def _sm1(s):
            s.activation(s64a[:], dtlam[0:64, :], AF.Abs, bias=dtb[:, 0:1])
            s.activation(s64a[:], s64a[:], AF.Exp, scale=-1.0)
            s.activation(s64a[:], s64a[:], AF.Ln, bias=onesf[0:64, 0:1])
            s.activation(s64b[:], dtlam[0:64, :], AF.Relu, bias=dtb[:, 0:1])
            s.activation(s128a[64:128, :], dtlam[64:128, :], AF.Exp, scale=-1.0)
        S(_sm1)
        sm1 = mark("scalar")
        wait("vector", "sS", sm1)
        def _sm2(v):
            v.tensor_add(dt_t[:], s64a[:], s64b[:])
            v.tensor_copy(dt_r[:], dt_t[:])
            v.tensor_scalar(a_fm_t[:], dt_t[:], mexpA[:, 0:1], None, ALU.mult)
            v.tensor_scalar(s128a[64:128, :], s128a[64:128, :], 1.0, None, ALU.add)
            v.reciprocal(s128a[64:128, :], s128a[64:128, :])
        V(_sm2)
        sm2 = mark("vector")
        wait("sync", "sV", sm2)
        lam_mv = dma("sync", "sLam", lam_t[:], s128a[64:128, :])
        wait("tensor", "sV", sm2)
        wait("tensor", "sS", evac_mark.get(7, 0))
        wait("tensor", "sS", inproj_S)
        TT(lambda tt: tt.matmul(pmA[0:1, 0:T], onesr[0:64, 0:1], dt_r[:], start=True, stop=True))
        t_dtavg = mark("tensor")
        wait("scalar", "sT", t_dtavg)
        S(lambda s: s.activation(dtavg_row[:], pmA[0:1, 0:T], AF.Copy, scale=float(1.0 / NH)))
        s_dtavg = mark("scalar")
        wait("vector", "sS", s_dtavg)
        V(lambda v: v.tensor_copy(dtavg_r[:], dtavg_row[:]))
        v_dtavg = mark("vector")
        wait("tensor", "sV", v_dtavg)
        TT(lambda tt: tt.matmul(pmA[0:32, 0:T], onesr[0:1, 0:32], dtavg_r[:], start=True, stop=True))
        t_dtrep = mark("tensor")
        w3 = s64c[0:32, :]
        wait("vector", "sT", t_dtrep)
        def _ang(v):
            v.tensor_mul(w1[:], theta_t[:], pmA[0:32, 0:T])
            v.tensor_tensor_scan(ang_t[:], w1[:], zeros_t[0:32, 0:T], 0.0, ALU.add, ALU.add)
            v.tensor_scalar(ang_t[:], ang_t[:], ang_carry[:, 0:1], None, ALU.add)
            v.tensor_copy(ang_carry[:], ang_t[:, T - 1:T])
            v.tensor_scalar_mul(w1[:], ang_t[:], INV2PI)
        V(_ang)
        v_ang = mark("vector")
        wait("scalar", "sV", v_ang)
        S(lambda s: s.copy(ang_i[:], w1[:]))
        s_angi = mark("scalar")
        wait("vector", "sS", s_angi)
        def _ang2(v):
            v.tensor_copy(w2[:], ang_i[:])
            v.scalar_tensor_tensor(w3[:], w1[:], 0.5, w2[:], ALU.subtract, ALU.is_ge)
            v.scalar_tensor_tensor(theta_t[:], w1[:], 0.5, w2[:], ALU.add, ALU.is_le)
            v.tensor_add(w2[:], w2[:], w3[:])
            v.tensor_sub(w2[:], w2[:], theta_t[:])
            v.scalar_tensor_tensor(w1[:], w2[:], -TWOPI, ang_t[:], ALU.mult, ALU.add)
            v.tensor_scalar(w2[:], ang_t[:], INV2PI, 0.25, ALU.mult, ALU.add)
        V(_ang2)
        v_ang2 = mark("vector")
        wait("scalar", "sV", v_ang2)
        S(lambda s: s.copy(ang_i[:], w2[:]))
        s_angi2 = mark("scalar")
        wait("vector", "sS", s_angi2)
        def _ang3(v):
            v.tensor_copy(w3[:], ang_i[:])
            v.scalar_tensor_tensor(theta_t[:], w2[:], 0.5, w3[:], ALU.subtract, ALU.is_ge)
            v.scalar_tensor_tensor(w2[:], w2[:], 0.5, w3[:], ALU.add, ALU.is_le)
            v.tensor_add(w3[:], w3[:], theta_t[:])
            v.tensor_sub(w3[:], w3[:], w2[:])
            v.scalar_tensor_tensor(w2[:], w3[:], -TWOPI, ang_t[:], ALU.mult, ALU.add)
            v.tensor_scalar(w2[:], w2[:], float(np.pi / 2), None, ALU.add)
        V(_ang3)
        v_ang3 = mark("vector")
        wait("scalar", "sV", v_ang3)
        def _trig(s):
            s.activation(trig[0:32, 0:T], w1[:], AF.Sin)
            s.activation(trig[0:32, T:2 * T], w2[:], AF.Sin)
        S(_trig)
        s_trig = mark("scalar")
        wait("sync", "sS", s_trig)
        trig_mv = dma("sync", "sTrg", trig[32:64, :], trig[0:32, :])

        if debug and t == 0:
            wait("sync", "sV", v_ang)
            dma("sync", "sDbg", dbg["dbg_ang"], ang_t[:])
            wait("sync", "sDbg", sems["sDbg"][1])
            wait("sync", "sS", inproj_S)
            dma("sync", "sDbg", dbg["dbg_dt"], dt_t[:])
            wait("sync", "sDbg", sems["sDbg"][1])

        wait("sync", "sV", v_ang3)
        cm_mv = dma("sync", "sCm", craw[:], bc_t[64:128, :])
        wait("vector", "sS", inproj_S)
        wait("vector", "sS", s_trig)
        wait("vector", "sT", t_dtrep)
        def _bc1(v):
            v.tensor_mul(s64a[:], bc_t[0:64, :], bc_t[0:64, :])
            v.tensor_copy(dt_r[:], s64a[:])
        V(_bc1)
        v_bc1 = mark("vector")
        wait("tensor", "sV", v_bc1)
        TT(lambda tt: tt.matmul(pmB[0:1, 0:T], onesr[0:64, 0:1], dt_r[:], start=True, stop=True))
        t_bssq = mark("tensor")
        wtok("vector", cm_mv)
        wait("vector", "sT", t_bssq)
        def _bc2(v):
            v.tensor_mul(s64b[:], craw[:], craw[:])
            v.tensor_copy(dt_r[:], s64b[:])
        V(_bc2)
        v_bc2 = mark("vector")
        wait("tensor", "sV", v_bc2)
        TT(lambda tt: tt.matmul(pmB[0:1, T:2 * T], onesr[0:64, 0:1], dt_r[:], start=True, stop=True))
        t_cssq = mark("tensor")
        wait("scalar", "sT", t_cssq)
        def _bc3(s):
            s.activation(srow2[:], pmB[0:1, 0:T], AF.Ln, scale=float(1.0 / 64), bias=rmseps[0:1, 0:1])
            s.activation(srow2[:], srow2[:], AF.Exp, scale=-0.5)
        S(_bc3)
        s_brstd = mark("scalar")
        wait("vector", "sS", s_brstd)
        V(lambda v: v.tensor_copy(srow2_r[:], srow2[:]))
        v_brstd = mark("vector")
        wait("tensor", "sV", v_brstd)
        TT(lambda tt: tt.matmul(pmA[0:64, 0:T], onesr[0:1, 0:64], srow2_r[:], start=True, stop=True))
        t_brep = mark("tensor")
        wait("scalar", "sT", t_brep)
        def _bc4(s):
            s.activation(srow2[:], pmB[0:1, T:2 * T], AF.Ln, scale=float(1.0 / 64), bias=rmseps[0:1, 0:1])
            s.activation(srow2[:], srow2[:], AF.Exp, scale=-0.5)
        S(_bc4)
        s_crstd = mark("scalar")
        wait("vector", "sS", s_crstd)
        wait("vector", "sT", t_brep)
        def _bc5(v):
            v.tensor_mul(bmn[:], bc_t[0:64, :], pmA[0:64, 0:T])
            v.tensor_scalar(bmn[:], bmn[:], bbias[:, 0:1], None, ALU.add)
            v.tensor_copy(srow2_r[:], srow2[:])
        V(_bc5)
        v_bc5 = mark("vector")
        wait("tensor", "sV", v_bc5)
        TT(lambda tt: tt.matmul(pmA[0:64, T:2 * T], onesr[0:1, 0:64], srow2_r[:], start=True, stop=True))
        t_crep = mark("tensor")
        wait("vector", "sT", t_crep)
        def _bc6(v):
            v.tensor_mul(cmn[:], craw[:], pmA[0:64, T:2 * T])
            v.tensor_scalar(cmn[:], cmn[:], cbias[:, 0:1], None, ALU.add)
        V(_bc6)
        v_bc6 = mark("vector")

        def rope(src, v_src_mark, prev_mv2=None):
            wait("sync", "sV", v_src_mark)
            mv1 = dma("sync", "sRp1", w1[:], src[32:64, :])
            wtok("vector", mv1)
            wtok("vector", prev_mv2)
            wtok("vector", trig_mv)
            def _r(v, src=src):
                sin0 = trig[0:32, 0:T]
                cos0 = trig[0:32, T:2 * T]
                v.tensor_mul(w2[:], w1[:], sin0)
                v.tensor_mul(theta_t[:], src[0:32, :], cos0)
                v.tensor_sub(theta_t[:], theta_t[:], w2[:])
                v.tensor_mul(w2[:], src[0:32, :], sin0)
                v.tensor_mul(w1[:], w1[:], cos0)
                v.tensor_add(w2[:], w2[:], w1[:])
                v.tensor_copy(src[0:32, :], theta_t[:])
            V(_r)
            vr = mark("vector")
            wait("sync", "sV", vr)
            mv2 = dma("sync", "sRp2", src[32:64, :], w2[:])
            return mv2
        b_roped = rope(bmn, v_bc5)
        c_roped = rope(cmn, v_bc6, prev_mv2=b_roped)
        if debug and t == 0:
            wtok("sync", c_roped)
            dma("sync", "sDbg", dbg["dbg_cro"], cmn[:])
            wait("sync", "sDbg", sems["sDbg"][1])
        wtok("vector", c_roped)
        def _bc7(v):
            v.tensor_copy(bm_bf[:], bmn[:])
            v.tensor_copy(cm_bf[:], cmn[:])
        V(_bc7)
        v_bcbf = mark("vector")
        wait("sync", "sV", v_bcbf)
        btm_v = None
        for c in range(NCH):
            btm_v = dma("sync", "sTrB", b_tm[:, c * 64:(c + 1) * 64],
                        bm_bf[0:64, c * 128:(c + 1) * 128], transpose=True)

        if debug and t == 0:
            wait("sync", "sV", v_bcbf)
            dma("sync", "sDbg", dbg["dbg_bro"], bmn[:])
            wait("sync", "sDbg", sems["sDbg"][1])

        wait("scalar", "sV", sm2)
        wait("scalar", "sV", v_bc6)
        S(lambda s: s.activation(s64c[:], a_fm_t[:], AF.Exp))
        s_expA = mark("scalar")
        wait("vector", "sS", s_expA)
        wtok("vector", lam_mv)
        wtok("vector", c_roped)
        def _gb(v):
            v.tensor_mul(s64a[:], lam_t[:], dt_t[:])
            v.tensor_copy(gam_r[:], s64a[:])
            v.tensor_scalar(s64b[:], lam_t[:], -1.0, 1.0, ALU.mult, ALU.add)
            v.tensor_mul(s64b[:], s64b[:], dt_t[:])
            v.tensor_mul(s64b[:], s64b[:], s64c[:])
            v.tensor_copy(bet_r[:], s64b[:])
        V(_gb)
        v_gb = mark("vector")
        if debug and t == 0:
            wait("sync", "sV", v_gb)
            dma("sync", "sDbg", dbg["dbg_gb"][0:64, 0:T], s64a[:])
            wait("sync", "sDbg", sems["sDbg"][1])
            dma("sync", "sDbg", dbg["dbg_gb"][0:64, T:2 * T], s64b[:])
            wait("sync", "sDbg", sems["sDbg"][1])

        prev_chunk_V = 0
        for c in range(NCH):
            ccol = c * Q
            xh_c = xh_tm[:, c * DI:(c + 1) * DI]
            xhs = arena_bf[:, 0:4096]
            Xc = arena3_bf[:, 0:4096]
            Xd = arena3_bf[:, 4096:8192]
            ge = arena2_bf[:, 0:4096]
            be = arena2_bf[:, 4096:8192]

            wtok("sync", xh_tm_done)
            wait("sync", "sS", evac_mark[7])
            wait("sync", "sV", prev_chunk_V)
            if c == 0:
                sh1 = dma("sync", "sSh", xhs[1:128, :], xh_c[0:127, :])
                if t == 0:
                    sh2 = sh1
                    wtok("vector", sh1)
                    V(lambda v: v.memset(xhs[0:1, :], 0.0))
                    mark("vector")
                else:
                    wtok("sync", m_prev["xl"])
                    sh2 = dma("sync", "sSh", xhs[0:1, :], scr_xlast[0:1, :])
            else:
                sh1 = dma("sync", "sSh", xhs[1:128, :], xh_c[0:127, :])
                sh2 = dma("sync", "sSh", xhs[0:1, :], xh_tm[127:128, (c - 1) * DI:c * DI])
            wait("tensor", "sV", v_gb)
            wait("tensor", "sV", prev_chunk_V)
            def _gbe(tt, ccol=ccol):
                for half in range(2):
                    pm = PM[half]
                    for blk in range(4):
                        h0 = half * 32 + blk * 8
                        rhs = i64r[:, h0:h0 + 8].unsqueeze(2).broadcast_to([64, 8, 64])
                        tt.matmul(pm[:, blk * 512:(blk + 1) * 512].rearrange("p (h q) -> p h q", h=8),
                                  gam_r[:, ccol:ccol + Q], rhs, start=True, stop=True)
            TT(_gbe)
            t_ge = mark("tensor")
            wait("scalar", "sT", t_ge)
            def _gee(s):
                s.copy(ge[:, 0:2048], pmA[:, 0:2048])
                s.copy(ge[:, 2048:4096], pmB[:, 0:2048])
            S(_gee)
            s_ge = mark("scalar")
            if debug and t == 0 and c == 0:
                wait("sync", "sS", s_ge)
                dma("sync", "sDbg", dbg["dbg_ge"], ge[:])
                wait("sync", "sDbg", sems["sDbg"][1])
            wait("tensor", "sS", s_ge)
            def _bbe(tt, ccol=ccol):
                for half in range(2):
                    pm = PM[half]
                    for blk in range(4):
                        h0 = half * 32 + blk * 8
                        rhs = i64r[:, h0:h0 + 8].unsqueeze(2).broadcast_to([64, 8, 64])
                        tt.matmul(pm[:, blk * 512:(blk + 1) * 512].rearrange("p (h q) -> p h q", h=8),
                                  bet_r[:, ccol:ccol + Q], rhs, start=True, stop=True)
            TT(_bbe)
            t_be = mark("tensor")
            wait("scalar", "sT", t_be)
            def _bee(s):
                s.copy(be[:, 0:2048], pmA[:, 0:2048])
                s.copy(be[:, 2048:4096], pmB[:, 0:2048])
            S(_bee)
            s_be = mark("scalar")
            wait("vector", "sS", s_be)
            if debug:
                wait("vector", "sDbg", sems["sDbg"][1])
            wtok("vector", sh2)
            def _X(v, xh_c=xh_c):
                v.tensor_mul(Xc[:], xh_c[:], ge[:])
                v.tensor_mul(Xd[:], xhs[:], be[:])
                v.tensor_add(Xc[:], Xc[:], Xd[:])
            V(_X)
            v_X = mark("vector")

            wait("vector", "sV", v_gb)
            def _acs(v, ccol=ccol):
                v.tensor_tensor_scan(acs_fm[:], a_fm_t[:, ccol:ccol + Q],
                                     zeros_t[0:64, 0:Q], 0.0, ALU.add, ALU.add)
                v.tensor_copy(acs_r[:], acs_fm[:])
                v.tensor_scalar(nacs_r[:], acs_fm[:], -1.0, None, ALU.mult)
            V(_acs)
            v_acs = mark("vector")
            wait("scalar", "sV", v_acs)
            def _acs2(s):
                s.activation(eacs_bf[:], acs_fm[:], AF.Exp)
            S(_acs2)
            s_acs2 = mark("scalar")
            wait("vector", "sS", s_acs2)
            def _vd2e(v):
                v.tensor_scalar(d2e_f[:], acs_fm[:], acs_fm[:, Q - 1:Q], -1.0,
                                ALU.subtract, ALU.mult)
                v.tensor_copy(d2e_r[:], d2e_f[:])
            V(_vd2e)
            v_d2e = mark("vector")
            wait("scalar", "sV", v_acs)
            S(lambda s: s.activation(e_col[:], acs_fm[:, Q - 1:Q], AF.Exp))
            s_erow = mark("scalar")
            wait("vector", "sS", s_erow)
            V(lambda v: v.tensor_scalar(exp_t[:], ones64[:], e_col[:, 0:1], None, ALU.mult))
            v_erow = mark("vector")
            wait("sync", "sV", v_erow)
            el_w = dma("sync", "sEw", scr_eline[0:1, :].rearrange("o (h p) -> o h p", h=64), exp_t[:])

            wait("sync", "sS", s_acs2)
            e2d_v = dma("sync", "sE2d", scr_line[:].rearrange("o (h q) -> o h q", h=64), eacs_bf[:])
            if debug and t == 0 and c == 0:
                wait("sync", "sV", v_X)
                m_prev["dbgX"] = dma("sync", "sDbg", dbg["dbg_X"], Xc[:])
                wait("sync", "sDbg", sems["sDbg"][1])
            wait("tensor", "sV", v_bcbf)
            wait("tensor", "sS", s_be)
            TT(lambda tt, ccol=ccol: tt.matmul(pmA[0:128, 0:128], bm_bf[:, ccol:ccol + Q],
                                               cm_bf[:, ccol:ccol + Q], start=True, stop=True))
            t_G = mark("tensor")
            wait("vector", "sT", t_G)
            V(lambda v: v.tensor_copy(gt_bf[:], pmA[0:128, 0:128]))
            v_G = mark("vector")

            mp = arena_bf[:, 4096:5120]
            m1a = arena_bf[:, 5120:6144]
            m1b = arena_bf[:, 6144:7168]
            grep = arena_bf[:, 0:1024]
            m2 = arena_bf[:, 1024:2048]
            cmrep = arena_bf[:, 2048:3072]
            earep = arena_bf[:, 3072:4096]
            ydone = []
            prev_evac = 0
            prev_m1 = {}
            prev_diff = {}
            prev_mk = {}
            for e8 in range(8):
                h0 = e8 * 8
                qpm = PM[e8 % 2]
                aslot = acsline[0:1, 0:1024]
                wait("sync", "sT", prev_diff.get(e8 - 1, 0))
                wait("sync", "sV", v_acs)
                acl_v = dma("sync", "sAcl", aslot.rearrange("o (h q) -> o h q", h=8),
                            acs_r[h0:h0 + 8, :])
                wtok("tensor", acl_v)
                wait("tensor", "sV", v_acs)
                wait("tensor", "sS", prev_m1.get(e8 - 2, 0))
                if e8 < 2:
                    wait("tensor", "sS", s_be)
                    wait("tensor", "sV", v_G)
                def _dmm(tt, h0=h0, qpm=qpm, aslot=aslot):
                    for half in range(2):
                        sl = qpm[:, half * 512:(half + 1) * 512]
                        h1 = h0 + half * 4
                        tt.matmul(sl, onesr[0:1, 0:128],
                                  aslot[0:1, half * 512:(half + 1) * 512],
                                  start=True, stop=False)
                        rhs = i64r[:, h1:h1 + 4].unsqueeze(2).broadcast_to([64, 4, 128])
                        tt.matmul(sl.rearrange("p (h q) -> p h q", h=4),
                                  nacs_r[:], rhs, start=False, stop=False)
                        rhs2 = i128r[:, :].unsqueeze(1).broadcast_to([128, 4, 128])
                        tt.matmul(sl.rearrange("p (h q) -> p h q", h=4),
                                  trinr[:], rhs2, start=False, stop=True)
                TT(_dmm)
                t_diff = mark("tensor")
                prev_diff[e8] = t_diff
                wait("sync", "sV", v_X if e8 == 0 else 0)
                wait("sync", "sV", prev_mk.get(e8 - 1, 0))
                wait("sync", "sV", v_G)
                g_v = dma("sync", "sRep", grep[:].rearrange("p (h q) -> p h q", h=8),
                          gt_bf[:].unsqueeze(1).broadcast_to([128, 8, 128]))
                c_v = dma("sync", "sRep", cmrep[0:64, :].rearrange("p (h q) -> p h q", h=8),
                          cm_bf[0:64, ccol:ccol + Q].unsqueeze(1).broadcast_to([64, 8, 128]))
                wtok("sync", e2d_v)
                e_v = dma("sync", "sRep", earep[0:64, :],
                          scr_line[0:1, h0 * 128:(h0 + 8) * 128].broadcast_to([64, 1024]))
                wait("scalar", "sT", t_diff)
                S(lambda s, qpm=qpm: s.activation(mp[:], qpm[:, 0:1024], AF.Exp))
                prev_m1[e8] = mark("scalar")
                m1 = (m1a, m1b)[e8 % 2]
                wait("vector", "sS", prev_m1[e8])
                if debug and e8 == 2:
                    wait("vector", "sDbg", sems["sDbg"][1])
                wtok("vector", e_v)
                def _mk(v, m1=m1):
                    v.tensor_mul(m1[:], mp[:], grep[:])
                    v.tensor_mul(m2[0:64, :], earep[0:64, :], cmrep[0:64, :])
                V(_mk)
                v_m1 = mark("vector")
                prev_mk[e8] = v_m1
                if debug and t == 0 and c == 0 and e8 == 0:
                    wait("sync", "sV", v_m1)
                    dma("sync", "sDbg", dbg["dbg_mask1"], m1[:])
                    wait("sync", "sDbg", sems["sDbg"][1])
                    wait("sync", "sV", v_G)
                    m_prev["dbgm1"] = dma("sync", "sDbg", dbg["dbg_gt"], gt_bf[:])
                    wait("sync", "sDbg", sems["sDbg"][1])
                ypm = PM[(e8 + 1) % 2]
                wait("tensor", "sV", v_m1)
                wait("tensor", "sV", v_X)
                if e8 >= 2:
                    wait("tensor", "sV", ydone[e8 - 2])
                def _ymm(tt, m1=m1, ypm=ypm, h0=h0, c=c):
                    for j in range(8):
                        h = h0 + j
                        tt.matmul(ypm[:, 1024 + j * 64: 1024 + (j + 1) * 64],
                                  m1[:, j * 128:(j + 1) * 128],
                                  Xc[:, h * 64:(h + 1) * 64], start=True, stop=False)
                        tt.matmul(ypm[:, 1024 + j * 64: 1024 + (j + 1) * 64],
                                  m2[0:64, j * 128:(j + 1) * 128],
                                  carry[0:64, h * 64:(h + 1) * 64], start=False, stop=True)
                TT(_ymm)
                t_y = mark("tensor")
                wait("vector", "sT", t_y)
                def _yev(v, ypm=ypm, h0=h0, c=c):
                    v.tensor_copy(y_tok[:, c * DI + h0 * 64: c * DI + (h0 + 8) * 64],
                                  ypm[:, 1024:1536])
                V(_yev)
                ydone.append(mark("vector"))
                prev_evac = ydone[-1]
            y_chunk_done = ydone[-1]

            d2ex = arena2_bf[:, 0:4096]
            wait("tensor", "sV", y_chunk_done)
            def _d2mm(tt):
                for half in range(2):
                    pm = PM[half]
                    for blk in range(4):
                        h0 = half * 32 + blk * 8
                        rhs = i64r[:, h0:h0 + 8].unsqueeze(2).broadcast_to([64, 8, 64])
                        tt.matmul(pm[:, blk * 512:(blk + 1) * 512].rearrange("p (h q) -> p h q", h=8),
                                  d2e_r[:], rhs, start=True, stop=True)
            TT(_d2mm)
            t_d2 = mark("tensor")
            wait("scalar", "sT", t_d2)
            def _d2ev(s):
                s.activation(d2ex[:, 0:2048], pmA[:, 0:2048], AF.Exp)
                s.activation(d2ex[:, 2048:4096], pmB[:, 0:2048], AF.Exp)
            S(_d2ev)
            s_d2 = mark("scalar")
            wait("vector", "sS", s_d2)
            V(lambda v: v.tensor_mul(Xd[:], Xc[:], d2ex[:]))
            v_Xd = mark("vector")
            wtok("tensor", btm_v)
            prev_cup = 0
            for half in range(2):
                hw0 = half * 2048
                wtok("sync", el_w)
                wait("sync", "sV", prev_cup)
                el_v = dma("sync", "sEv", e_sb[:], scr_eline[0:1, hw0:hw0 + 2048].broadcast_to([64, 2048]))
                wait("tensor", "sV", v_Xd)
                wait("tensor", "sV", prev_cup)
                def _sthalf(tt, hw0=hw0, c=c):
                    for blk in range(4):
                        tt.matmul(pmA[0:64, blk * 512:(blk + 1) * 512],
                                  b_tm[:, c * 64:(c + 1) * 64],
                                  Xd[:, hw0 + blk * 512: hw0 + (blk + 1) * 512],
                                  start=True, stop=True)
                TT(_sthalf)
                t_sth = mark("tensor")
                wait("vector", "sT", t_sth)
                wtok("vector", el_v)
                def _cup(v, hw0=hw0):
                    cu = arena3_bf[0:64, 4096 + hw0: 4096 + hw0 + 2048]
                    v.tensor_mul(cu, carry[0:64, hw0:hw0 + 2048], e_sb[:])
                    v.tensor_add(carry[0:64, hw0:hw0 + 2048], cu, pmA[0:64, 0:2048])
                V(_cup)
                prev_cup = mark("vector")
            v_carry = prev_cup
            prev_chunk_V = v_carry

        wait("sync", "sV", v_carry)
        xhl_v = dma("sync", "sXl", scr_xlast[0:1, :], xh_tm[127:128, (NCH - 1) * DI:NCH * DI])
        m_prev["xl"] = xhl_v

        dbg_ytok_tok = None
        if debug and t == 0:
            wait("sync", "sV", v_dterm) if False else None
            dbg_ytok_tok = dma("sync", "sDbg", dbg["dbg_ytok"], y_tok[:])
            wait("sync", "sDbg", sems["sDbg"][1])

        wait("sync", "sV", v_carry)
        if t == 0:
            wtok("sync", drep_v)
        wtok("sync", m_prev["drt"])
        drep_t = dma("sync", "sDrT", arena2_bf[:, 0:4096], scr_dline[0:1, :].broadcast_to([128, DI]))
        m_prev["drt"] = drep_t
        wtok("vector", drep_t)
        wtok("vector", dbg_ytok_tok)
        def _dterm(v):
            for c in range(NCH):
                v.tensor_mul(arena3_bf[:, 0:4096], xh_tm[:, c * DI:(c + 1) * DI],
                             arena2_bf[:, 0:4096])
                v.tensor_add(y_tok[:, c * DI:(c + 1) * DI], y_tok[:, c * DI:(c + 1) * DI],
                             arena3_bf[:, 0:4096])
        V(_dterm)
        v_dterm = mark("vector")

        y_fm = arena2_bf[:, 0:8192]
        wait("sync", "sV", v_dterm)
        ytr = None
        for kk in range(32):
            for c in range(NCH):
                ytr = dma("sync", "sTrY", y_fm[:, kk * T + c * 128: kk * T + c * 128 + 128],
                          y_tok[:, c * DI + kk * 128: c * DI + (kk + 1) * 128], transpose=True)
        wtok("vector", ytr)
        def _gate(v):
            v.tensor_mul(y_tok[:, 0:8192], y_fm[:], z_fm[:])
            v.tensor_mul(arena3_bf[:, 0:8192], y_tok[:, 0:8192], y_tok[:, 0:8192])
        V(_gate)
        v_gate = mark("vector")
        g_fm = y_tok[:, 0:8192]
        sq_fm = arena3_bf[:, 0:8192]
        wait("tensor", "sV", v_gate)
        def _ssq(tt):
            for kk in range(32):
                tt.matmul(pmA[0:1, 0:T], onesb[0:128, 0:1], sq_fm[:, kk * T:(kk + 1) * T],
                          start=(kk == 0), stop=(kk == 31))
        TT(_ssq)
        t_ssq = mark("tensor")
        wait("scalar", "sT", t_ssq)
        S(lambda s: s.copy(srow2[:], pmA[0:1, 0:T]))
        s_ssq = mark("scalar")
        wait("sync", "sS", s_ssq)
        r1 = dma("sync", "sRs", scr_row[0:1, 0:T], srow2[:])
        wtok("sync", r1)
        r2 = dma("sync", "sRs", rstd_c[:, 0:2], scr_row[0:1, 0:T].rearrange("o (c p) -> p (o c)", p=128), slow=True)
        wtok("scalar", r2)
        def _rstd(s):
            s.activation(rstd_c[:, 0:2], rstd_c[:, 0:2], AF.Ln, scale=float(1.0 / DI),
                         bias=rmseps[:, 0:1])
            s.activation(rstd_c[:, 0:2], rstd_c[:, 0:2], AF.Exp, scale=-0.5)
        S(_rstd)
        s_rstd = mark("scalar")

        attnout = arena3_bf[:, 0:4096]
        step2 = 0
        wo_mark = {}
        wo_evac = {}
        for g in range(4):
            pm = PM[g % 2]
            for k in range(32):
                b = step2 % 4
                wb = (wbufA, wbufB, wbufC, wbufD)[b]
                wait("sync", "sT", wo_mark.get(step2 - 4, 0))
                if step2 < 4:
                    wait("sync", "sT", t_ssq)
                wl_v = dma("sync", ("sWa", "sWb", "sWc", "sWd")[b], wb[:, 0:512],
                           scr_woutT[k * 128:(k + 1) * 128, g * 512:(g + 1) * 512])
                wtok("tensor", wl_v)
                if k == 0:
                    wait("tensor", "sV", v_gate)
                    wait("tensor", "sS", s_ssq)
                    wait("tensor", "sS", wo_evac.get(g - 2, 0))
                def _wmm(tt, pm=pm, wb=wb, k=k):
                    for mc in range(4):
                        tt.matmul(pm[0:128, mc * 512:mc * 512 + T], wb[:, mc * 128:(mc + 1) * 128],
                                  g_fm[:, k * T:(k + 1) * T], start=(k == 0), stop=(k == 31))
                TT(_wmm)
                wo_mark[step2] = mark("tensor")
                step2 += 1
            wait("scalar", "sT", wo_mark[step2 - 1])
            S(lambda s, pm=pm, g=g: s.copy(
                attnout[:, g * 1024:(g + 1) * 1024].rearrange("p (a b) -> p a b", a=4),
                pm[:, 0:2048].rearrange("p (a b) -> p a b", a=4)[:, :, 0:T]))
            wo_evac[g] = mark("scalar")

        attn_tok = arena2_bf[:, 0:4096]
        wait("sync", "sS", wo_evac[3])
        atr = None
        for kk in range(16):
            for c in range(2):
                atr = dma("sync", "sTrA", attn_tok[:, c * 2048 + kk * 128: c * 2048 + (kk + 1) * 128],
                          attnout[:, kk * T + c * 128: kk * T + c * 128 + 128], transpose=True)
        wtok("vector", atr)
        wait("vector", "sS", s_rstd)
        def _res1(v):
            for c in range(2):
                v.scalar_tensor_tensor(x_tok[:, c * DIM:(c + 1) * DIM],
                                       attn_tok[:, c * 2048:(c + 1) * 2048],
                                       rstd_c[:, c:c + 1], x_tok[:, c * DIM:(c + 1) * DIM],
                                       ALU.mult, ALU.add)
        V(_res1)
        v_x1 = mark("vector")

        if debug and t == 0:
            wait("sync", "sV", v_x1)
            m_prev["dbgx1"] = dma("sync", "sDbg", dbg["dbg_x1"], x_tok[:])
            wait("sync", "sDbg", sems["sDbg"][1])

        ln2_tm = arena3_bf[:, 4096:8192]
        wait("vector", "sV", v_x1)
        def _l2a(v):
            for c in range(2):
                v.reduce_sum(mcol[:, c:c + 1], x_tok[:, c * DIM:(c + 1) * DIM], axis=AXX)
            v.tensor_scalar_mul(mcol[:, 0:2], mcol[:, 0:2], float(1.0 / DIM))
        V(_l2a)
        v_l2a = mark("vector")
        wait("scalar", "sV", v_l2a)
        def _l2b(s):
            for c in range(2):
                s.activation(arena2_bf[:, 4096 + c * DIM: 4096 + (c + 1) * DIM],
                             x_tok[:, c * DIM:(c + 1) * DIM], AF.Square)
        S(_l2b)
        s_l2b = mark("scalar")
        wait("vector", "sS", s_l2b)
        def _l2c(v):
            for c in range(2):
                v.reduce_sum(mcol[:, 2 + c:3 + c], arena2_bf[:, 4096 + c * DIM:4096 + (c + 1) * DIM],
                             axis=AXX)
            v.tensor_scalar_mul(mcol[:, 2:4], mcol[:, 2:4], float(1.0 / DIM))
            v.tensor_mul(rstd_c[:, 0:2], mcol[:, 0:2], mcol[:, 0:2])
            v.tensor_sub(mcol[:, 2:4], mcol[:, 2:4], rstd_c[:, 0:2])
        V(_l2c)
        v_l2c = mark("vector")
        wait("scalar", "sV", v_l2c)
        def _l2d(s):
            s.activation(mcol[:, 2:4], mcol[:, 2:4], AF.Ln, bias=epscol[:, 0:1])
            s.activation(mcol[:, 2:4], mcol[:, 2:4], AF.Exp, scale=-0.5)
        S(_l2d)
        s_l2d = mark("scalar")
        wait("vector", "sS", s_l2d)
        def _l2e(v):
            for c in range(2):
                v.tensor_scalar(ln2_tm[:, c * DIM:(c + 1) * DIM], x_tok[:, c * DIM:(c + 1) * DIM],
                                mcol[:, c:c + 1], mcol[:, 2 + c:3 + c], ALU.subtract, ALU.mult)
        V(_l2e)
        v_l2e = mark("vector")
        wait("sync", "sV", v_l2e)
        l2tr = None
        for c in range(2):
            for k in range(16):
                l2tr = dma("sync", "sTr1", arena2_bf[:, k * T + c * 128: k * T + c * 128 + 128],
                           ln2_tm[:, c * DIM + k * 128:c * DIM + (k + 1) * 128], transpose=True)
        wtok("vector", l2tr)
        V(lambda v: v.tensor_copy(ln2_f8[:], arena2_bf[:, 0:4096]))
        v_ln2f8 = mark("vector")

        gel = arena3_bf[:, 0:1024]
        ff1b = arena3_bf[:, 1024:2048]
        step3 = 0
        f_mark = {}
        f_evac = {}
        for g in range(16):
            for j, (scr, pm) in enumerate(((scr_wg0, pmA), (scr_wg1, pmB))):
                for k in range(16):
                    b = step3 % 3
                    wb = (w8A, w8B, w8C)[b]
                    wait("sync", "sT", f_mark.get(step3 - 3, 0))
                    if step3 < 3:
                        wait("sync", "sV", v_ln2f8)
                    wv = dma("sync", ("sWa", "sWb", "sWc")[b], wb[:, 0:512],
                             scr[k * 128:(k + 1) * 128, g * 512:(g + 1) * 512])
                    wtok("tensor", wv)
                    if k == 0:
                        wait("tensor", "sV", v_ln2f8)
                        wait("tensor", "sS", f_evac.get((g - 1, j), 0))
                        wait("tensor", "sV", f_evac.get((g - 1, "a"), 0))
                    def _fmm(tt, pm=pm, wb=wb, k=k):
                        for mc in range(4):
                            tt.matmul(pm[0:128, mc * 512:mc * 512 + T], wb[:, mc * 128:(mc + 1) * 128],
                                      ln2_f8[:, k * T:(k + 1) * T], start=(k == 0),
                                      stop=(k == 15))
                    TT(_fmm)
                    f_mark[step3] = mark("tensor")
                    step3 += 1
            wait("scalar", "sT", f_mark[step3 - 1])
            def _fe(s, g=g):
                s.activation(gel[:].rearrange("p (a b) -> p a b", a=4),
                             pmA[:, 0:2048].rearrange("p (a b) -> p a b", a=4)[:, :, 0:T],
                             AF.Gelu_apprx_tanh)
                s.copy(ff1b[:].rearrange("p (a b) -> p a b", a=4),
                       pmB[:, 0:2048].rearrange("p (a b) -> p a b", a=4)[:, :, 0:T])
            S(_fe)
            f_evac[(g, 0)] = f_evac[(g, 1)] = mark("scalar")
            wait("vector", "sS", f_evac[(g, 0)])
            def _fa(v, g=g):
                v.tensor_mul(a8[:, g * 1024:(g + 1) * 1024], gel[:], ff1b[:])
            V(_fa)
            f_evac[(g, "a")] = mark("vector")

        ffnout = arena3_bf[:, 4096:8192]
        step4 = 0
        w3_mark = {}
        w3_evac = {}
        for g in range(4):
            pm = PM[g % 2]
            for k in range(64):
                b = step4 % 3
                wb = (w8A, w8B, w8C)[b]
                wait("sync", "sT", w3_mark.get(step4 - 3, 0))
                if step4 < 3:
                    wait("sync", "sS", f_evac[(15, 0)])
                    wait("sync", "sV", f_evac[(15, "a")])
                wv = dma("sync", ("sWa", "sWb", "sWc")[b], wb[:, 0:512],
                         scr_wl[k * 128:(k + 1) * 128, g * 512:(g + 1) * 512])
                wtok("tensor", wv)
                if k == 0:
                    wait("tensor", "sS", f_evac[(15, 0)])
                    wait("tensor", "sV", f_evac[(15, "a")])
                    wait("tensor", "sS", w3_evac.get(g - 2, 0))
                def _f3(tt, pm=pm, wb=wb, k=k):
                    for mc in range(4):
                        tt.matmul(pm[0:128, mc * 512:mc * 512 + T], wb[:, mc * 128:(mc + 1) * 128],
                                  a8[:, k * T:(k + 1) * T], start=(k == 0),
                                  stop=(k == 63))
                TT(_f3)
                w3_mark[step4] = mark("tensor")
                step4 += 1
            wait("scalar", "sT", w3_mark[step4 - 1])
            S(lambda s, pm=pm, g=g: s.copy(
                ffnout[:, g * 1024:(g + 1) * 1024].rearrange("p (a b) -> p a b", a=4),
                pm[:, 0:2048].rearrange("p (a b) -> p a b", a=4)[:, :, 0:T]))
            w3_evac[g] = mark("scalar")

        wait("sync", "sS", w3_evac[3])
        ftr = None
        for kk in range(16):
            for c in range(2):
                ftr = dma("sync", "sTrA", arena2_bf[:, c * 2048 + kk * 128: c * 2048 + (kk + 1) * 128],
                          ffnout[:, kk * T + c * 128: kk * T + c * 128 + 128], transpose=True)
        wtok("vector", ftr)
        wtok("vector", m_prev["out_st"])
        V(lambda v: v.tensor_add(arena_f32[:, 0:2048], x_tok[:, 0:DIM],
                                 arena2_bf[:, 0:2048]))
        v_outa = mark("vector")
        wait("sync", "sV", v_outa)
        o1 = dma("sync", "sOut", y_d[r0:r0 + 128, :], arena_f32[:, 0:2048])
        wtok("vector", o1)
        V(lambda v: v.tensor_add(arena_f32[:, 0:2048], x_tok[:, DIM:2 * DIM],
                                 arena2_bf[:, 2048:4096]))
        v_out = mark("vector")
        wait("sync", "sV", v_out)
        o2 = dma("sync", "sOut", y_d[r0 + 128:r0 + 256, :], arena_f32[:, 0:2048])
        m_prev["out_st"] = o2
        m_prev["tile_done_V"] = v_out
        m_prev["tile_done_S"] = w3_evac[3]
        m_prev["tile_done_T"] = w3_mark[step4 - 1]

    for s in DMA_SEMS:
        if sems[s][1] > 0:
            wait("sync", s, sems[s][1])

    with nc.Block() as block:
        @block.sync
        def _(sync):
            for f in thunks["sync"]:
                f(sync)

        @block.gpsimd
        def _(g):
            for f in thunks["gpsimd"]:
                f(g)

        @block.vector
        def _(v):
            vp = DrainAfter(v)
            for f in thunks["vector"]:
                f(vp)

        @block.scalar
        def _(s):
            sp = DrainAfter(s)
            for f in thunks["scalar"]:
                f(sp)

        @block.tensor
        def _(tt):
            tp_ = DrainAfter(tt)
            for f in thunks["tensor"]:
                f(tp_)

    es.close()
    return nc


def make_inmaps(inputs):
    x = np.ascontiguousarray(inputs["L0_attn_x"], dtype=np.float32)
    cm = {}
    cm["w_in"] = np.ascontiguousarray(inputs["L0_attn_W_in"], np.float32)
    cm["w_lambda"] = np.ascontiguousarray(inputs["L0_attn_W_lambda"], np.float32)
    cm["w_theta"] = np.ascontiguousarray(inputs["L0_attn_W_theta"], np.float32)
    cm["w_out"] = np.ascontiguousarray(inputs["L0_attn_W_out"], np.float32)
    wg = np.ascontiguousarray(inputs["L0_ffn_w_gating"], np.float32)
    cm["wg0"] = np.ascontiguousarray(wg[0])
    cm["wg1"] = np.ascontiguousarray(wg[1])
    cm["wl"] = np.ascontiguousarray(inputs["L0_ffn_w_linear"], np.float32)
    cm["bbias_c"] = np.ascontiguousarray(np.asarray(inputs["L0_attn_B_bias"], np.float32).reshape(64, 1))
    cm["cbias_c"] = np.ascontiguousarray(np.asarray(inputs["L0_attn_C_bias"], np.float32).reshape(64, 1))
    cm["alog_c"] = np.ascontiguousarray(np.asarray(inputs["L0_attn_A_log"], np.float32).reshape(64, 1))
    cm["d_c"] = np.ascontiguousarray(np.asarray(inputs["L0_attn_D_param"], np.float32).reshape(64, 1))
    cm["dtb_c"] = np.ascontiguousarray(np.asarray(inputs["L0_attn_dt_bias"], np.float32).reshape(64, 1))
    cm["i64"] = np.eye(64, dtype=np.float32)
    cm["i128"] = np.eye(128, dtype=np.float32)
    tri = np.zeros((128, 128), np.float32)
    for lp in range(128):
        tri[lp, lp + 1:] = BIGNEG
    cm["trineg"] = tri
    maps = []
    for b in range(8):
        m = dict(cm)
        m["x"] = np.ascontiguousarray(x[b])
        maps.append(m)
    return maps


def kernel(**inputs):
    nc = build_program(debug=False)
    in_maps = make_inmaps(inputs)
    res = run_bass_kernel_spmd(nc, in_maps, core_ids=list(range(8)))
    out = np.stack([np.asarray(r["y"], np.float32) for r in res.results], axis=0)
    return out




# version 05
# speedup vs baseline: 1.9692x; 1.2406x over previous
import numpy as np
import ml_dtypes
import concourse.bass as bass
import concourse.mybir as mybir
from concourse.bass_utils import run_bass_kernel_spmd

dt = mybir.dt
AF = mybir.ActivationFunctionType
ALU = mybir.AluOpType
AXX = mybir.AxisListType.X
ML_BF16 = ml_dtypes.bfloat16

DIM = 2048
DI = 4096
NH = 64
HD = 64
HID = 8192
NPROJ = 8480
T = 256
NT = 2048 // T
Q = 128
NCH = T // Q
BIGNEG = -1.0e30
RMS_EPS = 1.1920929e-07
LN_EPS = 1e-5
INV2PI = float(1.0 / (2.0 * np.pi))
TWOPI = float(2.0 * np.pi)
GCONST = float(np.sqrt(2.0 / np.pi))
GC2 = 0.044715

F32, F32R, BF, FP8, I32 = dt.float32, dt.float32r, dt.bfloat16, dt.float8e4, dt.int32


def build_program(debug=False):
    nc = bass.Bass("TRN2", target_bir_lowering=False)

    def din(name, shape, dtype=F32):
        return nc.dram_tensor(name, shape, dtype, kind="ExternalInput").ap()

    x_d = din("x", [2048, DIM])
    w_in_d = din("w_in", [8384, DIM])
    w_lam_d = din("w_lambda", [64, DIM])
    w_th_d = din("w_theta", [32, DIM])
    w_out_d = din("w_out", [DIM, DI])
    wg0_d = din("wg0", [DIM, HID])
    wg1_d = din("wg1", [DIM, HID])
    wl_d = din("wl", [HID, DIM])
    bbias_d = din("bbias_c", [64, 1])
    cbias_d = din("cbias_c", [64, 1])
    alog_d = din("alog_c", [64, 1])
    dpar_d = din("d_c", [64, 1])
    dtb_d = din("dtb_c", [64, 1])
    i64_d = din("i64", [64, 64])
    i128_d = din("i128", [128, 128])
    trineg_d = din("trineg", [128, 128])

    y_d = nc.dram_tensor("y", [2048, DIM], F32, kind="ExternalOutput").ap()

    scr_wallT = nc.dram_tensor("scr_wallT", [DIM, NPROJ], BF, kind="Internal").ap()
    scr_woutT = nc.dram_tensor("scr_woutT", [DI, DIM], BF, kind="Internal").ap()
    scr_wg0 = nc.dram_tensor("scr_wg0", [DIM, HID], FP8, kind="Internal").ap()
    scr_wg1 = nc.dram_tensor("scr_wg1", [DIM, HID], FP8, kind="Internal").ap()
    scr_wl = nc.dram_tensor("scr_wl", [HID, DIM], FP8, kind="Internal").ap()
    scr_line = nc.dram_tensor("scr_line", [1, NH * Q], BF, kind="Internal").ap()
    scr_eline = nc.dram_tensor("scr_eline", [1, DI], BF, kind="Internal").ap()
    scr_xlast = nc.dram_tensor("scr_xlast", [1, DI], BF, kind="Internal").ap()
    scr_row = nc.dram_tensor("scr_row", [1, 256], F32, kind="Internal").ap()

    dbg = {}
    def dout(name, shape, dtype):
        dbg[name] = nc.dram_tensor(name, shape, dtype, kind="ExternalOutput").ap()
        return dbg[name]
    if debug:
        dout("dbg_ln1fm", [128, 16 * T], BF)
        dout("dbg_zfm", [128, 32 * T], BF)
        dout("dbg_bc", [128, T], F32)
        dout("dbg_dt", [64, T], F32)
        dout("dbg_ang", [32, T], F32)
        dout("dbg_bro", [64, T], F32)
        dout("dbg_mask1", [128, 1024], BF)
        dout("dbg_X", [128, 4096], BF)
        dout("dbg_gt", [128, 128], BF)
        dout("dbg_cro", [64, T], F32)
        dout("dbg_gb", [64, 2 * T], F32)
        dout("dbg_ge", [128, 4096], BF)
        dout("dbg_ytok", [128, NCH * DI], BF)
        dout("dbg_x1", [128, 2 * DIM], F32)
        dout("dbg_a8", [128, 8 * T], FP8)

    from contextlib import ExitStack
    es = ExitStack()
    sbt = lambda name, shape, dtype: es.enter_context(nc.sbuf_tensor(name, shape, dtype))
    pst = lambda name, shape: es.enter_context(nc.psum_tensor(name, shape, F32))

    x_tok = sbt("x_tok", [128, 2 * DIM], F32)
    ln1_fm = sbt("ln1_fm", [128, 16 * T], BF)
    z_fm = sbt("z_fm", [128, 32 * T], BF)
    xh_tm = sbt("xh_tm", [128, NCH * DI], BF)
    y_tok = sbt("y_tok", [128, NCH * DI], BF)
    carry = sbt("carry", [64, NH * HD], BF)
    arena_bf = sbt("arena_bf", [128, 8192], BF)
    arena2_bf = sbt("arena2_bf", [128, 8192], BF)
    arena3_bf = sbt("arena3_bf", [128, 8192], BF)
    arena_f32 = sbt("arena_f32", [128, 2048], F32)
    arena_i32 = sbt("arena_i32", [128, 2048], I32)
    wbufA = sbt("wbufA", [128, 512], BF)
    wbufB = sbt("wbufB", [128, 512], BF)
    wbufC = sbt("wbufC", [128, 512], BF)
    wbufD = sbt("wbufD", [128, 512], BF)
    w8A = sbt("w8A", [128, 2048], FP8)
    w8B = sbt("w8B", [128, 2048], FP8)
    w8C = sbt("w8C", [128, 512], FP8)
    a8 = sbt("a8", [128, 64 * T], FP8)
    ln2_f8 = sbt("ln2_f8", [128, 16 * T], FP8)
    trig = sbt("trig", [64, 2 * T], BF)
    zeros_t = sbt("zeros_t", [64, 256], F32)
    bc_t = sbt("bc_t", [128, T], F32)
    dtlam = sbt("dtlam", [128, T], F32)
    s128a = sbt("s128a", [128, T], F32)
    dt_t = sbt("dt_t", [64, T], F32)
    dt_r = sbt("dt_r", [64, T], F32R)
    lam_t = sbt("lam_t", [64, T], F32)
    a_fm_t = sbt("a_fm_t", [64, T], F32)
    gam_r = sbt("gam_r", [64, T], F32R)
    bet_r = sbt("bet_r", [64, T], F32R)
    s64a = sbt("s64a", [64, T], F32)
    s64b = sbt("s64b", [64, T], F32)
    s64c = sbt("s64c", [64, T], F32)
    craw = s64c[0:64, :]
    w1 = s64a[0:32, :]
    w2 = s64b[0:32, :]
    bmn = sbt("bmn", [64, T], F32)
    cmn = sbt("cmn", [64, T], F32)
    bm_bf = sbt("bm_bf", [64, T], BF)
    cm_bf = sbt("cm_bf", [64, T], BF)
    b_tm = sbt("b_tm", [128, NCH * 64], BF)
    theta_t = sbt("theta_t", [32, T], F32)
    ang_t = sbt("ang_t", [32, T], F32)
    ang_carry = sbt("ang_carry", [32, 1], F32)
    ang_i = arena_i32[0:32, 0:T]
    acs_fm = sbt("acs_fm", [64, Q], F32)
    acs_r = sbt("acs_r", [64, Q], F32R)
    nacs_r = sbt("nacs_r", [64, Q], F32R)
    acsline = sbt("acsline", [1, 1024], F32R)
    eacs_bf = sbt("eacs_bf", [64, Q], BF)
    d2e_f = sbt("d2e_f", [64, Q], F32)
    d2e_r = sbt("d2e_r", [64, Q], F32R)
    gt_bf = sbt("gt_bf", [128, 128], BF)
    e_sb = sbt("e_sb", [64, 2048], BF)
    row_f = sbt("row_f", [1, T], F32)
    row_r = sbt("row_r", [1, T], F32R)
    dtavg_row = row_f
    dtavg_r = row_r
    srow2 = row_f
    srow2_r = row_r
    rstd_c = sbt("rstd_c", [128, 2], F32)
    mcol = sbt("mcol", [128, 4], F32)
    i64c = sbt("i64c_s", [64, 64], F32)
    i64r = sbt("i64r", [64, 64], F32R)
    i128r = sbt("i128r", [128, 128], F32R)
    trinr = sbt("trinr", [128, 128], F32R)
    onesf = sbt("onesf_s", [128, 1], F32)
    ones64 = sbt("ones64", [64, 64], F32)
    exp_t = sbt("exp_t", [64, 64], BF)
    e_col = sbt("e_col", [64, 1], F32)
    onesr = sbt("onesr", [128, 128], F32R)
    onesb = sbt("onesb", [128, 2], BF)
    identb = sbt("identb", [128, 128], BF)
    smalls_dc = sbt("smalls_dc", [64, 1], F32)
    bbias = sbt("bbias", [64, 1], F32)
    cbias = sbt("cbias", [64, 1], F32)
    dtb = sbt("dtb", [64, 1], F32)
    alog_s = sbt("alog_s", [64, 1], F32)
    mexpA = sbt("mexpA", [64, 1], F32)
    epscol = sbt("epscol", [128, 1], F32)
    rmseps = sbt("rmseps", [128, 1], F32)
    halfpi = sbt("halfpi", [128, 1], F32)

    pmA = pst("pmA", [128, 2048])
    pmB = pst("pmB", [128, 2048])
    PM = [pmA, pmB]

    thunks = {e: [] for e in ("sync", "gpsimd", "vector", "scalar", "tensor")}
    sems = {}
    DMA_SEMS = ("sInit", "sDl", "sDrT", "sPa", "sPb", "sPstA", "sPstB", "sG8a", "sG8b",
                "sP8a", "sP8b", "sX", "sTr1", "sTrXh", "sTrB", "sTrY", "sTrA",
                "sWa", "sWb", "sWc", "sWd", "sTq", "sLam", "sCm", "sTrg", "sRp1", "sRp2", "sSh", "sEw",
                "sEv", "sE2d", "sRep", "sAcl", "sXl", "sRs", "sOut", "sDbg")
    for s in ("sT", "sV", "sS") + DMA_SEMS:
        sems[s] = [es.enter_context(nc.semaphore(s)), 0]

    def on(eng, fn):
        thunks[eng].append(fn)

    def wait(eng, sname, value):
        if value <= 0:
            return
        h = sems[sname][0]
        on(eng, lambda e, h=h, v=value: e.wait_ge(h, v))

    state = {"last_T": None, "last_SS": None}

    def _is_sbuf(ap):
        try:
            return str(ap.space) in ("MemorySpace.SBUF", "SBUF") or "SB" in str(ap.space)
        except Exception:
            return False

    def dma(eng, sname, out, in_, transpose=False, slow=False):
        h = sems[sname][0]
        if eng == "sync":
            ss = (not transpose) and _is_sbuf(out) and _is_sbuf(in_)
            if transpose and state["last_SS"]:
                sn = state["last_SS"][0]
                wait(eng, sn, sems[sn][1])
            if ss and state["last_T"]:
                sn = state["last_T"][0]
                wait(eng, sn, sems[sn][1])
        sems[sname][1] += 16
        v = sems[sname][1]
        on(eng, lambda e, o=out, i=in_, t=transpose, h=h, sl=slow: e.dma_start(
            o, i, transpose=t, allow_slow_non_contiguous=sl).then_inc(h, 16))
        tok = (sname, v)
        if eng == "sync":
            if transpose:
                state["last_T"] = tok
            elif _is_sbuf(out) and _is_sbuf(in_):
                state["last_SS"] = tok
        return tok

    def wtok(eng, tok):
        if tok is None or tok == 0:
            return
        sname, v = tok
        wait(eng, sname, v)

    ENG_SEM = {"tensor": "sT", "vector": "sV", "scalar": "sS", "gpsimd": "sG_nop"}

    DATAPATH = {"tensor_copy", "tensor_tensor", "tensor_add", "tensor_sub", "tensor_mul",
                "tensor_max", "tensor_scalar", "tensor_single_scalar", "tensor_scalar_add",
                "tensor_scalar_sub", "tensor_scalar_mul", "tensor_scalar_max",
                "tensor_scalar_min", "scalar_tensor_tensor", "tensor_reduce", "reduce_max",
                "reduce_sum", "reciprocal", "memset", "memzero", "tensor_tensor_scan",
                "select", "activation", "copy", "mul", "add", "square", "sqrt", "sign",
                "tensor_relu", "transpose", "matmul", "ldweights"}

    class DrainAfter:

        def __init__(self, e):
            self._e = e
            self._W = []
            self._R = []
            self.last = None

        @staticmethod
        def _fp(a):
            t = getattr(a, "tensor", None)
            nm = getattr(t, "name", None)
            if nm is None:
                return None
            try:
                off = a.offset
                dims = [(d[0], d[1]) for d in a.ap]
                rowlen = int(t.shape[-1]) if len(t.shape) > 1 else 1
                st0, sz0 = dims[0]
                p_lo = off // rowlen
                p_step = max(st0 // rowlen, 1)
                p_hi = p_lo + (sz0 - 1) * p_step + 1
                f_lo = off % rowlen
                f_hi = f_lo + sum((sz - 1) * st for st, sz in dims[1:]) + 1
                return (nm, p_lo, p_hi, f_lo, f_hi)
            except Exception:
                return (nm, 0, 1 << 30, 0, 1 << 30)

        @staticmethod
        def _ov(a, b):
            return (a[0] == b[0] and a[1] < b[2] and b[1] < a[2]
                    and a[3] < b[4] and b[3] < a[4])

        def _conflict(self, reads, writes):
            for w in self._W:
                for x in reads:
                    if self._ov(w, x):
                        return True
                for x in writes:
                    if self._ov(w, x):
                        return True
            for r in self._R:
                for x in writes:
                    if self._ov(r, x):
                        return True
            return False

        def __getattr__(self, n):
            a = getattr(self._e, n)
            if n in DATAPATH:
                if n in ("matmul", "ldweights", "transpose"):
                    def wm(*args, **kw):
                        r = a(*args, **kw)
                        self.last = r
                        return r
                    return wm
                def w(*args, **kw):
                    fps = [self._fp(x) for x in list(args) + list(kw.values())]
                    fps = [(i, f) for i, f in enumerate(fps) if f is not None]
                    writes = [f for i, f in fps if i == 0]
                    reads = [f for i, f in fps if i > 0]
                    if self._conflict(reads, writes):
                        self._e.drain()
                        self._W, self._R = [], []
                    r = a(*args, **kw)
                    self._W += writes
                    self._R += reads
                    self.last = r
                    return r
                return w
            if n == "drain":
                def wd(*args, **kw):
                    self._W, self._R = [], []
                    r = a(*args, **kw)
                    self.last = r
                    return r
                return wd
            return a

    def mark(eng):
        sname = {"tensor": "sT", "vector": "sV", "scalar": "sS"}[eng]
        h = sems[sname][0]
        sems[sname][1] += 1
        v = sems[sname][1]

        def _m(e, h=h):
            last = getattr(e, "last", None)
            ok = False
            if last is not None:
                try:
                    last.then_inc(h, 1)
                    ok = True
                except Exception:
                    ok = False
                e.last = None
            if not ok:
                e.drain().then_inc(h, 1)
        on(eng, _m)
        return v

    def attach_inc(eng, buildfn):
        sname = {"tensor": "sT", "vector": "sV", "scalar": "sS"}[eng]
        h = sems[sname][0]
        sems[sname][1] += 1
        v = sems[sname][1]
        on(eng, lambda e, h=h, f=buildfn: f(e).then_inc(h, 1))
        return v

    V = lambda fn: on("vector", fn)
    S = lambda fn: on("scalar", fn)
    TT = lambda fn: on("tensor", fn)

    dma("sync", "sInit", i64c[:], i64_d)
    dma("sync", "sInit", bbias[:], bbias_d)
    dma("sync", "sInit", cbias[:], cbias_d)
    dma("sync", "sInit", dtb[:], dtb_d)
    dma("sync", "sInit", alog_s[:], alog_d)
    dma("sync", "sInit", arena_f32[:, 0:128], i128_d)
    dma("sync", "sInit", arena_f32[:, 128:256], trineg_d)
    c_loaded = dcol_v = dma("sync", "sInit", smalls_dc[:], dpar_d)

    wtok("vector", c_loaded)
    def _cinit(v):
        v.tensor_copy(i64r[:], i64c[:])
        v.tensor_copy(i128r[:], arena_f32[:, 0:128])
        v.tensor_copy(identb[:], arena_f32[:, 0:128])
        v.tensor_copy(trinr[:], arena_f32[:, 128:256])
        v.memset(onesf[:], 1.0)
        v.memset(ones64[:], 1.0)
        v.memset(arena_f32[:, 256:384], 1.0)
        v.tensor_copy(onesr[:], arena_f32[:, 256:384])
        v.tensor_copy(onesb[:], arena_f32[:, 256:258])
        v.memset(epscol[:], LN_EPS)
        v.memset(rmseps[:], RMS_EPS)
        v.memset(halfpi[:], float(np.pi / 2))
        v.memset(zeros_t[:], 0.0)
        v.memset(carry[:], 0.0)
        v.memset(ang_carry[:], 0.0)
    V(_cinit)
    v_cinit = mark("vector")

    wtok("scalar", c_loaded)
    S(lambda s: s.activation(mexpA[:], alog_s[:], AF.Exp))
    S(lambda s: s.mul(mexpA[:], mexpA[:], -1.0))
    s_mexpA = mark("scalar")

    scr_dline = nc.dram_tensor("scr_dline", [1, DI], BF, kind="Internal").ap()
    V(lambda v: v.tensor_scalar(exp_t[:], ones64[:], smalls_dc[:, 0:1], None, ALU.mult))
    v_dline = mark("vector")
    wait("sync", "sV", v_dline)
    drep_v = dma("sync", "sDl", scr_dline[0:1, :].rearrange("o (h p) -> o h p", h=64), exp_t[:])

    inbufs = [arena_f32, x_tok[:, 2048:4096]]
    wq_out = arena3_bf[:, 0:4096]
    wqT_out = arena3_bf[:, 4096:8192]
    prev_q2 = [v_cinit, v_cinit]
    prev_tr = [None, None]
    prev_trst = [None, None]

    def fq_load(ch):
        idx, nrows, loads, dstf = ch
        p = idx % 2
        wait("sync", "sV", prev_q2[p])
        vload = None
        for sl, srcap in loads:
            vload = dma("sync", ("sPa", "sPb")[p], sl, srcap)
        ch.append(vload)

    def fq_q1(ch):
        idx, nrows, loads, dstf, vload = ch
        p = idx % 2
        buf = inbufs[p]
        sc = s128a[0:nrows, p * 32:p * 32 + 32]
        wtok("vector", vload)
        def _q1(v, buf=buf, nrows=nrows, sc=sc):
            for s_ in range(16):
                v.reduce_max(sc[:, s_:s_ + 1],
                             buf[0:nrows, s_ * 128:(s_ + 1) * 128],
                             axis=AXX, apply_absolute_value=True)
            v.tensor_scalar_max(sc[:, 0:16], sc[:, 0:16], 1e-10)
            v.reciprocal(sc[:, 16:32], sc[:, 0:16])
            v.tensor_scalar_mul(sc[:, 16:32], sc[:, 16:32], 7.0)
            v.tensor_scalar_mul(sc[:, 0:16], sc[:, 0:16], float(1.0 / 7.0))
            for s_ in range(16):
                v.tensor_scalar(buf[0:nrows, s_ * 128:(s_ + 1) * 128],
                                buf[0:nrows, s_ * 128:(s_ + 1) * 128],
                                sc[:, 16 + s_:17 + s_], 8.0, ALU.mult, ALU.add)
        V(_q1)
        ch.append(mark("vector"))

    def fq_castA(ch):
        idx, nrows, loads, dstf, vload, q1 = ch
        p = idx % 2
        buf = inbufs[p]
        wait("scalar", "sV", q1)
        S(lambda s, buf=buf, nrows=nrows, p=p: s.copy(
            arena_i32[0:nrows, p * 1024:p * 1024 + 1024], buf[0:nrows, 0:1024]))
        ch.append(mark("scalar"))

    def fq_backA(ch):
        idx, nrows, loads, dstf, vload, q1, qsA = ch
        p = idx % 2
        c1f = arena_bf[0:nrows, p * 2048:p * 2048 + 2048]
        wait("vector", "sS", qsA)
        V(lambda v, c1f=c1f, nrows=nrows, p=p: v.tensor_copy(
            c1f[:, 0:1024], arena_i32[0:nrows, p * 1024:p * 1024 + 1024]))
        ch.append(mark("vector"))

    def fq_castB(ch):
        idx, nrows, loads, dstf, vload, q1, qsA, q2a = ch
        p = idx % 2
        buf = inbufs[p]
        wait("scalar", "sV", q2a)
        S(lambda s, buf=buf, nrows=nrows, p=p: s.copy(
            arena_i32[0:nrows, p * 1024:p * 1024 + 1024], buf[0:nrows, 1024:2048]))
        ch.append(mark("scalar"))

    def fq_final(ch):
        idx, nrows, loads, dstf, vload, q1, qsA, q2a, qsB = ch
        p = idx % 2
        buf = inbufs[p]
        sc = s128a[0:nrows, p * 32:p * 32 + 32]
        c1f = arena_bf[0:nrows, p * 2048:p * 2048 + 2048]
        gp = arena_bf[0:nrows, 4096 + p * 2048:4096 + p * 2048 + 2048]
        wq = wq_out[:, p * 2048:p * 2048 + 2048]
        wait("vector", "sS", qsB)
        wtok("vector", prev_tr[p])
        def _q2c(v, buf=buf, wq=wq, nrows=nrows, sc=sc, c1f=c1f, gp=gp, p=p):
            v.tensor_copy(c1f[:, 1024:2048], arena_i32[0:nrows, p * 1024:p * 1024 + 1024])
            v.scalar_tensor_tensor(gp[:], buf[0:nrows, :], 0.5, c1f[:],
                                   ALU.subtract, ALU.is_ge)
            v.tensor_add(buf[0:nrows, :], c1f[:], gp[:])
            for s_ in range(16):
                v.tensor_scalar(wq[0:nrows, s_ * 128:(s_ + 1) * 128],
                                buf[0:nrows, s_ * 128:(s_ + 1) * 128],
                                -8.0, sc[:, s_:s_ + 1], ALU.add, ALU.mult)
        V(_q2c)
        q2 = mark("vector")
        prev_q2[p] = q2
        ch.append(q2)

    prev_evac = [0, 0]

    def fq_ptr(ch):
        idx, nrows, loads, dstf, vload, q1, qsA, q2a, qsB, q2 = ch
        p = idx % 2
        wq = wq_out[:, p * 2048:p * 2048 + 2048]
        pm = PM[p]
        wait("tensor", "sV", q2)
        wait("tensor", "sS", prev_evac[p])
        if idx < 2:
            wait("tensor", "sV", v_cinit)
        def _tr(tt, pm=pm, wq=wq, nrows=nrows):
            for k in range(16):
                tt.matmul(pm[0:128, k * 128:k * 128 + nrows],
                          wq[0:nrows, k * 128:(k + 1) * 128],
                          identb[0:nrows, 0:nrows], start=True, stop=True)
        TT(_tr)
        tm = mark("tensor")
        prev_tr[p] = ("sT", tm)
        ch.append(tm)

    def fq_pevac(ch):
        idx, nrows, loads, dstf, vload, q1, qsA, q2a, qsB, q2, tm = ch
        p = idx % 2
        wqT = wqT_out[:, p * 2048:p * 2048 + 2048]
        pm = PM[p]
        wait("scalar", "sT", tm)
        wtok("scalar", prev_trst[p])
        if nrows == 128:
            S(lambda s, wqT=wqT, pm=pm: s.copy(wqT[:], pm[:, 0:2048]))
        else:
            def _ev(s, wqT=wqT, pm=pm, nrows=nrows):
                for k in range(16):
                    s.copy(wqT[:, k * 128:k * 128 + nrows],
                           pm[:, k * 128:k * 128 + nrows])
            S(_ev)
        ev = mark("scalar")
        prev_evac[p] = ev
        ch.append(ev)

    def fq_pst(ch):
        idx, nrows, loads, dstf, vload, q1, qsA, q2a, qsB, q2, tm, ev = ch
        p = idx % 2
        wqT = wqT_out[:, p * 2048:p * 2048 + 2048]
        wait("sync", "sS", ev)
        st = None
        for k in range(16):
            st = dma("sync", ("sPstA", "sPstB")[p], dstf(k),
                     wqT[0:128, k * 128:k * 128 + nrows])
        prev_trst[p] = st

    STAGES = [fq_load, fq_q1, fq_castA, fq_backA, fq_castB, fq_final,
              fq_ptr, fq_pevac, fq_pst]

    chunks = []
    idx = 0
    for ci in range(67):
        r0 = ci * 128
        nrows = min(128, NPROJ - r0)
        buf = inbufs[idx % 2]
        loads = []
        for srcap, lo, hi in ((w_in_d, 0, 8384), (w_lam_d, 8384, 8448), (w_th_d, 8448, 8480)):
            s0, s1 = max(r0, lo), min(r0 + nrows, hi)
            if s0 < s1:
                loads.append((buf[s0 - r0:s1 - r0, :], srcap[s0 - lo:s1 - lo, :]))
        chunks.append([idx, nrows, loads,
                       (lambda k, r0=r0, nrows=nrows: scr_wallT[k * 128:(k + 1) * 128, r0:r0 + nrows])])
        idx += 1
    for ci in range(32):
        r0 = (ci // 2) * 128
        c0 = (ci % 2) * 2048
        buf = inbufs[idx % 2]
        chunks.append([idx, 128, [(buf[:], w_out_d[r0:r0 + 128, c0:c0 + 2048])],
                       (lambda k, r0=r0, c0=c0: scr_woutT[c0 + k * 128:c0 + (k + 1) * 128, r0:r0 + 128])])
        idx += 1

    NCHK = len(chunks)
    f8_pending = []

    def emit_pair(j):
        for st in STAGES[1:6]:
            for i in (2 * j, 2 * j + 1):
                if i < NCHK:
                    st(chunks[i])
        for i in (2 * (j + 1), 2 * (j + 1) + 1):
            if i < NCHK:
                fq_load(chunks[i])
        for st in STAGES[6:]:
            for i in (2 * j, 2 * j + 1):
                if i < NCHK:
                    st(chunks[i])
        for _ in range(4):
            if f8_pending:
                f8_pending.pop(0)()

    for i in (0, 1):
        fq_load(chunks[i])
    prev8 = [None, None]
    j8 = 0
    def f8hop(src_ap, dst_ap):
        nonlocal j8
        b = (w8A, w8B)[j8 % 2]
        wtok("gpsimd", prev8[j8 % 2])
        v8 = dma("gpsimd", ("sG8a", "sG8b")[j8 % 2], b[:], src_ap)
        wtok("sync", v8)
        prev8[j8 % 2] = dma("sync", ("sP8a", "sP8b")[j8 % 2], dst_ap, b[:])
        j8 += 1
    for srcd, dstd in ((wg0_d, scr_wg0), (wg1_d, scr_wg1)):
        for ci in range(16):
            r0 = ci * 128
            for half in range(4):
                c0 = half * 2048
                f8_pending.append(lambda s=srcd[r0:r0 + 128, c0:c0 + 2048],
                                  d=dstd[r0:r0 + 128, c0:c0 + 2048]: f8hop(s, d))
    for ci in range(64):
        r0 = ci * 128
        f8_pending.append(lambda s=wl_d[r0:r0 + 128, :],
                          d=scr_wl[r0:r0 + 128, :]: f8hop(s, d))

    for j in range((NCHK + 1) // 2):
        emit_pair(j)
    while f8_pending:
        f8_pending.pop(0)()

    prep_V = sems["sV"][1]
    prep_S = sems["sS"][1]
    prep_toks = [(s, sems[s][1]) for s in ("sPstA", "sPstB", "sP8a", "sP8b", "sDl")
                 if sems[s][1] > 0]

    m_prev = {"tile_done_V": prep_V, "tile_done_S": 0, "tile_done_T": 0,
              "out_st": None, "xl": None, "drt": None}

    for t in range(NT):
        r0 = t * T
        if t == 0:
            for tk in prep_toks:
                wtok("sync", tk)
        wait("sync", "sV", m_prev["tile_done_V"])
        wait("sync", "sS", m_prev["tile_done_S"])
        wait("sync", "sT", m_prev["tile_done_T"])
        wtok("sync", m_prev.get("dbgx1"))
        x_in = None
        for c in range(2):
            x_in = dma("sync", "sX", x_tok[:, c * DIM:(c + 1) * DIM], x_d[r0 + c * 128:r0 + (c + 1) * 128, :])

        ln_tm = arena3_bf[:, 4096:8192]
        wtok("vector", x_in)
        wait("vector", "sS", s_mexpA)
        def _ln(v, src=x_tok):
            for c in range(2):
                xc = src[:, c * DIM:(c + 1) * DIM]
                v.reduce_sum(mcol[:, c:c + 1], xc, axis=AXX)
            v.tensor_scalar_mul(mcol[:, 0:2], mcol[:, 0:2], float(1.0 / DIM))
        V(_ln)
        vln = mark("vector")
        wait("scalar", "sV", vln)
        def _lns(s, src=x_tok):
            for c in range(2):
                s.activation(arena2_bf[:, c * DIM:(c + 1) * DIM], src[:, c * DIM:(c + 1) * DIM],
                             AF.Square)
        S(_lns)
        sln = mark("scalar")
        wait("vector", "sS", sln)
        def _ln2(v, src=x_tok, dstv=ln_tm):
            for c in range(2):
                v.reduce_sum(mcol[:, 2 + c:3 + c], arena2_bf[:, c * DIM:(c + 1) * DIM], axis=AXX)
            v.tensor_scalar_mul(mcol[:, 2:4], mcol[:, 2:4], float(1.0 / DIM))
            v.tensor_mul(rstd_c[:, 0:2], mcol[:, 0:2], mcol[:, 0:2])
            v.tensor_sub(mcol[:, 2:4], mcol[:, 2:4], rstd_c[:, 0:2])
        V(_ln2)
        vln2 = mark("vector")
        wait("scalar", "sV", vln2)
        def _ln3(s):
            s.activation(mcol[:, 2:4], mcol[:, 2:4], AF.Ln, bias=epscol[:, 0:1])
            s.activation(mcol[:, 2:4], mcol[:, 2:4], AF.Exp, scale=-0.5)
        S(_ln3)
        sln3 = mark("scalar")
        wait("vector", "sS", sln3)
        def _ln4(v, src=x_tok, dstv=ln_tm):
            for c in range(2):
                v.tensor_scalar(dstv[:, c * DIM:(c + 1) * DIM], src[:, c * DIM:(c + 1) * DIM],
                                mcol[:, c:c + 1], mcol[:, 2 + c:3 + c], ALU.subtract, ALU.mult)
        V(_ln4)
        vln4 = mark("vector")
        wait("sync", "sV", vln4)
        tr_v = None
        for c in range(2):
            for k in range(16):
                tr_v = dma("sync", "sTr1", ln1_fm[:, k * T + c * 128: k * T + c * 128 + 128],
                           ln_tm[:, c * DIM + k * 128: c * DIM + (k + 1) * 128], transpose=True)
        ln1fm_done = tr_v

        if debug and t == 0:
            wtok("sync", ln1fm_done)
            dma("sync", "sDbg", dbg["dbg_ln1fm"], ln1_fm[:])
            wait("sync", "sDbg", sems["sDbg"][1])

        xh_fm = arena_bf[:, 0:8192]
        wload = [None, None]
        gmark = {}
        evac_mark = {}
        step = 0
        for g in range(17):
            rg0 = g * 512
            mg = min(512, NPROJ - rg0)
            pm = PM[g % 2]
            for k in range(16):
                b = step % 4
                wb = (wbufA, wbufB, wbufC, wbufD)[b]
                wait("sync", "sT", gmark.get(("buf", step - 4), 0))
                wload[b % 2] = dma("sync", ("sWa", "sWb", "sWc", "sWd")[b], wb[:, 0:mg],
                               scr_wallT[k * 128:(k + 1) * 128, rg0:rg0 + mg])
                wtok("tensor", wload[b % 2])
                if k == 0:
                    wait("tensor", "sS", evac_mark.get(g - 2, prep_S if (t == 0 and g < 2) else 0))
                    wtok("tensor", ln1fm_done)
                def _mm(tt, pm=pm, wb=wb, k=k, mg=mg):
                    nmc = (mg + 127) // 128
                    for mc in range(nmc):
                        M = min(128, mg - mc * 128)
                        tt.matmul(pm[0:M, mc * 512:mc * 512 + T],
                                  wb[:, mc * 128:mc * 128 + M],
                                  ln1_fm[:, k * T:(k + 1) * T],
                                  start=(k == 0), stop=(k == 15))
                TT(_mm)
                gmark[("buf", step)] = mark("tensor")
                step += 1
            gm = gmark[("buf", step - 1)]
            wait("scalar", "sT", gm)
            pm_str = pm[:, 0:2048].rearrange("p (a b) -> p a b", a=4)[:, :, 0:T]
            if g < 8:
                S(lambda s, pm_str=pm_str, g=g: s.activation(
                    z_fm[:, g * 1024:(g + 1) * 1024].rearrange("p (a b) -> p a b", a=4),
                    pm_str, AF.Silu))
            elif g < 16:
                S(lambda s, pm_str=pm_str, g=g: s.activation(
                    xh_fm[:, (g - 8) * 1024:(g - 7) * 1024].rearrange("p (a b) -> p a b", a=4),
                    pm_str, AF.Silu))
            else:
                def _e8(s, pm=pm):
                    s.copy(bc_t[:], pm[:, 0:T])
                    s.copy(dtlam[:], pm[:, 512:512 + T])
                    s.copy(theta_t[:], pm[0:32, 1024:1024 + T])
                S(_e8)
            evac_mark[g] = mark("scalar")
        evac_mark[7] = evac_mark[15]
        inproj_S = evac_mark[16]

        if debug and t == 0:
            wait("sync", "sS", evac_mark[7])
            dma("sync", "sDbg", dbg["dbg_zfm"], z_fm[:])
            wait("sync", "sDbg", sems["sDbg"][1])
            wait("sync", "sS", inproj_S)
            dma("sync", "sDbg", dbg["dbg_bc"], bc_t[:])
            wait("sync", "sDbg", sems["sDbg"][1])

        wtok("sync", m_prev["xl"])
        wait("sync", "sS", evac_mark[7])
        xh_tr = None
        for kk in range(32):
            for c in range(2):
                xh_tr = dma("sync", "sTrXh", xh_tm[:, c * DI + kk * 128: c * DI + (kk + 1) * 128],
                            xh_fm[:, kk * T + c * 128: kk * T + c * 128 + 128], transpose=True)
        xh_tm_done = xh_tr

        wait("scalar", "sS", inproj_S)
        def _sm1(s):
            s.activation(s64a[:], dtlam[0:64, :], AF.Abs, bias=dtb[:, 0:1])
            s.activation(s64a[:], s64a[:], AF.Exp, scale=-1.0)
            s.activation(s64a[:], s64a[:], AF.Ln, bias=onesf[0:64, 0:1])
            s.activation(s64b[:], dtlam[0:64, :], AF.Relu, bias=dtb[:, 0:1])
            s.activation(s128a[64:128, :], dtlam[64:128, :], AF.Exp, scale=-1.0)
        S(_sm1)
        sm1 = mark("scalar")
        wait("vector", "sS", sm1)
        def _sm2(v):
            v.tensor_add(dt_t[:], s64a[:], s64b[:])
            v.tensor_copy(dt_r[:], dt_t[:])
            v.tensor_scalar(a_fm_t[:], dt_t[:], mexpA[:, 0:1], None, ALU.mult)
            v.tensor_scalar(s128a[64:128, :], s128a[64:128, :], 1.0, None, ALU.add)
            v.reciprocal(s128a[64:128, :], s128a[64:128, :])
        V(_sm2)
        sm2 = mark("vector")
        wait("sync", "sV", sm2)
        lam_mv = dma("sync", "sLam", lam_t[:], s128a[64:128, :])
        wait("tensor", "sV", sm2)
        wait("tensor", "sS", evac_mark.get(7, 0))
        wait("tensor", "sS", inproj_S)
        TT(lambda tt: tt.matmul(pmA[0:1, 0:T], onesr[0:64, 0:1], dt_r[:], start=True, stop=True))
        t_dtavg = mark("tensor")
        wait("scalar", "sT", t_dtavg)
        S(lambda s: s.activation(dtavg_row[:], pmA[0:1, 0:T], AF.Copy, scale=float(1.0 / NH)))
        s_dtavg = mark("scalar")
        wait("vector", "sS", s_dtavg)
        V(lambda v: v.tensor_copy(dtavg_r[:], dtavg_row[:]))
        v_dtavg = mark("vector")
        wait("tensor", "sV", v_dtavg)
        TT(lambda tt: tt.matmul(pmA[0:32, 0:T], onesr[0:1, 0:32], dtavg_r[:], start=True, stop=True))
        t_dtrep = mark("tensor")
        w3 = s64c[0:32, :]
        wait("vector", "sT", t_dtrep)
        def _ang(v):
            v.tensor_mul(w1[:], theta_t[:], pmA[0:32, 0:T])
            v.tensor_tensor_scan(ang_t[:], w1[:], zeros_t[0:32, 0:T], 0.0, ALU.add, ALU.add)
            v.tensor_scalar(ang_t[:], ang_t[:], ang_carry[:, 0:1], None, ALU.add)
            v.tensor_copy(ang_carry[:], ang_t[:, T - 1:T])
            v.tensor_scalar_mul(w1[:], ang_t[:], INV2PI)
        V(_ang)
        v_ang = mark("vector")
        wait("scalar", "sV", v_ang)
        S(lambda s: s.copy(ang_i[:], w1[:]))
        s_angi = mark("scalar")
        wait("vector", "sS", s_angi)
        def _ang2(v):
            v.tensor_copy(w2[:], ang_i[:])
            v.scalar_tensor_tensor(w3[:], w1[:], 0.5, w2[:], ALU.subtract, ALU.is_ge)
            v.scalar_tensor_tensor(theta_t[:], w1[:], 0.5, w2[:], ALU.add, ALU.is_le)
            v.tensor_add(w2[:], w2[:], w3[:])
            v.tensor_sub(w2[:], w2[:], theta_t[:])
            v.scalar_tensor_tensor(w1[:], w2[:], -TWOPI, ang_t[:], ALU.mult, ALU.add)
            v.tensor_scalar(w2[:], ang_t[:], INV2PI, 0.25, ALU.mult, ALU.add)
        V(_ang2)
        v_ang2 = mark("vector")
        wait("scalar", "sV", v_ang2)
        S(lambda s: s.copy(ang_i[:], w2[:]))
        s_angi2 = mark("scalar")
        wait("vector", "sS", s_angi2)
        def _ang3(v):
            v.tensor_copy(w3[:], ang_i[:])
            v.scalar_tensor_tensor(theta_t[:], w2[:], 0.5, w3[:], ALU.subtract, ALU.is_ge)
            v.scalar_tensor_tensor(w2[:], w2[:], 0.5, w3[:], ALU.add, ALU.is_le)
            v.tensor_add(w3[:], w3[:], theta_t[:])
            v.tensor_sub(w3[:], w3[:], w2[:])
            v.scalar_tensor_tensor(w2[:], w3[:], -TWOPI, ang_t[:], ALU.mult, ALU.add)
            v.tensor_scalar(w2[:], w2[:], float(np.pi / 2), None, ALU.add)
        V(_ang3)
        v_ang3 = mark("vector")
        wait("scalar", "sV", v_ang3)
        def _trig(s):
            s.activation(trig[0:32, 0:T], w1[:], AF.Sin)
            s.activation(trig[0:32, T:2 * T], w2[:], AF.Sin)
        S(_trig)
        s_trig = mark("scalar")
        wait("sync", "sS", s_trig)
        trig_mv = dma("sync", "sTrg", trig[32:64, :], trig[0:32, :])

        if debug and t == 0:
            wait("sync", "sV", v_ang)
            dma("sync", "sDbg", dbg["dbg_ang"], ang_t[:])
            wait("sync", "sDbg", sems["sDbg"][1])
            wait("sync", "sS", inproj_S)
            dma("sync", "sDbg", dbg["dbg_dt"], dt_t[:])
            wait("sync", "sDbg", sems["sDbg"][1])

        wait("sync", "sV", v_ang3)
        cm_mv = dma("sync", "sCm", craw[:], bc_t[64:128, :])
        wait("vector", "sS", inproj_S)
        wait("vector", "sS", s_trig)
        wait("vector", "sT", t_dtrep)
        def _bc1(v):
            v.tensor_mul(s64a[:], bc_t[0:64, :], bc_t[0:64, :])
            v.tensor_copy(dt_r[:], s64a[:])
        V(_bc1)
        v_bc1 = mark("vector")
        wait("tensor", "sV", v_bc1)
        TT(lambda tt: tt.matmul(pmB[0:1, 0:T], onesr[0:64, 0:1], dt_r[:], start=True, stop=True))
        t_bssq = mark("tensor")
        wtok("vector", cm_mv)
        wait("vector", "sT", t_bssq)
        def _bc2(v):
            v.tensor_mul(s64b[:], craw[:], craw[:])
            v.tensor_copy(dt_r[:], s64b[:])
        V(_bc2)
        v_bc2 = mark("vector")
        wait("tensor", "sV", v_bc2)
        TT(lambda tt: tt.matmul(pmB[0:1, T:2 * T], onesr[0:64, 0:1], dt_r[:], start=True, stop=True))
        t_cssq = mark("tensor")
        wait("scalar", "sT", t_cssq)
        def _bc3(s):
            s.activation(srow2[:], pmB[0:1, 0:T], AF.Ln, scale=float(1.0 / 64), bias=rmseps[0:1, 0:1])
            s.activation(srow2[:], srow2[:], AF.Exp, scale=-0.5)
        S(_bc3)
        s_brstd = mark("scalar")
        wait("vector", "sS", s_brstd)
        V(lambda v: v.tensor_copy(srow2_r[:], srow2[:]))
        v_brstd = mark("vector")
        wait("tensor", "sV", v_brstd)
        TT(lambda tt: tt.matmul(pmA[0:64, 0:T], onesr[0:1, 0:64], srow2_r[:], start=True, stop=True))
        t_brep = mark("tensor")
        wait("scalar", "sT", t_brep)
        def _bc4(s):
            s.activation(srow2[:], pmB[0:1, T:2 * T], AF.Ln, scale=float(1.0 / 64), bias=rmseps[0:1, 0:1])
            s.activation(srow2[:], srow2[:], AF.Exp, scale=-0.5)
        S(_bc4)
        s_crstd = mark("scalar")
        wait("vector", "sS", s_crstd)
        wait("vector", "sT", t_brep)
        def _bc5(v):
            v.tensor_mul(bmn[:], bc_t[0:64, :], pmA[0:64, 0:T])
            v.tensor_scalar(bmn[:], bmn[:], bbias[:, 0:1], None, ALU.add)
            v.tensor_copy(srow2_r[:], srow2[:])
        V(_bc5)
        v_bc5 = mark("vector")
        wait("tensor", "sV", v_bc5)
        TT(lambda tt: tt.matmul(pmA[0:64, T:2 * T], onesr[0:1, 0:64], srow2_r[:], start=True, stop=True))
        t_crep = mark("tensor")
        wait("vector", "sT", t_crep)
        def _bc6(v):
            v.tensor_mul(cmn[:], craw[:], pmA[0:64, T:2 * T])
            v.tensor_scalar(cmn[:], cmn[:], cbias[:, 0:1], None, ALU.add)
        V(_bc6)
        v_bc6 = mark("vector")

        def rope(src, v_src_mark, prev_mv2=None):
            wait("sync", "sV", v_src_mark)
            mv1 = dma("sync", "sRp1", w1[:], src[32:64, :])
            wtok("vector", mv1)
            wtok("vector", prev_mv2)
            wtok("vector", trig_mv)
            def _r(v, src=src):
                sin0 = trig[0:32, 0:T]
                cos0 = trig[0:32, T:2 * T]
                v.tensor_mul(w2[:], w1[:], sin0)
                v.tensor_mul(theta_t[:], src[0:32, :], cos0)
                v.tensor_sub(theta_t[:], theta_t[:], w2[:])
                v.tensor_mul(w2[:], src[0:32, :], sin0)
                v.tensor_mul(w1[:], w1[:], cos0)
                v.tensor_add(w2[:], w2[:], w1[:])
                v.tensor_copy(src[0:32, :], theta_t[:])
            V(_r)
            vr = mark("vector")
            wait("sync", "sV", vr)
            mv2 = dma("sync", "sRp2", src[32:64, :], w2[:])
            return mv2
        b_roped = rope(bmn, v_bc5)
        c_roped = rope(cmn, v_bc6, prev_mv2=b_roped)
        if debug and t == 0:
            wtok("sync", c_roped)
            dma("sync", "sDbg", dbg["dbg_cro"], cmn[:])
            wait("sync", "sDbg", sems["sDbg"][1])
        wtok("vector", c_roped)
        def _bc7(v):
            v.tensor_copy(bm_bf[:], bmn[:])
            v.tensor_copy(cm_bf[:], cmn[:])
        V(_bc7)
        v_bcbf = mark("vector")
        wait("sync", "sV", v_bcbf)
        btm_v = None
        for c in range(NCH):
            btm_v = dma("sync", "sTrB", b_tm[:, c * 64:(c + 1) * 64],
                        bm_bf[0:64, c * 128:(c + 1) * 128], transpose=True)

        if debug and t == 0:
            wait("sync", "sV", v_bcbf)
            dma("sync", "sDbg", dbg["dbg_bro"], bmn[:])
            wait("sync", "sDbg", sems["sDbg"][1])

        wait("scalar", "sV", sm2)
        wait("scalar", "sV", v_bc6)
        S(lambda s: s.activation(s64c[:], a_fm_t[:], AF.Exp))
        s_expA = mark("scalar")
        wait("vector", "sS", s_expA)
        wtok("vector", lam_mv)
        wtok("vector", c_roped)
        def _gb(v):
            v.tensor_mul(s64a[:], lam_t[:], dt_t[:])
            v.tensor_copy(gam_r[:], s64a[:])
            v.tensor_scalar(s64b[:], lam_t[:], -1.0, 1.0, ALU.mult, ALU.add)
            v.tensor_mul(s64b[:], s64b[:], dt_t[:])
            v.tensor_mul(s64b[:], s64b[:], s64c[:])
            v.tensor_copy(bet_r[:], s64b[:])
        V(_gb)
        v_gb = mark("vector")
        if debug and t == 0:
            wait("sync", "sV", v_gb)
            dma("sync", "sDbg", dbg["dbg_gb"][0:64, 0:T], s64a[:])
            wait("sync", "sDbg", sems["sDbg"][1])
            dma("sync", "sDbg", dbg["dbg_gb"][0:64, T:2 * T], s64b[:])
            wait("sync", "sDbg", sems["sDbg"][1])

        prev_chunk_V = 0
        for c in range(NCH):
            ccol = c * Q
            xh_c = xh_tm[:, c * DI:(c + 1) * DI]
            xhs = arena_bf[:, 0:4096]
            Xc = arena3_bf[:, 0:4096]
            Xd = arena3_bf[:, 4096:8192]
            ge = arena2_bf[:, 0:4096]
            be = arena2_bf[:, 4096:8192]

            wtok("sync", xh_tm_done)
            wait("sync", "sS", evac_mark[7])
            wait("sync", "sV", prev_chunk_V)
            if c == 0:
                sh1 = dma("sync", "sSh", xhs[1:128, :], xh_c[0:127, :])
                if t == 0:
                    sh2 = sh1
                    wtok("vector", sh1)
                    V(lambda v: v.memset(xhs[0:1, :], 0.0))
                    mark("vector")
                else:
                    wtok("sync", m_prev["xl"])
                    sh2 = dma("sync", "sSh", xhs[0:1, :], scr_xlast[0:1, :])
            else:
                sh1 = dma("sync", "sSh", xhs[1:128, :], xh_c[0:127, :])
                sh2 = dma("sync", "sSh", xhs[0:1, :], xh_tm[127:128, (c - 1) * DI:c * DI])
            wait("tensor", "sV", v_gb)
            wait("tensor", "sV", prev_chunk_V)
            def _gbe(tt, ccol=ccol):
                for half in range(2):
                    pm = PM[half]
                    for blk in range(4):
                        h0 = half * 32 + blk * 8
                        rhs = i64r[:, h0:h0 + 8].unsqueeze(2).broadcast_to([64, 8, 64])
                        tt.matmul(pm[:, blk * 512:(blk + 1) * 512].rearrange("p (h q) -> p h q", h=8),
                                  gam_r[:, ccol:ccol + Q], rhs, start=True, stop=True)
            TT(_gbe)
            t_ge = mark("tensor")
            wait("scalar", "sT", t_ge)
            def _gee(s):
                s.copy(ge[:, 0:2048], pmA[:, 0:2048])
                s.copy(ge[:, 2048:4096], pmB[:, 0:2048])
            S(_gee)
            s_ge = mark("scalar")
            if debug and t == 0 and c == 0:
                wait("sync", "sS", s_ge)
                dma("sync", "sDbg", dbg["dbg_ge"], ge[:])
                wait("sync", "sDbg", sems["sDbg"][1])
            wait("tensor", "sS", s_ge)
            def _bbe(tt, ccol=ccol):
                for half in range(2):
                    pm = PM[half]
                    for blk in range(4):
                        h0 = half * 32 + blk * 8
                        rhs = i64r[:, h0:h0 + 8].unsqueeze(2).broadcast_to([64, 8, 64])
                        tt.matmul(pm[:, blk * 512:(blk + 1) * 512].rearrange("p (h q) -> p h q", h=8),
                                  bet_r[:, ccol:ccol + Q], rhs, start=True, stop=True)
            TT(_bbe)
            t_be = mark("tensor")
            wait("scalar", "sT", t_be)
            def _bee(s):
                s.copy(be[:, 0:2048], pmA[:, 0:2048])
                s.copy(be[:, 2048:4096], pmB[:, 0:2048])
            S(_bee)
            s_be = mark("scalar")
            wait("vector", "sS", s_be)
            if debug:
                wait("vector", "sDbg", sems["sDbg"][1])
            wtok("vector", sh2)
            def _X(v, xh_c=xh_c):
                v.tensor_mul(Xc[:], xh_c[:], ge[:])
                v.tensor_mul(Xd[:], xhs[:], be[:])
                v.tensor_add(Xc[:], Xc[:], Xd[:])
            V(_X)
            v_X = mark("vector")

            wait("vector", "sV", v_gb)
            def _acs(v, ccol=ccol):
                v.tensor_tensor_scan(acs_fm[:], a_fm_t[:, ccol:ccol + Q],
                                     zeros_t[0:64, 0:Q], 0.0, ALU.add, ALU.add)
                v.tensor_copy(acs_r[:], acs_fm[:])
                v.tensor_scalar(nacs_r[:], acs_fm[:], -1.0, None, ALU.mult)
            V(_acs)
            v_acs = mark("vector")
            wait("scalar", "sV", v_acs)
            def _acs2(s):
                s.activation(eacs_bf[:], acs_fm[:], AF.Exp)
            S(_acs2)
            s_acs2 = mark("scalar")
            wait("vector", "sS", s_acs2)
            def _vd2e(v):
                v.tensor_scalar(d2e_f[:], acs_fm[:], acs_fm[:, Q - 1:Q], -1.0,
                                ALU.subtract, ALU.mult)
                v.tensor_copy(d2e_r[:], d2e_f[:])
            V(_vd2e)
            v_d2e = mark("vector")
            wait("scalar", "sV", v_acs)
            S(lambda s: s.activation(e_col[:], acs_fm[:, Q - 1:Q], AF.Exp))
            s_erow = mark("scalar")
            wait("vector", "sS", s_erow)
            V(lambda v: v.tensor_scalar(exp_t[:], ones64[:], e_col[:, 0:1], None, ALU.mult))
            v_erow = mark("vector")
            wait("sync", "sV", v_erow)
            el_w = dma("sync", "sEw", scr_eline[0:1, :].rearrange("o (h p) -> o h p", h=64), exp_t[:])

            wait("sync", "sS", s_acs2)
            e2d_v = dma("sync", "sE2d", scr_line[:].rearrange("o (h q) -> o h q", h=64), eacs_bf[:])
            if debug and t == 0 and c == 0:
                wait("sync", "sV", v_X)
                m_prev["dbgX"] = dma("sync", "sDbg", dbg["dbg_X"], Xc[:])
                wait("sync", "sDbg", sems["sDbg"][1])
            wait("tensor", "sV", v_bcbf)
            wait("tensor", "sS", s_be)
            TT(lambda tt, ccol=ccol: tt.matmul(pmA[0:128, 0:128], bm_bf[:, ccol:ccol + Q],
                                               cm_bf[:, ccol:ccol + Q], start=True, stop=True))
            t_G = mark("tensor")
            wait("vector", "sT", t_G)
            V(lambda v: v.tensor_copy(gt_bf[:], pmA[0:128, 0:128]))
            v_G = mark("vector")

            mp = arena_bf[:, 4096:5120]
            m1a = arena_bf[:, 5120:6144]
            m1b = arena_bf[:, 6144:7168]
            grep = arena_bf[:, 0:1024]
            m2 = arena_bf[:, 1024:2048]
            cmrep = arena_bf[:, 2048:3072]
            earep = arena_bf[:, 3072:4096]
            ydone = []
            prev_evac = 0
            prev_m1 = {}
            prev_diff = {}
            prev_mk = {}
            for e8 in range(8):
                h0 = e8 * 8
                qpm = PM[e8 % 2]
                aslot = acsline[0:1, 0:1024]
                wait("sync", "sT", prev_diff.get(e8 - 1, 0))
                wait("sync", "sV", v_acs)
                acl_v = dma("sync", "sAcl", aslot.rearrange("o (h q) -> o h q", h=8),
                            acs_r[h0:h0 + 8, :])
                wtok("tensor", acl_v)
                wait("tensor", "sV", v_acs)
                wait("tensor", "sS", prev_m1.get(e8 - 2, 0))
                if e8 < 2:
                    wait("tensor", "sS", s_be)
                    wait("tensor", "sV", v_G)
                def _dmm(tt, h0=h0, qpm=qpm, aslot=aslot):
                    for half in range(2):
                        sl = qpm[:, half * 512:(half + 1) * 512]
                        h1 = h0 + half * 4
                        tt.matmul(sl, onesr[0:1, 0:128],
                                  aslot[0:1, half * 512:(half + 1) * 512],
                                  start=True, stop=False)
                        rhs = i64r[:, h1:h1 + 4].unsqueeze(2).broadcast_to([64, 4, 128])
                        tt.matmul(sl.rearrange("p (h q) -> p h q", h=4),
                                  nacs_r[:], rhs, start=False, stop=False)
                        rhs2 = i128r[:, :].unsqueeze(1).broadcast_to([128, 4, 128])
                        tt.matmul(sl.rearrange("p (h q) -> p h q", h=4),
                                  trinr[:], rhs2, start=False, stop=True)
                TT(_dmm)
                t_diff = mark("tensor")
                prev_diff[e8] = t_diff
                wait("sync", "sV", v_X if e8 == 0 else 0)
                wait("sync", "sV", prev_mk.get(e8 - 1, 0))
                wait("sync", "sV", v_G)
                g_v = dma("sync", "sRep", grep[:].rearrange("p (h q) -> p h q", h=8),
                          gt_bf[:].unsqueeze(1).broadcast_to([128, 8, 128]))
                c_v = dma("sync", "sRep", cmrep[0:64, :].rearrange("p (h q) -> p h q", h=8),
                          cm_bf[0:64, ccol:ccol + Q].unsqueeze(1).broadcast_to([64, 8, 128]))
                wtok("sync", e2d_v)
                e_v = dma("sync", "sRep", earep[0:64, :],
                          scr_line[0:1, h0 * 128:(h0 + 8) * 128].broadcast_to([64, 1024]))
                wait("scalar", "sT", t_diff)
                S(lambda s, qpm=qpm: s.activation(mp[:], qpm[:, 0:1024], AF.Exp))
                prev_m1[e8] = mark("scalar")
                m1 = (m1a, m1b)[e8 % 2]
                wait("vector", "sS", prev_m1[e8])
                if debug and e8 == 2:
                    wait("vector", "sDbg", sems["sDbg"][1])
                wtok("vector", e_v)
                def _mk(v, m1=m1):
                    v.tensor_mul(m1[:], mp[:], grep[:])
                    v.tensor_mul(m2[0:64, :], earep[0:64, :], cmrep[0:64, :])
                V(_mk)
                v_m1 = mark("vector")
                prev_mk[e8] = v_m1
                if debug and t == 0 and c == 0 and e8 == 0:
                    wait("sync", "sV", v_m1)
                    dma("sync", "sDbg", dbg["dbg_mask1"], m1[:])
                    wait("sync", "sDbg", sems["sDbg"][1])
                    wait("sync", "sV", v_G)
                    m_prev["dbgm1"] = dma("sync", "sDbg", dbg["dbg_gt"], gt_bf[:])
                    wait("sync", "sDbg", sems["sDbg"][1])
                ypm = PM[(e8 + 1) % 2]
                wait("tensor", "sV", v_m1)
                wait("tensor", "sV", v_X)
                if e8 >= 2:
                    wait("tensor", "sV", ydone[e8 - 2])
                def _ymm(tt, m1=m1, ypm=ypm, h0=h0, c=c):
                    for j in range(8):
                        h = h0 + j
                        tt.matmul(ypm[:, 1024 + j * 64: 1024 + (j + 1) * 64],
                                  m1[:, j * 128:(j + 1) * 128],
                                  Xc[:, h * 64:(h + 1) * 64], start=True, stop=False)
                        tt.matmul(ypm[:, 1024 + j * 64: 1024 + (j + 1) * 64],
                                  m2[0:64, j * 128:(j + 1) * 128],
                                  carry[0:64, h * 64:(h + 1) * 64], start=False, stop=True)
                TT(_ymm)
                t_y = mark("tensor")
                wait("vector", "sT", t_y)
                def _yev(v, ypm=ypm, h0=h0, c=c):
                    v.tensor_copy(y_tok[:, c * DI + h0 * 64: c * DI + (h0 + 8) * 64],
                                  ypm[:, 1024:1536])
                V(_yev)
                ydone.append(mark("vector"))
                prev_evac = ydone[-1]
            y_chunk_done = ydone[-1]

            d2ex = arena2_bf[:, 0:4096]
            wait("tensor", "sV", y_chunk_done)
            def _d2mm(tt):
                for half in range(2):
                    pm = PM[half]
                    for blk in range(4):
                        h0 = half * 32 + blk * 8
                        rhs = i64r[:, h0:h0 + 8].unsqueeze(2).broadcast_to([64, 8, 64])
                        tt.matmul(pm[:, blk * 512:(blk + 1) * 512].rearrange("p (h q) -> p h q", h=8),
                                  d2e_r[:], rhs, start=True, stop=True)
            TT(_d2mm)
            t_d2 = mark("tensor")
            wait("scalar", "sT", t_d2)
            def _d2ev(s):
                s.activation(d2ex[:, 0:2048], pmA[:, 0:2048], AF.Exp)
                s.activation(d2ex[:, 2048:4096], pmB[:, 0:2048], AF.Exp)
            S(_d2ev)
            s_d2 = mark("scalar")
            wait("vector", "sS", s_d2)
            V(lambda v: v.tensor_mul(Xd[:], Xc[:], d2ex[:]))
            v_Xd = mark("vector")
            wtok("tensor", btm_v)
            prev_cup = 0
            for half in range(2):
                hw0 = half * 2048
                wtok("sync", el_w)
                wait("sync", "sV", prev_cup)
                el_v = dma("sync", "sEv", e_sb[:], scr_eline[0:1, hw0:hw0 + 2048].broadcast_to([64, 2048]))
                wait("tensor", "sV", v_Xd)
                wait("tensor", "sV", prev_cup)
                def _sthalf(tt, hw0=hw0, c=c):
                    for blk in range(4):
                        tt.matmul(pmA[0:64, blk * 512:(blk + 1) * 512],
                                  b_tm[:, c * 64:(c + 1) * 64],
                                  Xd[:, hw0 + blk * 512: hw0 + (blk + 1) * 512],
                                  start=True, stop=True)
                TT(_sthalf)
                t_sth = mark("tensor")
                wait("vector", "sT", t_sth)
                wtok("vector", el_v)
                def _cup(v, hw0=hw0):
                    cu = arena3_bf[0:64, 4096 + hw0: 4096 + hw0 + 2048]
                    v.tensor_mul(cu, carry[0:64, hw0:hw0 + 2048], e_sb[:])
                    v.tensor_add(carry[0:64, hw0:hw0 + 2048], cu, pmA[0:64, 0:2048])
                V(_cup)
                prev_cup = mark("vector")
            v_carry = prev_cup
            prev_chunk_V = v_carry

        wait("sync", "sV", v_carry)
        xhl_v = dma("sync", "sXl", scr_xlast[0:1, :], xh_tm[127:128, (NCH - 1) * DI:NCH * DI])
        m_prev["xl"] = xhl_v

        dbg_ytok_tok = None
        if debug and t == 0:
            wait("sync", "sV", v_dterm) if False else None
            dbg_ytok_tok = dma("sync", "sDbg", dbg["dbg_ytok"], y_tok[:])
            wait("sync", "sDbg", sems["sDbg"][1])

        wait("sync", "sV", v_carry)
        if t == 0:
            wtok("sync", drep_v)
        wtok("sync", m_prev["drt"])
        drep_t = dma("sync", "sDrT", arena2_bf[:, 0:4096], scr_dline[0:1, :].broadcast_to([128, DI]))
        m_prev["drt"] = drep_t
        wtok("vector", drep_t)
        wtok("vector", dbg_ytok_tok)
        def _dterm(v):
            for c in range(NCH):
                v.tensor_mul(arena3_bf[:, 0:4096], xh_tm[:, c * DI:(c + 1) * DI],
                             arena2_bf[:, 0:4096])
                v.tensor_add(y_tok[:, c * DI:(c + 1) * DI], y_tok[:, c * DI:(c + 1) * DI],
                             arena3_bf[:, 0:4096])
        V(_dterm)
        v_dterm = mark("vector")

        y_fm = arena2_bf[:, 0:8192]
        wait("sync", "sV", v_dterm)
        ytr = None
        for kk in range(32):
            for c in range(NCH):
                ytr = dma("sync", "sTrY", y_fm[:, kk * T + c * 128: kk * T + c * 128 + 128],
                          y_tok[:, c * DI + kk * 128: c * DI + (kk + 1) * 128], transpose=True)
        wtok("vector", ytr)
        def _gate(v):
            v.tensor_mul(y_tok[:, 0:8192], y_fm[:], z_fm[:])
            v.tensor_mul(arena3_bf[:, 0:8192], y_tok[:, 0:8192], y_tok[:, 0:8192])
        V(_gate)
        v_gate = mark("vector")
        g_fm = y_tok[:, 0:8192]
        sq_fm = arena3_bf[:, 0:8192]
        wait("tensor", "sV", v_gate)
        def _ssq(tt):
            for kk in range(32):
                tt.matmul(pmA[0:1, 0:T], onesb[0:128, 0:1], sq_fm[:, kk * T:(kk + 1) * T],
                          start=(kk == 0), stop=(kk == 31))
        TT(_ssq)
        t_ssq = mark("tensor")
        wait("scalar", "sT", t_ssq)
        S(lambda s: s.copy(srow2[:], pmA[0:1, 0:T]))
        s_ssq = mark("scalar")
        wait("sync", "sS", s_ssq)
        r1 = dma("sync", "sRs", scr_row[0:1, 0:T], srow2[:])
        wtok("sync", r1)
        r2 = dma("sync", "sRs", rstd_c[:, 0:2], scr_row[0:1, 0:T].rearrange("o (c p) -> p (o c)", p=128), slow=True)
        wtok("scalar", r2)
        def _rstd(s):
            s.activation(rstd_c[:, 0:2], rstd_c[:, 0:2], AF.Ln, scale=float(1.0 / DI),
                         bias=rmseps[:, 0:1])
            s.activation(rstd_c[:, 0:2], rstd_c[:, 0:2], AF.Exp, scale=-0.5)
        S(_rstd)
        s_rstd = mark("scalar")

        attnout = arena3_bf[:, 0:4096]
        step2 = 0
        wo_mark = {}
        wo_evac = {}
        for g in range(4):
            pm = PM[g % 2]
            for k in range(32):
                b = step2 % 4
                wb = (wbufA, wbufB, wbufC, wbufD)[b]
                wait("sync", "sT", wo_mark.get(step2 - 4, 0))
                if step2 < 4:
                    wait("sync", "sT", t_ssq)
                wl_v = dma("sync", ("sWa", "sWb", "sWc", "sWd")[b], wb[:, 0:512],
                           scr_woutT[k * 128:(k + 1) * 128, g * 512:(g + 1) * 512])
                wtok("tensor", wl_v)
                if k == 0:
                    wait("tensor", "sV", v_gate)
                    wait("tensor", "sS", s_ssq)
                    wait("tensor", "sS", wo_evac.get(g - 2, 0))
                def _wmm(tt, pm=pm, wb=wb, k=k):
                    for mc in range(4):
                        tt.matmul(pm[0:128, mc * 512:mc * 512 + T], wb[:, mc * 128:(mc + 1) * 128],
                                  g_fm[:, k * T:(k + 1) * T], start=(k == 0), stop=(k == 31))
                TT(_wmm)
                wo_mark[step2] = mark("tensor")
                step2 += 1
            wait("scalar", "sT", wo_mark[step2 - 1])
            S(lambda s, pm=pm, g=g: s.copy(
                attnout[:, g * 1024:(g + 1) * 1024].rearrange("p (a b) -> p a b", a=4),
                pm[:, 0:2048].rearrange("p (a b) -> p a b", a=4)[:, :, 0:T]))
            wo_evac[g] = mark("scalar")

        attn_tok = arena2_bf[:, 0:4096]
        wait("sync", "sS", wo_evac[3])
        atr = None
        for kk in range(16):
            for c in range(2):
                atr = dma("sync", "sTrA", attn_tok[:, c * 2048 + kk * 128: c * 2048 + (kk + 1) * 128],
                          attnout[:, kk * T + c * 128: kk * T + c * 128 + 128], transpose=True)
        wtok("vector", atr)
        wait("vector", "sS", s_rstd)
        def _res1(v):
            for c in range(2):
                v.scalar_tensor_tensor(x_tok[:, c * DIM:(c + 1) * DIM],
                                       attn_tok[:, c * 2048:(c + 1) * 2048],
                                       rstd_c[:, c:c + 1], x_tok[:, c * DIM:(c + 1) * DIM],
                                       ALU.mult, ALU.add)
        V(_res1)
        v_x1 = mark("vector")

        if debug and t == 0:
            wait("sync", "sV", v_x1)
            m_prev["dbgx1"] = dma("sync", "sDbg", dbg["dbg_x1"], x_tok[:])
            wait("sync", "sDbg", sems["sDbg"][1])

        ln2_tm = arena3_bf[:, 4096:8192]
        wait("vector", "sV", v_x1)
        def _l2a(v):
            for c in range(2):
                v.reduce_sum(mcol[:, c:c + 1], x_tok[:, c * DIM:(c + 1) * DIM], axis=AXX)
            v.tensor_scalar_mul(mcol[:, 0:2], mcol[:, 0:2], float(1.0 / DIM))
        V(_l2a)
        v_l2a = mark("vector")
        wait("scalar", "sV", v_l2a)
        def _l2b(s):
            for c in range(2):
                s.activation(arena2_bf[:, 4096 + c * DIM: 4096 + (c + 1) * DIM],
                             x_tok[:, c * DIM:(c + 1) * DIM], AF.Square)
        S(_l2b)
        s_l2b = mark("scalar")
        wait("vector", "sS", s_l2b)
        def _l2c(v):
            for c in range(2):
                v.reduce_sum(mcol[:, 2 + c:3 + c], arena2_bf[:, 4096 + c * DIM:4096 + (c + 1) * DIM],
                             axis=AXX)
            v.tensor_scalar_mul(mcol[:, 2:4], mcol[:, 2:4], float(1.0 / DIM))
            v.tensor_mul(rstd_c[:, 0:2], mcol[:, 0:2], mcol[:, 0:2])
            v.tensor_sub(mcol[:, 2:4], mcol[:, 2:4], rstd_c[:, 0:2])
        V(_l2c)
        v_l2c = mark("vector")
        wait("scalar", "sV", v_l2c)
        def _l2d(s):
            s.activation(mcol[:, 2:4], mcol[:, 2:4], AF.Ln, bias=epscol[:, 0:1])
            s.activation(mcol[:, 2:4], mcol[:, 2:4], AF.Exp, scale=-0.5)
        S(_l2d)
        s_l2d = mark("scalar")
        wait("vector", "sS", s_l2d)
        def _l2e(v):
            for c in range(2):
                v.tensor_scalar(ln2_tm[:, c * DIM:(c + 1) * DIM], x_tok[:, c * DIM:(c + 1) * DIM],
                                mcol[:, c:c + 1], mcol[:, 2 + c:3 + c], ALU.subtract, ALU.mult)
        V(_l2e)
        v_l2e = mark("vector")
        wait("sync", "sV", v_l2e)
        l2tr = None
        for c in range(2):
            for k in range(16):
                l2tr = dma("sync", "sTr1", arena2_bf[:, k * T + c * 128: k * T + c * 128 + 128],
                           ln2_tm[:, c * DIM + k * 128:c * DIM + (k + 1) * 128], transpose=True)
        wtok("vector", l2tr)
        V(lambda v: v.tensor_copy(ln2_f8[:], arena2_bf[:, 0:4096]))
        v_ln2f8 = mark("vector")

        gel = arena3_bf[:, 0:1024]
        ff1b = arena3_bf[:, 1024:2048]
        ln2_pairs = ln2_f8.rearrange("p (two c n) -> p c two n", two=2, c=8)
        fbufs = (w8A[:, 0:1024], w8A[:, 1024:2048], w8B[:, 0:1024], w8B[:, 1024:2048])
        step3 = 0
        f_mark = {}
        f_evac = {}
        for g in range(16):
            for j, (scr, pm) in enumerate(((scr_wg0, pmA), (scr_wg1, pmB))):
                scrp = scr.rearrange("(two c p) m -> c p two m", two=2, p=128)
                for k in range(8):
                    b = step3 % 4
                    wb = fbufs[b]
                    wait("sync", "sT", f_mark.get(step3 - 4, 0))
                    if step3 < 4:
                        wait("sync", "sV", v_ln2f8)
                    wv = dma("sync", ("sWa", "sWb", "sWc", "sWd")[b], wb,
                             scrp[k][:, :, g * 512:(g + 1) * 512])
                    wtok("tensor", wv)
                    if k == 0:
                        wait("tensor", "sV", v_ln2f8)
                        wait("tensor", "sS", f_evac.get((g - 1, j), 0))
                        wait("tensor", "sV", f_evac.get((g - 1, "a"), 0))
                    def _fmm(tt, pm=pm, wb=wb, k=k):
                        wp = wb.rearrange("p (two m) -> p two m", two=2)
                        for mc in range(4):
                            tt.matmul(pm[0:128, mc * 512:mc * 512 + T],
                                      wp[:, :, mc * 128:(mc + 1) * 128],
                                      ln2_pairs[:, k], start=(k == 0), stop=(k == 7),
                                      perf_mode=mybir.MatmulPerfMode.DoubleRow)
                    TT(_fmm)
                    f_mark[step3] = mark("tensor")
                    step3 += 1
            wait("scalar", "sT", f_mark[step3 - 1])
            def _fe(s, g=g):
                s.activation(gel[:].rearrange("p (a b) -> p a b", a=4),
                             pmA[:, 0:2048].rearrange("p (a b) -> p a b", a=4)[:, :, 0:T],
                             AF.Gelu_apprx_tanh)
                s.copy(ff1b[:].rearrange("p (a b) -> p a b", a=4),
                       pmB[:, 0:2048].rearrange("p (a b) -> p a b", a=4)[:, :, 0:T])
            S(_fe)
            f_evac[(g, 0)] = f_evac[(g, 1)] = mark("scalar")
            wait("vector", "sS", f_evac[(g, 0)])
            def _fa(v, g=g):
                v.tensor_mul(a8[:, g * 1024:(g + 1) * 1024], gel[:], ff1b[:])
            V(_fa)
            f_evac[(g, "a")] = mark("vector")

        ffnout = arena3_bf[:, 4096:8192]
        a8_pairs = a8.rearrange("p (two c n) -> p c two n", two=2, c=32)
        wlp = scr_wl.rearrange("(two c p) m -> c p two m", two=2, p=128)
        step4 = 0
        w3_mark = {}
        w3_evac = {}
        for g in range(4):
            pm = PM[g % 2]
            for k in range(32):
                b = step4 % 4
                wb = fbufs[b]
                wait("sync", "sT", w3_mark.get(step4 - 4, 0))
                if step4 < 4:
                    wait("sync", "sS", f_evac[(15, 0)])
                    wait("sync", "sV", f_evac[(15, "a")])
                wv = dma("sync", ("sWa", "sWb", "sWc", "sWd")[b], wb,
                         wlp[k][:, :, g * 512:(g + 1) * 512])
                wtok("tensor", wv)
                if k == 0:
                    wait("tensor", "sS", f_evac[(15, 0)])
                    wait("tensor", "sV", f_evac[(15, "a")])
                    wait("tensor", "sS", w3_evac.get(g - 2, 0))
                def _f3(tt, pm=pm, wb=wb, k=k):
                    wp = wb.rearrange("p (two m) -> p two m", two=2)
                    for mc in range(4):
                        tt.matmul(pm[0:128, mc * 512:mc * 512 + T],
                                  wp[:, :, mc * 128:(mc + 1) * 128],
                                  a8_pairs[:, k], start=(k == 0), stop=(k == 31),
                                  perf_mode=mybir.MatmulPerfMode.DoubleRow)
                TT(_f3)
                w3_mark[step4] = mark("tensor")
                step4 += 1
            wait("scalar", "sT", w3_mark[step4 - 1])
            S(lambda s, pm=pm, g=g: s.copy(
                ffnout[:, g * 1024:(g + 1) * 1024].rearrange("p (a b) -> p a b", a=4),
                pm[:, 0:2048].rearrange("p (a b) -> p a b", a=4)[:, :, 0:T]))
            w3_evac[g] = mark("scalar")

        wait("sync", "sS", w3_evac[3])
        ftr = None
        for kk in range(16):
            for c in range(2):
                ftr = dma("sync", "sTrA", arena2_bf[:, c * 2048 + kk * 128: c * 2048 + (kk + 1) * 128],
                          ffnout[:, kk * T + c * 128: kk * T + c * 128 + 128], transpose=True)
        wtok("vector", ftr)
        wtok("vector", m_prev["out_st"])
        V(lambda v: v.tensor_add(arena_f32[:, 0:2048], x_tok[:, 0:DIM],
                                 arena2_bf[:, 0:2048]))
        v_outa = mark("vector")
        wait("sync", "sV", v_outa)
        o1 = dma("sync", "sOut", y_d[r0:r0 + 128, :], arena_f32[:, 0:2048])
        wtok("vector", o1)
        V(lambda v: v.tensor_add(arena_f32[:, 0:2048], x_tok[:, DIM:2 * DIM],
                                 arena2_bf[:, 2048:4096]))
        v_out = mark("vector")
        wait("sync", "sV", v_out)
        o2 = dma("sync", "sOut", y_d[r0 + 128:r0 + 256, :], arena_f32[:, 0:2048])
        m_prev["out_st"] = o2
        m_prev["tile_done_V"] = v_out
        m_prev["tile_done_S"] = w3_evac[3]
        m_prev["tile_done_T"] = w3_mark[step4 - 1]

    for s in DMA_SEMS:
        if sems[s][1] > 0:
            wait("sync", s, sems[s][1])

    with nc.Block() as block:
        @block.sync
        def _(sync):
            for f in thunks["sync"]:
                f(sync)

        @block.gpsimd
        def _(g):
            for f in thunks["gpsimd"]:
                f(g)

        @block.vector
        def _(v):
            vp = DrainAfter(v)
            for f in thunks["vector"]:
                f(vp)

        @block.scalar
        def _(s):
            sp = DrainAfter(s)
            for f in thunks["scalar"]:
                f(sp)

        @block.tensor
        def _(tt):
            tp_ = DrainAfter(tt)
            for f in thunks["tensor"]:
                f(tp_)

    es.close()
    return nc


def make_inmaps(inputs):
    x = np.ascontiguousarray(inputs["L0_attn_x"], dtype=np.float32)
    cm = {}
    cm["w_in"] = np.ascontiguousarray(inputs["L0_attn_W_in"], np.float32)
    cm["w_lambda"] = np.ascontiguousarray(inputs["L0_attn_W_lambda"], np.float32)
    cm["w_theta"] = np.ascontiguousarray(inputs["L0_attn_W_theta"], np.float32)
    cm["w_out"] = np.ascontiguousarray(inputs["L0_attn_W_out"], np.float32)
    wg = np.ascontiguousarray(inputs["L0_ffn_w_gating"], np.float32)
    cm["wg0"] = np.ascontiguousarray(wg[0])
    cm["wg1"] = np.ascontiguousarray(wg[1])
    cm["wl"] = np.ascontiguousarray(inputs["L0_ffn_w_linear"], np.float32)
    cm["bbias_c"] = np.ascontiguousarray(np.asarray(inputs["L0_attn_B_bias"], np.float32).reshape(64, 1))
    cm["cbias_c"] = np.ascontiguousarray(np.asarray(inputs["L0_attn_C_bias"], np.float32).reshape(64, 1))
    cm["alog_c"] = np.ascontiguousarray(np.asarray(inputs["L0_attn_A_log"], np.float32).reshape(64, 1))
    cm["d_c"] = np.ascontiguousarray(np.asarray(inputs["L0_attn_D_param"], np.float32).reshape(64, 1))
    cm["dtb_c"] = np.ascontiguousarray(np.asarray(inputs["L0_attn_dt_bias"], np.float32).reshape(64, 1))
    cm["i64"] = np.eye(64, dtype=np.float32)
    cm["i128"] = np.eye(128, dtype=np.float32)
    tri = np.zeros((128, 128), np.float32)
    for lp in range(128):
        tri[lp, lp + 1:] = BIGNEG
    cm["trineg"] = tri
    maps = []
    for b in range(8):
        m = dict(cm)
        m["x"] = np.ascontiguousarray(x[b])
        maps.append(m)
    return maps


def kernel(**inputs):
    nc = build_program(debug=False)
    in_maps = make_inmaps(inputs)
    res = run_bass_kernel_spmd(nc, in_maps, core_ids=list(range(8)))
    out = np.stack([np.asarray(r["y"], np.float32) for r in res.results], axis=0)
    return out


